# version 12 (final)
# speedup vs baseline: 2.0398x; 1.0333x over previous
import math
from contextlib import ExitStack

import numpy as np

import concourse.bass as bass
import concourse.mybir as mybir
from concourse.bass_utils import run_bass_kernel_spmd

F32 = mybir.dt.float32
BF16 = mybir.dt.bfloat16
FP8 = mybir.dt.float8e4
FP16 = mybir.dt.float16
AX = mybir.AxisListType.X
ACT = mybir.ActivationFunctionType
ALU = mybir.AluOpType

E = 2048
S = 4096
H = 16
NC = 8
CH = 512
TOK = 2 * CH
EC = E // 128
FF = 4 * E
FC = FF // 128
SQD = 1.0 / math.sqrt(128.0)
EPS = 1e-5
NEG = -1e30
ESLOT = [2048, 4096]
SLOPES = [2.0 ** (-0.5 * (h + 1)) for h in range(H)]
WH = [128, 128, 256, 256] + [512] * 12


def core_chunks(c):
    return [(c // 4, 3 - (c % 4)), (c // 4, 4 + (c % 4))]


def attn_blocks():
    out = []
    for h in range(H):
        w = WH[h]
        for slot in range(2):
            poss = range(4) if slot == 0 else range(4, 8)
            for j in range(CH // w):
                nt = ESLOT[slot] // 128
                uni = sorted({(p * CH + j * w) // 128 + d
                              for p in poss for d in range(w // 128)})
                out.append((h, slot, j, nt, uni))
    return out


def build_tables(c):
    chunks = core_chunks(c)
    p = np.arange(128, dtype=np.float64)
    bias_cols, thr_cols = [], []
    for (h, slot, j, nt, uni) in attn_blocks():
        w = WH[h]
        pos = chunks[slot][1]
        tq_lo = pos * CH + j * w
        tq_hi = tq_lo + w - 1
        tq_mid = tq_lo + w // 2
        sl = SLOPES[h]
        for t in range(nt):
            tk0 = 128 * t
            if tk0 > tq_hi:
                bias_cols.append(np.full(128, NEG, np.float32))
            else:
                bias_cols.append((sl * (tk0 + p - tq_mid)).astype(np.float32))
        for t in uni:
            tk0 = 128 * t
            if tk0 > tq_hi:
                thr_cols.append(np.full(128, 1e9, np.float32))
            elif tk0 + 127 >= tq_lo:
                thr_cols.append(np.minimum(np.maximum(tk0 + p - tq_lo, 0.0),
                                           1e9).astype(np.float32))
            else:
                thr_cols.append(np.zeros(128, np.float32))
    return (np.ascontiguousarray(np.stack(bias_cols, 1)),
            np.ascontiguousarray(np.stack(thr_cols, 1)))


class Ctr:
    def __init__(self, h):
        self.h = h
        self.n = 0

    def inc(self, k=1):
        self.n += k
        return self.n


class Builder:
    def __init__(self, stage=None):
        self.stage = stage
        self.nc = bass.Bass("TRN2", target_bir_lowering=False)
        self.ops = {e: [] for e in ("sync", "gpsimd", "tensor", "scalar",
                                    "vector")}
        nb, nu = build_tables(0)
        self.NB = nb.shape[1]
        self.NU = nu.shape[1]
        self._tr_ve = {}
        self._f1pe = {}
        self._f2pe = {}
        self._trs = 0
        self._trs3 = 0
        self._pslot_se = {}

    def emit(self, eng, fn):
        self.ops[eng].append(fn)

    def wait(self, eng, sem, val):
        self.ops[eng].append(lambda e, s=sem.h, v=val: e.wait_ge(s, v))

    def join(self):
        ph = self.s_ph
        tgt = ph.n + 5
        fences = {
            "tensor": [self.s_pe], "scalar": [self.s_se],
            "vector": [self.s_ve],
            "sync": [self.s_c, self.s_in, self.s_y, self.s_out, self.s_x1s]
                    + self.s_x + self.s_wsl + self.s_wm + self.s_kst
                    + self.s_vst + self.s_kv + self.s_vs + self.s_ws
                    + self.s_xl,
            "gpsimd": [self.s_g, self.s_cc] + self.s_gt + self.s_gin,
        }
        for eng in self.ops:
            pre = [(sm.h, sm.n) for sm in fences[eng] if sm.n > 0]

            def f(e, pre=pre, s=ph.h, v=tgt):
                for h, n in pre:
                    e.wait_ge(h, n)
                e.sem_inc(s, 1)
                e.wait_ge(s, v)
            self.ops[eng].append(f)
        ph.n = tgt

    def build(self):
        nc = self.nc
        dt = nc.dram_tensor
        self.x_in = dt("x", [TOK, E], F32, kind="ExternalInput")
        self.wq = dt("wq", [EC, 128, EC, 128], F32, kind="ExternalInput")
        self.wk = dt("wk", [EC, 128, EC, 128], F32, kind="ExternalInput")
        self.wv = dt("wv", [EC, 128, EC, 128], F32, kind="ExternalInput")
        self.wo = dt("wo", [EC, 128, EC, 128], F32, kind="ExternalInput")
        self.wfc = dt("wfc", [32, 8, 128, 512], F32, kind="ExternalInput")
        self.wpr = dt("wpr", [8, 32, 128, 512], F32, kind="ExternalInput")
        self.wfshm = dt("wfshm", [128, 16384], F32, kind="ExternalInput")
        self.wpshm = dt("wpshm", [128, 16384], F32, kind="ExternalInput")
        self.bq_in = dt("bqT", [128, EC], F32, kind="ExternalInput")
        self.bk_in = dt("bkT", [128, EC], F32, kind="ExternalInput")
        self.bo_in = dt("boT", [128, EC], F32, kind="ExternalInput")
        self.bv_in = dt("bv", [1, E], F32, kind="ExternalInput")
        self.id_in = dt("ident", [128, 128], FP16, kind="ExternalInput")
        self.id32_in = dt("ident32", [128, 128], F32, kind="ExternalInput")
        self.on_in = dt("ones_col", [128, 1], BF16, kind="ExternalInput")
        self.or_in = dt("ones_row", [1, 128], F32, kind="ExternalInput")
        self.io_in = dt("iota512", [128, 512], F32, kind="ExternalInput")
        self.bt_in = dt("btab", [128, self.NB], F32, kind="ExternalInput")
        self.tt_in = dt("ttab", [128, self.NU], F32, kind="ExternalInput")
        self.y_out = dt("y", [TOK, E], F32, kind="ExternalOutput")

        self.k_st = dt("k_st", [EC, 128, TOK], FP16)
        self.v_st = dt("v_st", [EC, 8, 128, 128], FP16)
        self.K_all = dt("K_all", [4, EC, 128, TOK], FP16)
        self.V_all = dt("V_all", [4, EC, 8, 128, 128], FP16)
        self.ccw_in = dt("ccw_in", [1, 2], F32)
        self.ccw_out = dt("ccw_out", [1, 2], F32)
        self.ccx_in = dt("ccx_in", [1, 1], F32)
        self.ccx_out = dt("ccx_out", [1, 1], F32)
        self.cc2_in = dt("cc2_in", [1, 1], F32)
        self.cc2_out = dt("cc2_out", [1, 1], F32)
        self.g_buf = dt("g_buf", [FC, 128, TOK], F32)
        self.x1_buf = dt("x1_buf", [8, 128, E], F32)

        es = ExitStack()
        self.es = es
        sem = lambda n: Ctr(es.enter_context(nc.semaphore(n)))
        self.s_in = sem("s_in")
        self.s_g = sem("s_g")
        self.s_c = sem("s_c")
        self.s_x = [sem("s_x0"), sem("s_x1")]
        self.s_wsl = [sem("s_wsl0"), sem("s_wsl1")]
        self.s_wm = [sem("s_wm0"), sem("s_wm1")]
        self.s_kst = [sem("s_kst0"), sem("s_kst1")]
        self.s_vst = [sem(f"s_vst{i}") for i in range(4)]
        self.s_kv = [sem("s_kv0"), sem("s_kv1")]
        self.s_vs = [sem("s_vs0"), sem("s_vs1")]
        self.s_x1s = sem("s_x1s")
        self.s_ws = [sem(f"s_ws{i}") for i in range(8)]
        self.s_xl = [sem(f"s_xl{i}") for i in range(3)]
        self.s_gt = [sem("s_gt0"), sem("s_gt1")]
        self.s_gin = [sem(f"s_gin{i}") for i in range(4)]
        self.s_y = sem("s_y")
        self.s_pe = sem("s_pe")
        self.s_se = sem("s_se")
        self.s_ve = sem("s_ve")
        self.s_cc = sem("s_cc")
        self.s_out = sem("s_out")
        self.s_ph = sem("s_ph")

        sb = lambda *a: es.enter_context(nc.sbuf_tensor(*a))
        self.ident = sb("identc", [128, 128], FP16)
        self.ident32 = sb("identc32", [128, 128], F32)
        self.ones_c = sb("ones_c", [128, 1], BF16)
        self.ones_r = sb("ones_r", [1, 128], F32)
        self.iota = sb("iota", [128, 512], F32)
        self.btab = sb("btab_s", [128, self.NB], F32)
        self.ttab = sb("ttab_s", [128, self.NU], F32)
        self.bqd = sb("bqd", [128, EC], F32)
        self.bkT = sb("bkT_s", [128, EC], F32)
        self.boT = sb("boT_s", [128, EC], F32)
        self.bvr = sb("bvr", [1, E], F32)
        self.bv_bc = sb("bv_bc", [128, E], FP16)
        self.scr = sb("scr", [128, E], F32)
        self.st8 = sb("st8", [128, 8], F32)
        self.sml = sb("sml", [1, 16], F32)
        self.sbc = sb("sbc", [128, 8], F32)
        self.gred = sb("gred", [1, 4], F32)
        self.gacc = sb("gacc", [1, 104], F32)
        self.wmr = sb("wmr", [128, 2 * 512], F32)
        self._pslot = {}
        self.ps_bc = es.enter_context(nc.psum_tensor("ps_bc", [128, 512], F32))

        self._consts()
        if self.stage == "consts":
            self._dump(self.bv_bc, 2)
            return self._finish(es, [])
        self._wmax_pass()
        if self.stage == "wmax":
            self._dump(self.bv_bc, 2)
            return self._finish(es, [])

        at = ExitStack()
        self.attn = at.enter_context(
            nc.sbuf_tensor("attn", [128, H * TOK], FP16))
        lvl1 = ExitStack()
        s1 = lambda *a: lvl1.enter_context(nc.sbuf_tensor(*a))
        self.q = s1("q", [128, H * TOK], FP16)

        pj = ExitStack()
        s2 = lambda *a: pj.enter_context(nc.sbuf_tensor(*a))
        self.ln1T = s2("ln1T", [128, EC * TOK], FP16)
        self.xr = s2("xr", [128, 3 * E], F32)
        self.lnr = s2("lnr", [128, 2 * E], FP16)
        self.wsl = s2("wsl", [128, 3 * E], F32)
        self.wslb = s2("wslb", [128, 3 * E], FP16)
        self.kst = s2("kst", [128, 2 * TOK], FP16)
        self.vfm = s2("vfm", [128, 2 * TOK], FP16)
        self.ev = s2("ev", [128, 4 * 128], FP16)
        self.ps_mm = pj.enter_context(
            nc.psum_tensor("ps_mm", [128, 2 * TOK], F32))
        self.ps_trb = pj.enter_context(
            nc.psum_tensor("ps_trb", [128, 3 * 1024], FP16))

        self._ln1()
        if self.stage == "ln1o":
            self._dump(self.ln1T)
            return self._finish(es, [pj, lvl1, at])
        self._proj_qkv()
        if self.stage == "q":
            self._dump(self.q)
            return self._finish(es, [pj, lvl1, at])
        self.join()
        pj.close()

        at2 = ExitStack()
        s4 = lambda *a: at2.enter_context(nc.sbuf_tensor(*a))
        self.ksl = s4("ksl", [128, 2 * S], FP16)
        self.vsl = s4("vsl", [128, 2 * S], FP16)
        self.prb = s4("prb", [128, 4 * 512], BF16)
        self.mgen = s4("mgen", [128, 2 * 512], F32)
        self.mtab = s4("mtab", [128, 32 * 512], BF16)
        self.rsum = s4("rsum", [1, 512], F32)
        self.osb = s4("osb", [128, 512], F32)
        self.ps_s = at2.enter_context(
            nc.psum_tensor("ps_s", [128, 4 * 512], F32))
        self.ps_o = at2.enter_context(
            nc.psum_tensor("ps_o", [128, 2 * 512], F32))
        self.ps_m = at2.enter_context(
            nc.psum_tensor("ps_m", [1, 512], F32))

        self._attention()
        if self.stage == "attn":
            self._dump(self.attn)
            return self._finish(es, [at2, lvl1, at])
        self.join()
        at2.close()
        lvl1.close()

        wo = ExitStack()
        s5 = lambda *a: wo.enter_context(nc.sbuf_tensor(*a))
        self.wsl = s5("wsl2", [128, 3 * E], F32)
        self.wslb = s5("wslb2", [128, 3 * E], FP16)
        self.aofm = s5("aofm", [128, 2 * TOK], FP16)
        self.ao_tm = s5("ao_tm", [128, 8 * E], FP16)
        self.x1row = s5("x1row", [128, 2 * E], F32)
        self.ps_mm = wo.enter_context(
            nc.psum_tensor("ps_mm2", [128, 2 * TOK], F32))
        self.ps_t16 = wo.enter_context(
            nc.psum_tensor("ps_t16w", [128, 2048], FP16))
        self._wo_ln2()
        if self.stage == "x1":
            self._dump(self.ao_tm)
            return self._finish(es, [wo, at])
        self.join()
        wo.close()
        at.close()

        ffn = ExitStack()
        s6 = lambda *a: ffn.enter_context(nc.sbuf_tensor(*a))
        self.ln3 = s6("ln3", [128, 8 * E], F32)
        self.ln3_8 = s6("ln3_8", [128, EC * TOK], FP8)
        self.x1r = s6("x1r", [128, 2 * E], F32)
        self.ps_tr32 = ffn.enter_context(
            nc.psum_tensor("ps_tr32", [128, 1536], F32))
        self.wfsl = s6("wfsl", [128, 8 * 512], F32)
        self.wf8 = s6("wf8", [128, 8 * 512], FP8)
        self.gt = s6("gt", [128, 2 * 2048], F32)
        self.ps_h = ffn.enter_context(
            nc.psum_tensor("ps_h", [128, 2048], F32))
        self._ln3_and_cc1()
        if self.stage == "ln3":
            self._dump(self.ln3)
            return self._finish(es, [ffn])
        self._ffn1()
        self.join()
        ffn.close()
        ffn = ExitStack()

        f2 = ExitStack()
        s8 = lambda *a: f2.enter_context(nc.sbuf_tensor(*a))
        self.g8 = s8("g8", [128, FC * 1024], FP8)
        self.o2tm = s8("o2tm", [128, 8 * E], FP16)
        self.gin = s8("gin", [128, 4 * TOK], F32)
        self.ps_trb = f2.enter_context(
            nc.psum_tensor("ps_trb2", [128, 2048], FP16))
        self.wpsl = s8("wpsl", [128, 8 * 512], F32)
        self.wp8 = s8("wp8", [128, 8 * 512], FP8)
        self.o2fm = s8("o2fm", [128, 2 * 2048], FP16)
        self.yrow = s8("yrow", [128, E], F32)
        self.x1r2 = s8("x1r2", [128, 2 * E], F32)
        self.ps_o2 = f2.enter_context(
            nc.psum_tensor("ps_o2", [128, 2048], F32))
        self._ffn2()
        self._ln4_out()
        return self._finish(es, [f2, ffn])

    def _finish(self, es, extra):
        self.wait("sync", self.s_out, self.s_out.n)
        self.wait("gpsimd", self.s_out, self.s_out.n)
        self.wait("sync", self.s_y, self.s_y.n)
        self.wait("gpsimd", self.s_y, self.s_y.n)
        with self.nc.Block() as block:
            block.sync(lambda e: [f(e) for f in self.ops["sync"]])
            block.gpsimd(lambda e: [f(e) for f in self.ops["gpsimd"]])
            block.tensor(lambda e: [f(e) for f in self.ops["tensor"]])
            block.scalar(lambda e: [f(e) for f in self.ops["scalar"]])
            block.vector(lambda e: [f(e) for f in self.ops["vector"]])
        for x in extra:
            x.close()
        es.close()
        return self.nc

    def _dump(self, tile, nko=16):
        self.join()
        yv = self.y_out.ap().rearrange("(p r) c -> p (r c)", p=128)
        prev = None
        for i in range(nko):
            sl = tile[:, i * 1024:(i + 1) * 1024]

            def f(e, d=self.scr[:, 0:1024], s=sl, g=prev):
                if g is not None:
                    e.wait_ge(self.s_out.h, g)
                e.tensor_copy(d, s).then_inc(self.s_ve.h, 1)
            self.emit("vector", f)
            v = self.s_ve.inc(1)
            self.wait("sync", self.s_ve, v)
            self.emit("sync", lambda e, d=yv[:, i * 1024:(i + 1) * 1024],
                      s=self.scr[:, 0:1024]:
                      e.dma_start(out=d, in_=s).then_inc(self.s_out.h, 16))
            prev = self.s_out.inc(16)

    def _consts(self):
        pairs = [
            (self.ident[:, :], self.id_in[:, :]),
            (self.ident32[:, :], self.id32_in[:, :]),
            (self.ones_c[:, :], self.on_in[:, :]),
            (self.ones_r[:, :], self.or_in[:, :]),
            (self.iota[:, :], self.io_in[:, :]),
            (self.btab[:, :], self.bt_in[:, :]),
            (self.ttab[:, :], self.tt_in[:, :]),
            (self.bqd[:, :], self.bq_in[:, :]),
            (self.bkT[:, :], self.bk_in[:, :]),
            (self.boT[:, :], self.bo_in[:, :]),
            (self.bvr[:, :], self.bv_in[:, :]),
        ]

        def f(e, pairs=pairs):
            for d, s in pairs:
                e.dma_start(out=d, in_=s).then_inc(self.s_c.h, 16)
        self.emit("sync", f)
        need = self.s_c.inc(16 * len(pairs))

        ve0 = self.s_ve.n

        def f_pe(e, need=need, ve0=ve0):
            e.wait_ge(self.s_c.h, need)
            for i in range(4):
                if i >= 1:
                    e.wait_ge(self.s_ve.h, ve0 + i)
                e.matmul(self.ps_bc[:, :], self.ones_r[:, :],
                         self.bvr[0:1, i * 512:(i + 1) * 512],
                         start=True, stop=True).then_inc(self.s_pe.h, 1)
        pe0 = self.s_pe.n

        def f_ve(e, pe0=pe0):
            for i in range(4):
                e.wait_ge(self.s_pe.h, pe0 + i + 1)
                e.tensor_copy(self.bv_bc[:, i * 512:(i + 1) * 512],
                              self.ps_bc[:, :]).then_inc(self.s_ve.h, 1)
        self.emit("tensor", f_pe)
        self.emit("vector", f_ve)
        self.s_pe.inc(4)
        self.bvbc_ve = self.s_ve.inc(4)
        self.wait("vector", self.s_c, need)
        self.emit("vector", lambda e: e.tensor_scalar_mul(
            self.bqd[:, :], self.bqd[:, :], SQD).then_inc(self.s_ve.h, 1))
        self.s_ve.inc(1)

    def gmax_acc(self, idx, src, waits=()):
        def f(e, src=src, idx=idx, waits=waits):
            for sm, v in waits:
                e.wait_ge(sm.h, v)
            e.tensor_reduce(self.gacc[0:1, idx:idx + 1], src,
                            axis=mybir.AxisListType.XYZWC,
                            op=ALU.abs_max).then_inc(self.s_g.h, 1)
        self.emit("gpsimd", f)
        return self.s_g.inc(1)

    def gmax_fin(self, lane, lo, hi):
        def f(e, lane=lane, lo=lo, hi=hi, v=self.s_g.n):
            e.wait_ge(self.s_g.h, v)
            e.tensor_reduce(self.gred[0:1, lane:lane + 1],
                            self.gacc[0:1, lo:hi],
                            axis=mybir.AxisListType.XYZWC,
                            op=ALU.max).then_inc(self.s_g.h, 1)
        self.emit("gpsimd", f)
        return self.s_g.inc(1)

    def _wmax_pass(self):
        slabs = [(0, self.wfshm[:, i * 512:(i + 1) * 512])
                 for i in range(32)]
        slabs += [(1, self.wpshm[:, i * 512:(i + 1) * 512])
                  for i in range(32)]
        dmav = {}

        def emit_red(i):
            par = i % 2
            dst = self.wmr[:, par * 512:(par + 1) * 512]
            self.gmax_acc(i, dst, waits=(dmav[i],))

        for i, (lane, src) in enumerate(slabs):
            par = i % 2
            dst = self.wmr[:, par * 512:(par + 1) * 512]
            sm = self.s_wm[par]
            self.emit("gpsimd", lambda e, d=dst, s=src, h=sm.h:
                      e.dma_start(out=d, in_=s).then_inc(h, 16))
            dmav[i] = (sm, sm.inc(16))
            if i >= 1:
                emit_red(i - 1)
        emit_red(63)
        self.gmax_fin(0, 0, 32)
        self.gmax_fin(1, 32, 64)
        self.wmax_g = self.s_g.n

    def _ccw_start(self):
        self.emit("gpsimd", lambda e: e.dma_start(
            out=self.ccw_in[:, :],
            in_=self.gred[0:1, 0:2]).then_inc(self.s_g.h, 16))
        gv = self.s_g.inc(16)
        self.wait("gpsimd", self.s_g, gv)
        self.emit("gpsimd", lambda e: e.collective_compute(
            "AllReduce", ALU.max, replica_groups=[list(range(NC))],
            ins=[self.ccw_in.ap().opt()],
            outs=[self.ccw_out.ap().opt()]).then_inc(self.s_cc.h, 1))
        cv = self.s_cc.inc(1)
        self.wait("gpsimd", self.s_cc, cv)
        self.emit("gpsimd", lambda e: e.dma_start(
            out=self.sml[0:1, 0:2],
            in_=self.ccw_out[:, :]).then_inc(self.s_g.h, 16))
        self._ccw_gv2 = self.s_g.inc(16)

    def _ccw_finish(self):
        gv2 = self._ccw_gv2
        self.wait("vector", self.s_g, gv2)

        base = self.s_ve.n

        def f_sc(e, base=base):
            sml = self.sml
            k = [base]

            def step(inst):
                k[0] += 1
                inst.then_inc(self.s_ve.h, 1)
                e.wait_ge(self.s_ve.h, k[0])
            step(e.tensor_scalar(sml[0:1, 4:5], sml[0:1, 0:1], 1.0 / 448.0,
                                 1e-6, ALU.mult, ALU.max))
            step(e.tensor_scalar(sml[0:1, 5:6], sml[0:1, 1:2], 1.0 / 448.0,
                                 1e-6, ALU.mult, ALU.max))
            step(e.tensor_scalar_mul(sml[0:1, 7:8], sml[0:1, 4:5], 2.0))
            step(e.reciprocal(sml[0:1, 7:8], sml[0:1, 7:8]))
            step(e.tensor_scalar_mul(sml[0:1, 8:9], sml[0:1, 5:6], 2.0))
            step(e.reciprocal(sml[0:1, 8:9], sml[0:1, 8:9]))
        self.emit("vector", f_sc)
        scv = self.s_ve.inc(6)

        def f_b(e, v=scv):
            e.wait_ge(self.s_ve.h, v)
            e.matmul(self.ps_bc[:, 0:2], self.ones_r[:, :],
                     self.sml[0:1, 7:9], start=True,
                     stop=True).then_inc(self.s_pe.h, 1)
        self.emit("tensor", f_b)
        bv = self.s_pe.inc(1)

        def f_cp(e, v=bv):
            e.wait_ge(self.s_pe.h, v)
            e.tensor_copy(self.sbc[:, 0:2],
                          self.ps_bc[:, 0:2]).then_inc(self.s_ve.h, 1)
        self.emit("vector", f_cp)
        self.sbcw_ve = self.s_ve.inc(1)

    def _ln_tm(self, src_ap, dst_ap, scr_guard=None):
        st = self.st8
        m = st[:, 0:1]
        v1 = st[:, 1:2]
        v2 = st[:, 2:3]
        vb = self.s_ve.n
        self.emit("vector", lambda e, s=src_ap, vb=vb: (
            e.wait_ge(self.s_ve.h, vb),
            e.reduce_sum(m, s, axis=AX).then_inc(self.s_ve.h, 1)))
        vs = self.s_ve.inc(1)

        def f_se(e, s=src_ap, v=vs, g=scr_guard):
            e.wait_ge(self.s_ve.h, v)
            if g is not None:
                e.wait_ge(self.s_ve.h, g)
            e.activation(self.scr[:, :], s, ACT.Square,
                         accum_out=v1).then_inc(self.s_se.h, 1)
        self.emit("scalar", f_se)
        ss = self.s_se.inc(1)

        base = self.s_ve.n

        def f_v2(e, v=ss, base=base):
            e.wait_ge(self.s_se.h, v)
            e.tensor_scalar_mul(m, m, 1.0 / E).then_inc(self.s_ve.h, 1)
            e.tensor_scalar_mul(v1, v1, 1.0 / E).then_inc(self.s_ve.h, 1)
            e.wait_ge(self.s_ve.h, base + 1)
            e.scalar_tensor_tensor(v2, m, 1.0, m, ALU.mult,
                                   ALU.mult).then_inc(self.s_ve.h, 1)
            e.wait_ge(self.s_ve.h, base + 3)
            e.tensor_sub(v1, v1, v2).then_inc(self.s_ve.h, 1)
            e.wait_ge(self.s_ve.h, base + 4)
            e.tensor_scalar_add(v1, v1, EPS).then_inc(self.s_ve.h, 1)
        self.emit("vector", f_v2)
        vv = self.s_ve.inc(5)

        def f_s2(e, v=vv):
            e.wait_ge(self.s_ve.h, v)
            e.activation(v1, v1, ACT.Sqrt).then_inc(self.s_se.h, 1)
        self.emit("scalar", f_s2)
        s2 = self.s_se.inc(1)

        base3 = self.s_ve.n

        def f_v3(e, v=s2, base=base3):
            e.wait_ge(self.s_se.h, v)
            e.reciprocal(v1, v1).then_inc(self.s_ve.h, 1)
            e.wait_ge(self.s_ve.h, base + 1)
            e.scalar_tensor_tensor(v2, m, -1.0, v1, ALU.mult,
                                   ALU.mult).then_inc(self.s_ve.h, 1)
        self.emit("vector", f_v3)
        v3 = self.s_ve.inc(2)

        def f_s3(e, s=src_ap, d=dst_ap, v=v3):
            e.wait_ge(self.s_ve.h, v)
            e.activation(d, s, ACT.Identity, bias=v2,
                         scale=v1).then_inc(self.s_se.h, 1)
        self.emit("scalar", f_s3)
        return self.s_se.inc(1)

    def _tr_block(self, src, dst, gate_se=None, extra_ve=None, ps=None):
        if ps is None:
            ps = self.ps_trb
        sl = self._trs
        self._trs ^= 1
        pst = ps[:, sl * 1024: sl * 1024 + 128]
        gp = self._pslot.get((ps.name, sl))

        def f_pe(e, pst=pst, src=src, gse=gate_se, gp=gp):
            if gse is not None:
                e.wait_ge(self.s_se.h, gse)
            if gp is not None:
                e.wait_ge(self.s_ve.h, gp)
            e.transpose(pst, src, self.ident[:, :]).then_inc(self.s_pe.h, 1)
        self.emit("tensor", f_pe)
        pv = self.s_pe.inc(1)

        def f_ve(e, pst=pst, dst=dst, v=pv, vg=extra_ve):
            e.wait_ge(self.s_pe.h, v)
            if vg is not None:
                e.wait_ge(self.s_ve.h, vg)
            e.tensor_copy(dst, pst).then_inc(self.s_ve.h, 1)
        self.emit("vector", f_ve)
        last = self.s_ve.inc(1)
        self._pslot[(ps.name, sl)] = last
        return last

    def _transpose16(self, src_tile, src_col0, dst_cb, gate_se, tag):
        last = 0
        for i in range(16):
            src = src_tile[:, src_col0 + i * 128: src_col0 + (i + 1) * 128]
            dst, vg = dst_cb(i)
            last = self._tr_block(src, dst,
                                  gate_se=(gate_se if i == 0 else None),
                                  extra_ve=vg)
        return last

    def _ln1(self):
        self._tr_ve = {}
        tr_done = {}
        for tc in range(8):
            sl = tc % 3
            par = tc % 2
            xr = self.xr[:, sl * E:(sl + 1) * E]
            lr = self.lnr[:, par * E:(par + 1) * E]
            if tc >= 3:
                self.wait("sync", self.s_ve, tr_done[tc - 3])
            sm = self.s_xl[sl]
            self.emit("sync", lambda e, d=xr, h=sm.h,
                      s=self.x_in[tc * 128:(tc + 1) * 128, :]:
                      e.dma_start(out=d, in_=s).then_inc(h, 16))
            t = sm.inc(16)
            self.wait("vector", sm, t)
            self.wait("scalar", sm, t)
            se_ln = self._ln_tm(xr, lr, scr_guard=tr_done.get(tc - 2))

            def cb(i, tc=tc):
                return (self.ln1T[:, i * TOK + tc * 128:
                                  i * TOK + (tc + 1) * 128], None)
            tr_done[tc] = self._transpose16(self.lnr, par * E, cb, se_ln,
                                            f"ln1_{tc}")
        self.ln1_ve = self.s_ve.n

    def _proj16(self, w_hbm, rhs_tile, rhs_gate_ve, evac_cb):
        NSW = 3
        cast = {}
        gates = {}
        mms = {}
        prev = getattr(self, "_pj_prev", None)
        if prev is not None:
            gates[-2], gates[-1] = prev
        pm = getattr(self, "_pj_prev_mms", None)
        if pm is not None:
            for k in range(NSW):
                mms[-NSW + k] = pm[k]
        pc = getattr(self, "_pj_prev_cast", None)
        if pc is not None:
            for k in range(NSW):
                cast[-NSW + k] = pc[k]
        self._evac_sync = []
        for oc in range(EC):
            sl = oc % NSW
            par = oc % 2
            wsl = self.wsl[:, sl * E:(sl + 1) * E]
            wslb = self.wslb[:, sl * E:(sl + 1) * E]
            psb = self.ps_mm[:, par * TOK:(par + 1) * TOK]
            if cast.get(oc - NSW) is not None:
                self.wait("sync", self.s_ve, cast[oc - NSW])
            sm = self.s_ws[sl]
            self.emit("sync", lambda e, d=wsl, s=w_hbm[oc, :, :, :], h=sm.h:
                      e.dma_start(out=d, in_=s).then_inc(h, 16))
            t = sm.inc(16)
            for fl in self._evac_sync:
                fl()
            self._evac_sync = []
            self.wait("vector", sm, t)
            pmm = mms.get(oc - NSW)
            self.emit("vector", lambda e, d=wslb, s=wsl, pmm=pmm:
                      (pmm is not None and e.wait_ge(self.s_pe.h, pmm),
                       e.tensor_copy(d, s).then_inc(self.s_ve.h, 1)))
            cast[oc] = self.s_ve.inc(1)
            g = gates.get(oc - 2)

            def f_pe(e, wslb=wslb, psb=psb, cv=cast[oc], g=g,
                     rg=rhs_gate_ve, rhs=rhs_tile):
                e.wait_ge(self.s_ve.h, max(cv, rg))
                if g is not None:
                    e.wait_ge(self.s_se.h if g[0] == "se" else self.s_ve.h,
                              g[1])
                for ec in range(EC):
                    lhs = wslb[:, ec * 128:(ec + 1) * 128]
                    for hf in range(2):
                        mm_i = e.matmul(psb[:, hf * CH:(hf + 1) * CH], lhs,
                                        rhs[:, ec * TOK + hf * CH:
                                            ec * TOK + (hf + 1) * CH],
                                        start=(ec == 0), stop=(ec == EC - 1))
                mm_i.then_inc(self.s_pe.h, 1)
            self.emit("tensor", f_pe)
            mm = self.s_pe.inc(1)
            mms[oc] = mm
            gates[oc] = evac_cb(oc, psb, mm)
        for fl in self._evac_sync:
            fl()
        self._evac_sync = []
        self._pj_prev = (gates[EC - 2], gates[EC - 1])
        self._pj_prev_mms = tuple(mms[EC - NSW + k] for k in range(NSW))
        self._pj_prev_cast = tuple(cast[EC - NSW + k] for k in range(NSW))

    def _proj_qkv(self):
        kdma = {}

        vdma = {}
        vtr_pe = {}
        vtr_ve = {}

        def v_cb(oc, psb, mm):
            par = oc % 2
            fm = self.vfm[:, par * TOK:(par + 1) * TOK]
            g = vtr_pe.get(oc - 2)

            def f(e, fm=fm, psb=psb, v=mm, g=g):
                e.wait_ge(self.s_pe.h, v)
                if g is not None:
                    e.wait_ge(self.s_pe.h, g)
                e.tensor_copy(fm, psb).then_inc(self.s_ve.h, 1)
            self.emit("vector", f)
            fmv = self.s_ve.inc(1)
            for tc in range(8):
                sl = self._trs3
                self._trs3 = (self._trs3 + 1) % 3
                pst = self.ps_trb[:, sl * 1024: sl * 1024 + 128]
                src = fm[:, tc * 128:(tc + 1) * 128]
                evt = self.ev[:, (tc % 4) * 128:((tc % 4) + 1) * 128]
                gp = self._pslot.get((self.ps_trb.name, sl))

                def f_pe(e, pst=pst, src=src, fmv=fmv, g=gp):
                    e.wait_ge(self.s_ve.h, fmv)
                    if g is not None:
                        e.wait_ge(self.s_ve.h, g)
                    e.transpose(pst, src, self.ident[:, :]).then_inc(self.s_pe.h, 1)
                self.emit("tensor", f_pe)
                pv = self.s_pe.inc(1)
                vsm = self.s_vst[tc % 4]
                dg = vdma.get((oc, tc - 4), vdma.get((oc - 1, tc + 4)))

                def f_ve(e, pst=pst, evt=evt, v=pv, oc=oc, dg=dg, h=vsm.h):
                    e.wait_ge(self.s_pe.h, v)
                    if dg is not None:
                        e.wait_ge(h, dg)
                    e.scalar_tensor_tensor(
                        evt, pst, 1.0, self.bv_bc[:, oc * 128:(oc + 1) * 128],
                        ALU.mult, ALU.add).then_inc(self.s_ve.h, 1)
                self.emit("vector", f_ve)
                vv = self.s_ve.inc(1)
                self._pslot[(self.ps_trb.name, sl)] = vv

                def sync_part(vv=vv, evt=evt, oc=oc, tc=tc, h=vsm.h):
                    self.wait("sync", self.s_ve, vv)
                    self.emit("sync",
                              lambda e, d=self.v_st[oc, tc, :, :], s=evt:
                              e.dma_start(out=d, in_=s).then_inc(h, 16))
                self._evac_sync.append(sync_part)
                vdma[(oc, tc)] = vsm.inc(16)
            vtr_pe[oc] = self.s_pe.n
            return ("ve", fmv)
        self._proj16(self.wv, self.ln1T, self.ln1_ve, v_cb)
        self.v_staged = [(sm, sm.n) for sm in self.s_vst]
        self._ccw_start()
        self._gather_v()

        def k_cb(oc, psb, mm):
            par = oc % 2
            dst = self.kst[:, par * TOK:(par + 1) * TOK]
            g = kdma.get(oc - 2)

            sm = self.s_kst[par]

            def f(e, dst=dst, psb=psb, v=mm, oc=oc, g=g, h=sm.h):
                e.wait_ge(self.s_pe.h, v)
                if g is not None:
                    e.wait_ge(h, g)
                e.activation(dst, psb, ACT.Identity,
                             bias=self.bkT[:, oc:oc + 1]).then_inc(self.s_se.h, 1)
            self.emit("scalar", f)
            se = self.s_se.inc(1)

            def sync_part(se=se, dst=dst, oc=oc, h=sm.h):
                self.wait("sync", self.s_se, se)
                self.emit("sync", lambda e, d=self.k_st[oc, :, :], s=dst:
                          e.dma_start(out=d, in_=s).then_inc(h, 16))
            self._evac_sync.append(sync_part)
            kdma[oc] = sm.inc(16)
            return ("se", se)
        self._proj16(self.wk, self.ln1T, self.ln1_ve, k_cb)
        self.k_staged = [(sm, sm.n) for sm in self.s_kst]
        self._gather_k()

        def q_cb(oc, psb, mm):
            def f(e, oc=oc, psb=psb, v=mm):
                e.wait_ge(self.s_pe.h, v)
                e.activation(self.q[:, oc * TOK:(oc + 1) * TOK], psb,
                             ACT.Identity, bias=self.bqd[:, oc:oc + 1],
                             scale=SQD).then_inc(self.s_se.h, 1)
            self.emit("scalar", f)
            return ("se", self.s_se.inc(1))
        self._proj16(self.wq, self.ln1T, self.ln1_ve, q_cb)
        self._ccw_finish()

    def _gather_k(self):
        rg = [[0, 1, 2, 3], [4, 5, 6, 7]]
        for sm, v in self.k_staged:
            self.wait("gpsimd", sm, v)
        self.emit("gpsimd", lambda e: e.collective_compute(
            "AllGather", ALU.bypass, replica_groups=rg,
            ins=[self.k_st.ap().opt()],
            outs=[self.K_all.ap().opt()]).then_inc(self.s_cc.h, 1))
        self.k_cc = self.s_cc.inc(1)

    def _gather_v(self):
        rg = [[0, 1, 2, 3], [4, 5, 6, 7]]
        for sm, v in self.v_staged:
            self.wait("gpsimd", sm, v)
        self.emit("gpsimd", lambda e: e.collective_compute(
            "AllGather", ALU.bypass, replica_groups=rg,
            ins=[self.v_st.ap().opt()],
            outs=[self.V_all.ap().opt()]).then_inc(self.s_cc.h, 1))
        self.v_cc = self.s_cc.inc(1)

    def _attention(self):
        LAG = 3
        NS = 4
        blocks = attn_blocks()
        binfo = []
        tiles = []
        for b, (h, slot, j, nt, uni) in enumerate(blocks):
            binfo.append((2 * h + slot, (2 * h + slot) % 2, WH[h],
                          h * TOK + slot * CH + j * WH[h], nt, set(uni),
                          h, slot, j))
            for t in range(nt):
                tiles.append((b, t))
        N = len(tiles)

        mm_pe, exp_se, av_pe, mask_ve = {}, {}, {}, {}
        recip_se, evac_ve, norm_ve, last_av = {}, {}, {}, {}
        kgate, vgate = {}, {}
        seen_it = set()
        bcol = [0]
        ucol = [0]
        pend = []

        pat = {}
        uc = 0
        for (h, slot, j, nt, uni) in blocks:
            us = set(uni)
            for t in range(nt):
                if t in us:
                    if h == 4:
                        pat[(slot, t)] = (len(pat), uc)
                    uc += 1
        gen_base = self.s_ve.n

        def f_gen(e, items=sorted(pat.items(), key=lambda x: x[1][0]),
                  need=self.s_c.n, base=gen_base):
            e.wait_ge(self.s_c.h, need)
            e.wait_ge(self.s_ve.h, base)
            for (slot, t), (pid, col) in items:
                e.tensor_scalar(self.mtab[:, pid * 512:(pid + 1) * 512],
                                self.iota[:, 0:512],
                                self.ttab[:, col:col + 1], NEG,
                                ALU.is_lt, ALU.mult).then_inc(self.s_ve.h, 1)
        self.emit("vector", f_gen)
        mtab_ve = self.s_ve.inc(len(pat))

        def emit_slab(b):
            it, par, w, qc0, nt, uni, h, slot, j = binfo[b]
            if it in seen_it:
                return
            seen_it.add(it)
            if it >= 2:
                self.wait("gpsimd", self.s_pe, last_av[it - 2])
            if it == 0:
                self.wait("gpsimd", self.s_cc, self.k_cc)
            kd, vd = [], []
            for p5 in range(ESLOT[slot] // CH):
                rl = (3 - p5) if p5 <= 3 else (p5 - 4)
                lc = 0 if p5 <= 3 else CH
                kd.append((self.ksl[:, par * S + p5 * CH:
                                    par * S + (p5 + 1) * CH],
                           self.K_all[rl, h, :, lc:lc + CH]))
                for q4 in range(4):
                    g2 = p5 * 4 + q4
                    tcz = q4 + (0 if p5 <= 3 else 4)
                    vd.append((self.vsl[:, par * S + g2 * 128:
                                        par * S + (g2 + 1) * 128],
                               self.V_all[rl, h, tcz, :, :]))
            ksm, vsm = self.s_kv[par], self.s_vs[par]

            def f_dk(e, kd=kd, hh=ksm.h):
                for d, s in kd:
                    e.dma_start(out=d, in_=s).then_inc(hh, 16)
            self.emit("gpsimd", f_dk)
            kgate[it] = (ksm, ksm.inc(16 * len(kd)))
            if it == 0:
                self.wait("gpsimd", self.s_cc, self.v_cc)

            def f_dv(e, vd=vd, hh=vsm.h):
                for d, s in vd:
                    e.dma_start(out=d, in_=s).then_inc(hh, 16)
            self.emit("gpsimd", f_dv)
            vgate[it] = (vsm, vsm.inc(16 * len(vd)))

        def emit_score(g):
            b, t = tiles[g]
            it, par, w, qc0, nt, uni, h, slot, j = binfo[b]
            sp = g % NS
            pss = self.ps_s[:, sp * 512: sp * 512 + w]
            eg = exp_se.get(g - NS)
            kg = kgate.pop(it, None)

            def f_s(e, pss=pss, t=t, par=par, qc0=qc0, w=w, eg=eg, kg=kg):
                if kg is not None:
                    e.wait_ge(kg[0].h, kg[1])
                if eg is not None:
                    e.wait_ge(self.s_se.h, eg)
                e.matmul(pss, self.ksl[:, par * S + t * 128:
                                       par * S + (t + 1) * 128],
                         self.q[:, qc0:qc0 + w],
                         start=True, stop=True).then_inc(self.s_pe.h, 1)
            self.emit("tensor", f_s)
            mm_pe[g] = self.s_pe.inc(1)

            if t in uni:
                if w == 512:
                    pid = pat[(slot, t)][0]

                    def f_m(e, pss=pss, v=mm_pe[g], pid=pid, mg=mtab_ve):
                        e.wait_ge(self.s_pe.h, v)
                        e.wait_ge(self.s_ve.h, mg)
                        e.tensor_add(pss, pss,
                                     self.mtab[:, pid * 512:(pid + 1) * 512]
                                     ).then_inc(self.s_ve.h, 1)
                    self.emit("vector", f_m)
                    mask_ve[g] = self.s_ve.inc(1)
                else:
                    mp = ucol[0] % 2
                    mge = self.mgen[:, mp * 512: mp * 512 + w]
                    mbase = self.s_ve.n

                    def f_m(e, pss=pss, mge=mge, tcol=ucol[0], v=mm_pe[g],
                            w=w, base=mbase):
                        e.wait_ge(self.s_pe.h, v)
                        e.wait_ge(self.s_ve.h, base)
                        e.tensor_scalar(mge, self.iota[:, 0:w],
                                        self.ttab[:, tcol:tcol + 1], NEG,
                                        ALU.is_lt, ALU.mult
                                        ).then_inc(self.s_ve.h, 1)
                        e.wait_ge(self.s_ve.h, base + 1)
                        e.tensor_add(pss, pss, mge).then_inc(self.s_ve.h, 1)
                    self.emit("vector", f_m)
                    mask_ve[g] = self.s_ve.inc(2)
                ucol[0] += 1

            pg = av_pe.get(g - 4)
            gate = (("ve", mask_ve[g]) if g in mask_ve
                    else ("pe", mm_pe[g]))

            def f_e(e, pss=pss, col=bcol[0], g2=gate, pg=pg,
                    prb_t=self.prb[:, (g % 4) * 512: (g % 4) * 512 + w]):
                if g2[0] == "ve":
                    e.wait_ge(self.s_ve.h, g2[1])
                else:
                    e.wait_ge(self.s_pe.h, g2[1])
                if pg is not None:
                    e.wait_ge(self.s_pe.h, pg)
                e.activation(prb_t, pss, ACT.Exp,
                             bias=self.btab[:, col:col + 1],
                             scale=1.0).then_inc(self.s_se.h, 1)
            self.emit("scalar", f_e)
            exp_se[g] = self.s_se.inc(1)
            bcol[0] += 1

        def emit_av(g):
            b, t = tiles[g]
            it, par, w, qc0, nt, uni, h, slot, j = binfo[b]
            first, last = (t == 0), (t == nt - 1)
            gates = [(self.s_se, exp_se[g])]
            if first:
                vg = vgate.pop(it, None)
                if vg is not None:
                    gates.append(vg)
                rv = recip_se.get(b - 1)
                if rv is not None:
                    gates.append((self.s_ve, rv))
                ev = evac_ve.get(b - 2)
                if ev is not None:
                    gates.append((self.s_ve, ev))

            def f_av(e, t=t, w=w, par=par, first=first, last=last,
                     gates=gates,
                     prb_t=self.prb[:, (g % 4) * 512: (g % 4) * 512 + w]):
                for sm, v in gates:
                    e.wait_ge(sm.h, v)
                e.matmul(self.ps_m[0:1, 0:w],
                         self.ones_c[:, :], prb_t, start=first, stop=last)
                e.matmul(self.ps_o[:, par * 512: par * 512 + w],
                         self.vsl[:, par * S + t * 128:
                                  par * S + (t + 1) * 128],
                         prb_t, start=first,
                         stop=last).then_inc(self.s_pe.h, 1)
            self.emit("tensor", f_av)
            av_pe[g] = self.s_pe.inc(1)
            if last:
                last_av[it] = av_pe[g]
                pend.append((b, av_pe[g]))

        def emit_fin(b, avv):
            it, par, w, qc0, nt, uni, h, slot, j = binfo[b]

            def f_r(e, w=w, v=avv):
                e.wait_ge(self.s_pe.h, v)
                e.reciprocal(self.rsum[0:1, 0:w],
                             self.ps_m[0:1, 0:w]).then_inc(self.s_ve.h, 1)
            self.emit("vector", f_r)
            recip_se[b] = rv = self.s_ve.inc(1)

            def f_bc(e, w=w, v=rv, g=norm_ve.get(b - 1)):
                e.wait_ge(self.s_ve.h, v)
                if g is not None:
                    e.wait_ge(self.s_ve.h, g)
                e.matmul(self.ps_bc[:, 0:w], self.ones_r[:, :],
                         self.rsum[0:1, 0:w], start=True,
                         stop=True).then_inc(self.s_pe.h, 1)
            self.emit("tensor", f_bc)
            bcv = self.s_pe.inc(1)

            def f_ev(e, par=par, w=w, v=avv):
                e.wait_ge(self.s_pe.h, v)
                e.tensor_copy(self.osb[:, 0:w],
                              self.ps_o[:, par * 512: par * 512 + w]
                              ).then_inc(self.s_ve.h, 1)
            self.emit("vector", f_ev)
            evac_ve[b] = self.s_ve.inc(1)

            def f_n(e, qc0=qc0, w=w, b_=bcv):
                e.wait_ge(self.s_pe.h, b_)
                e.tensor_mul(self.attn[:, qc0:qc0 + w], self.osb[:, 0:w],
                             self.ps_bc[:, 0:w]).then_inc(self.s_ve.h, 1)
            self.emit("vector", f_n)
            norm_ve[b] = self.s_ve.inc(1)

        for g in range(N):
            b, t = tiles[g]
            if t == 0:
                emit_slab(b)
            emit_score(g)
            while pend:
                emit_fin(*pend.pop(0))
            if g >= LAG:
                emit_av(g - LAG)
        for g in range(N - LAG, N):
            emit_av(g)
            while pend:
                emit_fin(*pend.pop(0))
        self.attn_ve = self.s_ve.n

    def _wo_ln2(self):
        self._tr_ve = {}
        wtr = {}

        def wo_cb(oc, psb, mm):
            par = oc % 2

            def f(e, psb=psb, v=mm, par=par, oc=oc):
                e.wait_ge(self.s_pe.h, v)
                e.activation(self.aofm[:, par * TOK:(par + 1) * TOK], psb,
                             ACT.Identity,
                             bias=self.boT[:, oc:oc + 1]).then_inc(self.s_se.h, 1)
            self.emit("scalar", f)
            se = self.s_se.inc(1)
            for tc in range(8):
                src = self.aofm[:, par * TOK + tc * 128:
                                par * TOK + (tc + 1) * 128]
                dst = self.ao_tm[:, tc * E + oc * 128: tc * E + (oc + 1) * 128]
                self._tr_block(src, dst, gate_se=(se if tc == 0 else None),
                               ps=self.ps_t16)
            return ("se", se)
        self._proj16(self.wo, self.attn, self.attn_ve, wo_cb)
        ao_ve = self.s_ve.n
        self.wait("vector", self.s_ve, ao_ve)

        x1v = {}
        addv = {}
        for tc in range(8):
            par = tc % 2
            xr = self.wsl[:, par * E:(par + 1) * E]
            if tc == 0:
                self.wait("sync", self.s_ve, ao_ve)
            if tc >= 2:
                self.wait("sync", self.s_ve, addv[tc - 2])
            sm = self.s_wsl[par]
            self.emit("sync", lambda e, d=xr, h=sm.h,
                      s=self.x_in[tc * 128:(tc + 1) * 128, :]:
                      e.dma_start(out=d, in_=s).then_inc(h, 16))
            t = sm.inc(16)
            se_ln = self._ln_tm(self.ao_tm[:, tc * E:(tc + 1) * E],
                                self.scr[:, :],
                                scr_guard=addv.get(tc - 1))

            x1r = self.x1row[:, par * E:(par + 1) * E]
            dguard = 16 * (tc - 1) if tc >= 2 else None

            def f_add(e, xr=xr, v=se_ln, t=t, h=sm.h, x1r=x1r, dg=dguard):
                e.wait_ge(self.s_se.h, v)
                e.wait_ge(h, t)
                if dg is not None:
                    e.wait_ge(self.s_x1s.h, dg)
                e.tensor_add(x1r, xr, self.scr[:, :]).then_inc(self.s_ve.h, 1)
            self.emit("vector", f_add)
            vv = self.s_ve.inc(1)
            addv[tc] = vv
            self.wait("sync", self.s_ve, vv)
            if tc >= 1:
                self.wait("sync", self.s_x1s, 16 * tc)
            self.emit("sync", lambda e, d=self.x1_buf[tc, :, :], src=x1r:
                      e.dma_start(out=d, in_=src).then_inc(self.s_x1s.h, 16))
            self.s_x1s.inc(16)
        self.x1_ve = self.s_ve.n

    def _ln3_and_cc1(self):
        lnv = {}
        mxv = {}
        for tc in range(8):
            par = tc % 2
            xm = self.s_x[1]
            xr = self.x1r[:, par * E:(par + 1) * E]
            self.wait("sync", self.s_x1s, 16 * (tc + 1))
            if tc >= 2:
                self.wait("sync", self.s_se, lnv[tc - 2])
            self.emit("sync", lambda e, d=xr, s=self.x1_buf[tc, :, :],
                      h=xm.h: e.dma_start(out=d, in_=s).then_inc(h, 16))
            t = xm.inc(16)
            self.wait("vector", xm, t)
            self.wait("scalar", xm, t)
            se_ln = self._ln_tm(xr, self.ln3[:, tc * E:(tc + 1) * E])
            lnv[tc] = se_ln
            mxv[tc] = self.gmax_acc(64 + tc, self.ln3[:, tc * E:(tc + 1) * E],
                                    waits=((self.s_se, se_ln),))

        self.gmax_fin(2, 64, 72)
        self.emit("gpsimd", lambda e: e.dma_start(
            out=self.ccx_in[:, :],
            in_=self.gred[0:1, 2:3]).then_inc(self.s_g.h, 16))
        gv = self.s_g.inc(16)
        self.wait("gpsimd", self.s_g, gv)
        self.emit("gpsimd", lambda e: e.collective_compute(
            "AllReduce", ALU.max, replica_groups=[list(range(NC))],
            ins=[self.ccx_in.ap().opt()],
            outs=[self.ccx_out.ap().opt()]).then_inc(self.s_cc.h, 1))
        cv = self.s_cc.inc(1)
        self.wait("gpsimd", self.s_cc, cv)
        self.emit("gpsimd", lambda e: e.dma_start(
            out=self.sml[0:1, 2:3],
            in_=self.ccx_out[:, :]).then_inc(self.s_g.h, 16))
        gv2 = self.s_g.inc(16)
        self.wait("vector", self.s_g, gv2)

        base = self.s_ve.n

        def f_sc(e, base=base):
            sml = self.sml
            k = [base]

            def step(inst):
                k[0] += 1
                inst.then_inc(self.s_ve.h, 1)
                e.wait_ge(self.s_ve.h, k[0])
            step(e.tensor_scalar(sml[0:1, 6:7], sml[0:1, 2:3], 1.0 / 448.0,
                                 1e-6, ALU.mult, ALU.max))
            step(e.tensor_scalar_mul(sml[0:1, 9:10], sml[0:1, 6:7], 2.0))
            step(e.reciprocal(sml[0:1, 9:10], sml[0:1, 9:10]))
            step(e.tensor_mul(sml[0:1, 10:11], sml[0:1, 4:5], sml[0:1, 6:7]))
            step(e.tensor_scalar_mul(sml[0:1, 10:11], sml[0:1, 10:11], 4.0))
        self.emit("vector", f_sc)
        scv = self.s_ve.inc(5)

        def f_b(e, v=scv):
            e.wait_ge(self.s_ve.h, v)
            e.matmul(self.ps_bc[:, 0:2], self.ones_r[:, :],
                     self.sml[0:1, 9:11], start=True,
                     stop=True).then_inc(self.s_pe.h, 1)
        self.emit("tensor", f_b)
        bv = self.s_pe.inc(1)

        def f_cp(e, v=bv):
            e.wait_ge(self.s_pe.h, v)
            e.tensor_copy(self.sbc[:, 2:4],
                          self.ps_bc[:, 0:2]).then_inc(self.s_ve.h, 1)
        self.emit("vector", f_cp)
        self.sbcx_ve = self.s_ve.inc(1)

        qv = {}
        for tc in range(8):
            for ec in range(EC):
                i = tc * EC + ec
                sl = i % 3
                pst = self.ps_tr32[:, sl * 512: sl * 512 + 128]
                src = self.ln3[:, tc * E + ec * 128: tc * E + (ec + 1) * 128]
                dst = self.ln3_8[:, ec * TOK + tc * 128:
                                 ec * TOK + (tc + 1) * 128]
                g = self._pslot_se.get(sl)

                def f_pe(e, pst=pst, src=src, g=g,
                         gse=(lnv[tc] if ec == 0 else None)):
                    if gse is not None:
                        e.wait_ge(self.s_se.h, gse)
                    if g is not None:
                        e.wait_ge(self.s_se.h, g)
                    e.transpose(pst, src,
                                self.ident32[:, :]).then_inc(self.s_pe.h, 1)
                self.emit("tensor", f_pe)
                pv2 = self.s_pe.inc(1)

                def f_q(e, pst=pst, dst=dst, v=pv2,
                        sg=(self.sbcx_ve if i == 0 else None)):
                    e.wait_ge(self.s_pe.h, v)
                    if sg is not None:
                        e.wait_ge(self.s_ve.h, sg)
                    e.activation(dst, pst, ACT.Identity,
                                 scale=self.sbc[:, 2:3]).then_inc(self.s_se.h, 1)
                self.emit("scalar", f_q)
                qv[i] = self._pslot_se[sl] = self.s_se.inc(1)
        self.ln38_se = self.s_se.n

    def _ffn1(self):
        self.wait("scalar", self.s_se, self.ln38_se)
        q8v = {}
        relu_ve = {}
        gdma = {}
        mxg = {}
        NSW = 8
        NP = EC // 2
        for fg in range(32):
            for p in range(NP):
                i = fg * NP + p
                sl = i % NSW
                wsl = self.wfsl[:, sl * 512:(sl + 1) * 512]
                w8 = self.wf8[:, sl * 512:(sl + 1) * 512]
                g = q8v.get(i - NSW)
                if g is not None:
                    self.wait("sync", self.s_se, g)
                sm = self.s_ws[sl]
                self.emit("sync", lambda e, d=wsl, s=self.wfc[fg, p, :, :],
                          h=sm.h: e.dma_start(out=d, in_=s).then_inc(h, 16))
                t = sm.inc(16)
                pg = self._f1pe.get(i - NSW)

                def f_q(e, wsl=wsl, w8=w8, t=t, pg=pg, h=sm.h,
                        sg=(self.sbcw_ve if i == 0 else None)):
                    e.wait_ge(h, t)
                    if sg is not None:
                        e.wait_ge(self.s_ve.h, sg)
                    if pg is not None:
                        e.wait_ge(self.s_pe.h, pg)
                    e.activation(w8, wsl, ACT.Identity,
                                 scale=self.sbc[:, 0:1]).then_inc(self.s_se.h, 1)
                self.emit("scalar", f_q)
                q8v[i] = self.s_se.inc(1)

                def f_pe(e, w8=w8, p=p, v=q8v[i], fg=fg,
                         rg=relu_ve.get(fg - 1)):
                    e.wait_ge(self.s_se.h, v)
                    if p == 0 and rg is not None:
                        e.wait_ge(self.s_ve.h, rg)
                    w3 = w8.rearrange('q (i m) -> q i m', i=2)
                    x3 = self.ln3_8[:, 2 * p * TOK: (2 * p + 2) * TOK
                                    ].rearrange('q (i n) -> q i n', i=2)
                    for fc2 in range(2):
                        lhs = w3[:, :, fc2 * 128:(fc2 + 1) * 128]
                        for hf in range(2):
                            mm_i = e.matmul(
                                self.ps_h[:, fc2 * 1024 + hf * 512:
                                          fc2 * 1024 + (hf + 1) * 512],
                                lhs,
                                x3[:, :, hf * 512:(hf + 1) * 512],
                                start=(p == 0), stop=(p == NP - 1),
                                perf_mode=mybir.MatmulPerfMode.DoubleRow)
                    mm_i.then_inc(self.s_pe.h, 1)
                self.emit("tensor", f_pe)
                self._f1pe[i] = self.s_pe.inc(1)
            mmv = self._f1pe[fg * NP + NP - 1]
            par = fg % 2
            gts = self.gt[:, par * 2048:(par + 1) * 2048]
            dg = gdma.get(fg - 2)

            gsm = self.s_gt[par]

            def f_relu(e, v=mmv, gts=gts, dg=dg, h=gsm.h):
                e.wait_ge(self.s_pe.h, v)
                if dg is not None:
                    e.wait_ge(h, dg)
                e.tensor_scalar_max(self.scr[:, :], self.ps_h[:, :],
                                    0.0).then_inc(self.s_ve.h, 1)
            self.emit("vector", f_relu)
            relu_ve[fg] = rv = self.s_ve.inc(1)

            def f_sq(e, v=rv, gts=gts, mg=mxg.get(fg - 2)):
                e.wait_ge(self.s_ve.h, v)
                if mg is not None:
                    e.wait_ge(self.s_g.h, mg)
                e.activation(gts, self.scr[:, :], ACT.Square,
                             scale=self.sbc[:, 3:4]).then_inc(self.s_se.h, 1)
            self.emit("scalar", f_sq)
            sq = self.s_se.inc(1)

            mxg[fg] = self.gmax_acc(72 + fg, gts, waits=((self.s_se, sq),))

            def f_d(e, fg=fg, gts=gts, h=gsm.h):
                e.dma_start(out=self.g_buf[2 * fg, :, :],
                            in_=gts[:, 0:1024]).then_inc(h, 16)
                e.dma_start(out=self.g_buf[2 * fg + 1, :, :],
                            in_=gts[:, 1024:2048]).then_inc(h, 16)
            self.emit("gpsimd", f_d)
            gdma[fg] = gsm.inc(32)
        self.g_alldma = [(sm, sm.n) for sm in self.s_gt]

        self.gmax_fin(3, 72, 104)
        self.emit("gpsimd", lambda e: e.dma_start(
            out=self.cc2_in[:, :],
            in_=self.gred[0:1, 3:4]).then_inc(self.s_g.h, 16))
        gv = self.s_g.inc(16)
        self.wait("gpsimd", self.s_g, gv)
        self.emit("gpsimd", lambda e: e.collective_compute(
            "AllReduce", ALU.max, replica_groups=[list(range(NC))],
            ins=[self.cc2_in.ap().opt()],
            outs=[self.cc2_out.ap().opt()]).then_inc(self.s_cc.h, 1))
        cv = self.s_cc.inc(1)
        self.wait("gpsimd", self.s_cc, cv)
        self.emit("gpsimd", lambda e: e.dma_start(
            out=self.sml[0:1, 15:16],
            in_=self.cc2_out[:, :]).then_inc(self.s_g.h, 16))
        gv2 = self.s_g.inc(16)
        self.wait("vector", self.s_g, gv2)

        base = self.s_ve.n

        def f_sc(e, base=base):
            sml = self.sml
            k = [base]

            def step(inst):
                k[0] += 1
                inst.then_inc(self.s_ve.h, 1)
                e.wait_ge(self.s_ve.h, k[0])
            step(e.tensor_scalar(sml[0:1, 13:14], sml[0:1, 15:16],
                                 1.0 / 448.0, 1e-6, ALU.mult, ALU.max))
            step(e.tensor_scalar_mul(sml[0:1, 11:12], sml[0:1, 13:14], 2.0))
            step(e.reciprocal(sml[0:1, 11:12], sml[0:1, 11:12]))
            step(e.tensor_mul(sml[0:1, 12:13], sml[0:1, 13:14],
                              sml[0:1, 5:6]))
            step(e.tensor_scalar_mul(sml[0:1, 12:13], sml[0:1, 12:13], 4.0))
        self.emit("vector", f_sc)
        scv = self.s_ve.inc(5)

        def f_b(e, v=scv):
            e.wait_ge(self.s_ve.h, v)
            e.matmul(self.ps_bc[:, 0:2], self.ones_r[:, :],
                     self.sml[0:1, 11:13], start=True,
                     stop=True).then_inc(self.s_pe.h, 1)
        self.emit("tensor", f_b)
        bv = self.s_pe.inc(1)

        def f_cp(e, v=bv):
            e.wait_ge(self.s_pe.h, v)
            e.tensor_copy(self.sbc[:, 4:6],
                          self.ps_bc[:, 0:2]).then_inc(self.s_ve.h, 1)
        self.emit("vector", f_cp)
        self.sbc2_ve = self.s_ve.inc(1)

    def __init_f1(self):
        pass

    def _ffn2(self):
        q8v = {}
        w8v = {}
        evac_se = {}
        tr_done = {}
        self._o2tr = {}
        NP2 = FC // 2
        for og in range(8):
            for p in range(NP2):
                if og == 0:
                    for fc in (2 * p, 2 * p + 1):
                        gi = self.gin[:, (fc % 4) * 1024:(fc % 4 + 1) * 1024]
                        g = q8v.get(fc - 4)
                        if g is not None:
                            self.wait("gpsimd", self.s_se, g)
                        if fc == 0:
                            for sm_, v_ in self.g_alldma:
                                self.wait("gpsimd", sm_, v_)
                        gsm = self.s_gin[fc % 4]

                        def f_gd(e, gi=gi, fc=fc, h=gsm.h):
                            e.dma_start(out=gi,
                                        in_=self.g_buf[fc, :, :]
                                        ).then_inc(h, 16)
                        self.emit("gpsimd", f_gd)
                        gv = gsm.inc(16)

                        def f_gq(e, gi=gi, fc=fc, v=gv, h=gsm.h,
                                 sg=(self.sbc2_ve if fc == 0 else None)):
                            e.wait_ge(h, v)
                            if sg is not None:
                                e.wait_ge(self.s_ve.h, sg)
                            e.activation(
                                self.g8[:, fc * 1024:(fc + 1) * 1024],
                                gi, ACT.Identity,
                                scale=self.sbc[:, 4:5]).then_inc(self.s_se.h, 1)
                        self.emit("scalar", f_gq)
                        q8v[fc] = self.s_se.inc(1)

                i = og * NP2 + p
                sl = i % 8
                wsl = self.wpsl[:, sl * 512:(sl + 1) * 512]
                w8 = self.wp8[:, sl * 512:(sl + 1) * 512]
                g = w8v.get(i - 8)
                if g is not None:
                    self.wait("sync", self.s_se, g)
                sm = self.s_ws[sl]
                self.emit("sync", lambda e, d=wsl, s=self.wpr[og, p, :, :],
                          h=sm.h: e.dma_start(out=d, in_=s).then_inc(h, 16))
                t = sm.inc(16)
                pg = self._f2pe.get(i - 8)

                def f_wq(e, wsl=wsl, w8=w8, t=t, pg=pg, h=sm.h,
                         sg=(self.sbc2_ve if i == 0 else None)):
                    e.wait_ge(h, t)
                    if sg is not None:
                        e.wait_ge(self.s_ve.h, sg)
                    if pg is not None:
                        e.wait_ge(self.s_pe.h, pg)
                    e.activation(w8, wsl, ACT.Identity,
                                 scale=self.sbc[:, 1:2]).then_inc(self.s_se.h, 1)
                self.emit("scalar", f_wq)
                w8v[i] = self.s_se.inc(1)
                gq = q8v.get(2 * p + 1) if og == 0 else None

                def f_pe(e, w8=w8, p=p, v=w8v[i], gq=gq,
                         eg=(evac_se.get(og - 1) if p == 0 else None)):
                    e.wait_ge(self.s_se.h, v if gq is None else max(v, gq))
                    if eg is not None:
                        e.wait_ge(self.s_se.h, eg)
                    w3 = w8.rearrange('q (i m) -> q i m', i=2)
                    x3 = self.g8[:, 2 * p * 1024: (2 * p + 2) * 1024
                                 ].rearrange('q (i n) -> q i n', i=2)
                    for oc2 in range(2):
                        lhs = w3[:, :, oc2 * 128:(oc2 + 1) * 128]
                        for hf in range(2):
                            mm_i = e.matmul(
                                self.ps_o2[:, oc2 * 1024 + hf * 512:
                                           oc2 * 1024 + (hf + 1) * 512],
                                lhs,
                                x3[:, :, hf * 512:(hf + 1) * 512],
                                start=(p == 0), stop=(p == NP2 - 1),
                                perf_mode=mybir.MatmulPerfMode.DoubleRow)
                    mm_i.then_inc(self.s_pe.h, 1)
                self.emit("tensor", f_pe)
                self._f2pe[i] = self.s_pe.inc(1)
            mmv = self._f2pe[og * NP2 + NP2 - 1]
            par = og % 2
            o2 = self.o2fm[:, par * 2048:(par + 1) * 2048]
            g = tr_done.get(og - 1)

            def f_ev(e, v=mmv, o2=o2, g=g):
                e.wait_ge(self.s_pe.h, v)
                if g is not None:
                    e.wait_ge(self.s_ve.h, g)
                e.activation(o2, self.ps_o2[:, :], ACT.Identity,
                             scale=self.sbc[:, 5:6]).then_inc(self.s_se.h, 1)
            self.emit("scalar", f_ev)
            evac_se[og] = se = self.s_se.inc(1)
            last = 0
            for i in range(16):
                oc2, tc = i // 8, i % 8
                src = o2[:, oc2 * 1024 + tc * 128: oc2 * 1024 + (tc + 1) * 128]
                dst = self.o2tm[:, tc * E + og * 256 + oc2 * 128:
                                tc * E + og * 256 + (oc2 + 1) * 128]
                last = self._tr_block(src, dst,
                                      gate_se=(se if i == 0 else None))
            tr_done[og] = last
        self.o2_ve = self.s_ve.n

    def _ln4_out(self):
        self.wait("vector", self.s_ve, self.o2_ve)
        self.wait("scalar", self.s_ve, self.o2_ve)
        yv = {}
        x1ld = {}
        for tc in range(8):
            par = tc % 2
            xm = self.s_x[par]
            xr = self.x1r2[:, par * E:(par + 1) * E]
            if tc >= 2:
                self.wait("sync", self.s_ve, x1ld[tc - 2][1])
            self.emit("sync", lambda e, d=xr, s=self.x1_buf[tc, :, :],
                      h=xm.h: e.dma_start(out=d, in_=s).then_inc(h, 16))
            t = xm.inc(16)
            yr = self.yrow[:, :]
            se_ln = self._ln_tm(self.o2tm[:, tc * E:(tc + 1) * E],
                                self.scr[:, :],
                                scr_guard=(yv.get(tc - 1) or [None, None])[1])
            g = yv.get(tc - 1)

            def f_add(e, yr=yr, v=se_ln, g=g, xm=xm, t=t, xr=xr):
                e.wait_ge(self.s_se.h, v)
                e.wait_ge(xm.h, t)
                if g is not None:
                    e.wait_ge(self.s_y.h, g[0])
                e.tensor_add(yr, xr, self.scr[:, :]).then_inc(self.s_ve.h, 1)
            self.emit("vector", f_add)
            vv = self.s_ve.inc(1)
            x1ld[tc] = (xm, vv)
            self.wait("sync", self.s_ve, vv)
            self.emit("sync", lambda e, d=self.y_out[tc * 128:(tc + 1) * 128, :],
                      s=yr: e.dma_start(out=d, in_=s).then_inc(self.s_y.h, 16))
            yv[tc] = (self.s_y.inc(16), vv)


def _host_inputs(inputs, c):
    import ml_dtypes
    x = np.asarray(inputs["L0_attn_x"])
    (b0, p0), (b1, p1) = core_chunks(c)
    xs = np.concatenate([x[b0, p0 * CH:(p0 + 1) * CH, :],
                         x[b1, p1 * CH:(p1 + 1) * CH, :]], axis=0)
    xs = np.ascontiguousarray(xs, dtype=np.float32)

    def blk(w):
        return np.ascontiguousarray(
            np.asarray(w).reshape(EC, 128, EC, 128).transpose(0, 3, 2, 1))

    def bT(b):
        return np.ascontiguousarray(np.asarray(b).reshape(EC, 128).T)

    bias, thr = build_tables(c)
    d = {
        "x": xs,
        "wq": _host_inputs.wq, "wk": _host_inputs.wk,
        "wv": _host_inputs.wv, "wo": _host_inputs.wo,
        "wfc": _host_inputs.wfc, "wpr": _host_inputs.wpr,
        "wfshm": np.ascontiguousarray(
            _host_inputs.wfc[4 * c:4 * c + 4].transpose(2, 0, 1, 3)
            .reshape(128, 16384)),
        "wpshm": np.ascontiguousarray(
            _host_inputs.wpr[c].transpose(1, 0, 2).reshape(128, 16384)),
        "bqT": bT(inputs["L0_attn_bq"]),
        "bkT": bT(inputs["L0_attn_bk"]),
        "boT": bT(inputs["L0_attn_bo"]),
        "bv": np.ascontiguousarray(
            np.asarray(inputs["L0_attn_bv"], dtype=np.float32)[None, :]),
        "ident": np.eye(128, dtype=np.float16),
        "ident32": np.eye(128, dtype=np.float32),
        "ones_col": np.ones((128, 1), dtype=ml_dtypes.bfloat16),
        "ones_row": np.ones((1, 128), dtype=np.float32),
        "iota512": np.ascontiguousarray(np.broadcast_to(
            np.arange(512, dtype=np.float32)[None, :], (128, 512))),
        "btab": bias,
        "ttab": thr,
    }
    return d


def kernel(**inputs):
    import ml_dtypes
    f32 = lambda a: np.ascontiguousarray(np.asarray(a), dtype=np.float32)
    _host_inputs.wq = f32(np.asarray(inputs["L0_attn_Wq"])
                          .reshape(EC, 128, EC, 128).transpose(0, 3, 2, 1))
    _host_inputs.wk = f32(np.asarray(inputs["L0_attn_Wk"])
                          .reshape(EC, 128, EC, 128).transpose(0, 3, 2, 1))
    _host_inputs.wv = f32(np.asarray(inputs["L0_attn_Wv"])
                          .reshape(EC, 128, EC, 128).transpose(0, 3, 2, 1))
    _host_inputs.wo = f32(np.asarray(inputs["L0_attn_Wo"])
                          .reshape(EC, 128, EC, 128).transpose(0, 3, 2, 1))
    _host_inputs.wfc = f32(np.asarray(inputs["L0_ffn_w_fc"])
                           .reshape(32, 256, 8, 2, 128)
                           .transpose(0, 2, 4, 3, 1).reshape(32, 8, 128, 512))
    _host_inputs.wpr = f32(np.asarray(inputs["L0_ffn_w_proj"])
                           .reshape(8, 256, 32, 2, 128)
                           .transpose(0, 2, 4, 3, 1).reshape(8, 32, 128, 512))

    b = Builder()
    nc = b.build()
    in_maps = [_host_inputs(inputs, c) for c in range(NC)]
    res = run_bass_kernel_spmd(nc, in_maps, core_ids=list(range(NC)))
    out = np.zeros((2, S, E), dtype=np.float32)
    for c in range(NC):
        y = np.asarray(res.results[c]["y"])
        (b0, p0), (b1, p1) = core_chunks(c)
        out[b0, p0 * CH:(p0 + 1) * CH, :] = y[0:CH]
        out[b1, p1 * CH:(p1 + 1) * CH, :] = y[CH:TOK]
    return out


# version 13
# speedup vs baseline: 2.0583x; 1.0091x over previous
import math
from contextlib import ExitStack

import numpy as np

import concourse.bass as bass
import concourse.mybir as mybir
from concourse.bass_utils import run_bass_kernel_spmd

F32 = mybir.dt.float32
BF16 = mybir.dt.bfloat16
FP8 = mybir.dt.float8e4
FP16 = mybir.dt.float16
AX = mybir.AxisListType.X
ACT = mybir.ActivationFunctionType
ALU = mybir.AluOpType

E = 2048
S = 4096
H = 16
NC = 8
CH = 512
TOK = 2 * CH
EC = E // 128
FF = 4 * E
FC = FF // 128
SQD = 1.0 / math.sqrt(128.0)
EPS = 1e-5
NEG = -1e30
ESLOT = [2048, 4096]
SLOPES = [2.0 ** (-0.5 * (h + 1)) for h in range(H)]
WH = [128, 128, 256, 256] + [512] * 12


def core_chunks(c):
    return [(c // 4, 3 - (c % 4)), (c // 4, 4 + (c % 4))]


def attn_blocks():
    out = []
    for h in range(H):
        w = WH[h]
        for slot in range(2):
            poss = range(4) if slot == 0 else range(4, 8)
            for j in range(CH // w):
                nt = ESLOT[slot] // 128
                uni = sorted({(p * CH + j * w) // 128 + d
                              for p in poss for d in range(w // 128)})
                out.append((h, slot, j, nt, uni))
    return out


def build_tables(c):
    chunks = core_chunks(c)
    p = np.arange(128, dtype=np.float64)
    bias_cols, thr_cols = [], []
    for (h, slot, j, nt, uni) in attn_blocks():
        w = WH[h]
        pos = chunks[slot][1]
        tq_lo = pos * CH + j * w
        tq_hi = tq_lo + w - 1
        tq_mid = tq_lo + w // 2
        sl = SLOPES[h]
        for t in range(nt):
            tk0 = 128 * t
            if tk0 > tq_hi:
                bias_cols.append(np.full(128, NEG, np.float32))
            else:
                bias_cols.append((sl * (tk0 + p - tq_mid)).astype(np.float32))
        for t in uni:
            tk0 = 128 * t
            if tk0 > tq_hi:
                thr_cols.append(np.full(128, 4096.0, np.float32))
            elif tk0 + 127 >= tq_lo:
                thr_cols.append(np.minimum(np.maximum(tk0 + p - tq_lo, 0.0),
                                           4096.0).astype(np.float32))
            else:
                thr_cols.append(np.zeros(128, np.float32))
    return (np.ascontiguousarray(np.stack(bias_cols, 1)),
            np.ascontiguousarray(np.stack(thr_cols, 1)))


class Ctr:
    def __init__(self, h):
        self.h = h
        self.n = 0

    def inc(self, k=1):
        self.n += k
        return self.n


class Builder:
    def __init__(self, stage=None):
        self.stage = stage
        self.nc = bass.Bass("TRN2", target_bir_lowering=False)
        self.ops = {e: [] for e in ("sync", "gpsimd", "tensor", "scalar",
                                    "vector")}
        nb, nu = build_tables(0)
        self.NB = nb.shape[1]
        self.NU = nu.shape[1]
        self._tr_ve = {}
        self._f1pe = {}
        self._f2pe = {}
        self._trs = 0
        self._trs3 = 0
        self._pslot_se = {}

    def emit(self, eng, fn):
        self.ops[eng].append(fn)

    def wait(self, eng, sem, val):
        self.ops[eng].append(lambda e, s=sem.h, v=val: e.wait_ge(s, v))

    def join(self):
        ph = self.s_ph
        tgt = ph.n + 5
        fences = {
            "tensor": [self.s_pe], "scalar": [self.s_se],
            "vector": [self.s_ve],
            "sync": [self.s_c, self.s_in, self.s_y, self.s_out, self.s_x1s]
                    + self.s_x + self.s_wsl + self.s_wm + self.s_kst
                    + self.s_vst + self.s_kv + self.s_vs + self.s_ws
                    + self.s_xl,
            "gpsimd": [self.s_g, self.s_cc] + self.s_gt + self.s_gin,
        }
        for eng in self.ops:
            pre = [(sm.h, sm.n) for sm in fences[eng] if sm.n > 0]

            def f(e, pre=pre, s=ph.h, v=tgt):
                for h, n in pre:
                    e.wait_ge(h, n)
                e.sem_inc(s, 1)
                e.wait_ge(s, v)
            self.ops[eng].append(f)
        ph.n = tgt

    def build(self):
        nc = self.nc
        dt = nc.dram_tensor
        self.x_in = dt("x", [TOK, E], F32, kind="ExternalInput")
        self.wq = dt("wq", [EC, 128, EC, 128], F32, kind="ExternalInput")
        self.wk = dt("wk", [EC, 128, EC, 128], F32, kind="ExternalInput")
        self.wv = dt("wv", [EC, 128, EC, 128], F32, kind="ExternalInput")
        self.wo = dt("wo", [EC, 128, EC, 128], F32, kind="ExternalInput")
        self.wfc = dt("wfc", [32, 8, 128, 512], F32, kind="ExternalInput")
        self.wpr = dt("wpr", [8, 32, 128, 512], F32, kind="ExternalInput")
        self.wfshm = dt("wfshm", [128, 16384], F32, kind="ExternalInput")
        self.wpshm = dt("wpshm", [128, 16384], F32, kind="ExternalInput")
        self.bq_in = dt("bqT", [128, EC], F32, kind="ExternalInput")
        self.bk_in = dt("bkT", [128, EC], F32, kind="ExternalInput")
        self.bo_in = dt("boT", [128, EC], F32, kind="ExternalInput")
        self.bv_in = dt("bv", [1, E], F32, kind="ExternalInput")
        self.id_in = dt("ident", [128, 128], FP16, kind="ExternalInput")
        self.id32_in = dt("ident32", [128, 128], F32, kind="ExternalInput")
        self.on_in = dt("ones_col", [128, 1], BF16, kind="ExternalInput")
        self.or_in = dt("ones_row", [1, 128], F32, kind="ExternalInput")
        self.io_in = dt("iota512", [128, 512], F32, kind="ExternalInput")
        self.bt_in = dt("btab", [128, self.NB], F32, kind="ExternalInput")
        self.tt_in = dt("ttab", [128, self.NU], F32, kind="ExternalInput")
        self.y_out = dt("y", [TOK, E], F32, kind="ExternalOutput")

        self.k_st = dt("k_st", [EC, 128, TOK], FP16)
        self.v_st = dt("v_st", [EC, 8, 128, 128], FP16)
        self.K_all = dt("K_all", [4, EC, 128, TOK], FP16)
        self.V_all = dt("V_all", [4, EC, 8, 128, 128], FP16)
        self.ccw_in = dt("ccw_in", [1, 2], F32)
        self.ccw_out = dt("ccw_out", [1, 2], F32)
        self.ccx_in = dt("ccx_in", [1, 1], F32)
        self.ccx_out = dt("ccx_out", [1, 1], F32)
        self.cc2_in = dt("cc2_in", [1, 1], F32)
        self.cc2_out = dt("cc2_out", [1, 1], F32)
        self.g_buf = dt("g_buf", [FC, 128, TOK], F32)
        self.x1_buf = dt("x1_buf", [8, 128, E], F32)

        es = ExitStack()
        self.es = es
        sem = lambda n: Ctr(es.enter_context(nc.semaphore(n)))
        self.s_in = sem("s_in")
        self.s_g = sem("s_g")
        self.s_c = sem("s_c")
        self.s_x = [sem("s_x0"), sem("s_x1")]
        self.s_wsl = [sem("s_wsl0"), sem("s_wsl1")]
        self.s_wm = [sem("s_wm0"), sem("s_wm1")]
        self.s_kst = [sem("s_kst0"), sem("s_kst1")]
        self.s_vst = [sem(f"s_vst{i}") for i in range(4)]
        self.s_kv = [sem("s_kv0"), sem("s_kv1")]
        self.s_vs = [sem("s_vs0"), sem("s_vs1")]
        self.s_x1s = sem("s_x1s")
        self.s_ws = [sem(f"s_ws{i}") for i in range(8)]
        self.s_xl = [sem(f"s_xl{i}") for i in range(3)]
        self.s_gt = [sem("s_gt0"), sem("s_gt1")]
        self.s_gin = [sem(f"s_gin{i}") for i in range(4)]
        self.s_y = sem("s_y")
        self.s_pe = sem("s_pe")
        self.s_se = sem("s_se")
        self.s_ve = sem("s_ve")
        self.s_cc = sem("s_cc")
        self.s_out = sem("s_out")
        self.s_ph = sem("s_ph")

        sb = lambda *a: es.enter_context(nc.sbuf_tensor(*a))
        self.ident = sb("identc", [128, 128], FP16)
        self.ident32 = sb("identc32", [128, 128], F32)
        self.ones_c = sb("ones_c", [128, 1], BF16)
        self.ones_r = sb("ones_r", [1, 128], F32)
        self.iota = sb("iota", [128, 512], F32)
        self.btab = sb("btab_s", [128, self.NB], F32)
        self.ttab = sb("ttab_s", [128, self.NU], F32)
        self.bqd = sb("bqd", [128, EC], F32)
        self.bkT = sb("bkT_s", [128, EC], F32)
        self.boT = sb("boT_s", [128, EC], F32)
        self.bvr = sb("bvr", [1, E], F32)
        self.bv_bc = sb("bv_bc", [128, E], FP16)
        self.scr = sb("scr", [128, E], F32)
        self.st8 = sb("st8", [128, 8], F32)
        self.sml = sb("sml", [1, 16], F32)
        self.sbc = sb("sbc", [128, 8], F32)
        self.gred = sb("gred", [1, 4], F32)
        self.gacc = sb("gacc", [1, 168], F32)
        self.wmr = sb("wmr", [128, 2 * 256], F32)
        self._pslot = {}
        self.ps_bc = es.enter_context(nc.psum_tensor("ps_bc", [128, 512], F32))

        self._consts()
        if self.stage == "consts":
            self._dump(self.bv_bc, 2)
            return self._finish(es, [])
        self._wmax_pass()
        if self.stage == "wmax":
            self._dump(self.bv_bc, 2)
            return self._finish(es, [])

        at = ExitStack()
        self.attn = at.enter_context(
            nc.sbuf_tensor("attn", [128, H * TOK], FP16))
        lvl1 = ExitStack()
        s1 = lambda *a: lvl1.enter_context(nc.sbuf_tensor(*a))
        self.q = s1("q", [128, H * TOK], FP16)

        pj = ExitStack()
        s2 = lambda *a: pj.enter_context(nc.sbuf_tensor(*a))
        self.ln1T = s2("ln1T", [128, EC * TOK], FP16)
        self.xr = s2("xr", [128, 3 * E], F32)
        self.lnr = s2("lnr", [128, 2 * E], FP16)
        self.wsl = s2("wsl", [128, 3 * E], F32)
        self.wslb = s2("wslb", [128, 3 * E], FP16)
        self.kst = s2("kst", [128, 2 * TOK], FP16)
        self.vfm = s2("vfm", [128, 2 * TOK], FP16)
        self.ev = s2("ev", [128, 4 * 128], FP16)
        self.ps_mm = pj.enter_context(
            nc.psum_tensor("ps_mm", [128, 2 * TOK], F32))
        self.ps_trb = pj.enter_context(
            nc.psum_tensor("ps_trb", [128, 3 * 1024], FP16))

        self._ln1()
        if self.stage == "ln1o":
            self._dump(self.ln1T)
            return self._finish(es, [pj, lvl1, at])
        self._proj_qkv()
        if self.stage == "q":
            self._dump(self.q)
            return self._finish(es, [pj, lvl1, at])
        self.join()
        pj.close()

        at2 = ExitStack()
        s4 = lambda *a: at2.enter_context(nc.sbuf_tensor(*a))
        self.ksl = s4("ksl", [128, 2 * S], FP16)
        self.vsl = s4("vsl", [128, 2 * S], FP16)
        self.prb = s4("prb", [128, 4 * 512], BF16)
        self.mgen = s4("mgen", [128, 2 * 512], F32)
        self.mtab = s4("mtab", [128, 32 * 512], BF16)
        self.rsum = s4("rsum", [1, 512], F32)
        self.osb = s4("osb", [128, 512], F32)
        self.ps_s = at2.enter_context(
            nc.psum_tensor("ps_s", [128, 4 * 512], F32))
        self.ps_o = at2.enter_context(
            nc.psum_tensor("ps_o", [128, 2 * 512], F32))
        self.ps_m = at2.enter_context(
            nc.psum_tensor("ps_m", [1, 512], F32))

        self._attention()
        if self.stage == "attn":
            self._dump(self.attn)
            return self._finish(es, [at2, lvl1, at])
        self.join()
        at2.close()
        lvl1.close()

        wo = ExitStack()
        s5 = lambda *a: wo.enter_context(nc.sbuf_tensor(*a))
        self.wsl = s5("wsl2", [128, 3 * E], F32)
        self.wslb = s5("wslb2", [128, 3 * E], FP16)
        self.aofm = s5("aofm", [128, 2 * TOK], FP16)
        self.ao_tm = s5("ao_tm", [128, 8 * E], FP16)
        self.x1row = s5("x1row", [128, 2 * E], F32)
        self.ln2d = s5("ln2d", [128, 2 * E], F32)
        self.ps_mm = wo.enter_context(
            nc.psum_tensor("ps_mm2", [128, 2 * TOK], F32))
        self.ps_t16 = wo.enter_context(
            nc.psum_tensor("ps_t16w", [128, 2048], FP16))
        self._wo_ln2()
        if self.stage == "x1":
            self._dump(self.ao_tm)
            return self._finish(es, [wo, at])
        self.join()
        wo.close()
        at.close()

        ffn = ExitStack()
        s6 = lambda *a: ffn.enter_context(nc.sbuf_tensor(*a))
        self.ln3 = s6("ln3", [128, 8 * E], F32)
        self.ln3_8 = s6("ln3_8", [128, EC * TOK], FP8)
        self.x1r = s6("x1r", [128, 2 * E], F32)
        self.ps_tr32 = ffn.enter_context(
            nc.psum_tensor("ps_tr32", [128, 1536], F32))
        self.wfsl = s6("wfsl", [128, 8 * 512], F32)
        self.wf8 = s6("wf8", [128, 8 * 512], FP8)
        self.gt = s6("gt", [128, 2 * 2048], F32)
        self.ps_h = ffn.enter_context(
            nc.psum_tensor("ps_h", [128, 2048], F32))
        self._ln3_and_cc1()
        if self.stage == "ln3":
            self._dump(self.ln3)
            return self._finish(es, [ffn])
        self._ffn1()
        self.join()
        ffn.close()
        ffn = ExitStack()

        f2 = ExitStack()
        s8 = lambda *a: f2.enter_context(nc.sbuf_tensor(*a))
        self.g8 = s8("g8", [128, FC * 1024], FP8)
        self.o2tm = s8("o2tm", [128, 8 * E], FP16)
        self.gin = s8("gin", [128, 4 * TOK], F32)
        self.ps_trb = f2.enter_context(
            nc.psum_tensor("ps_trb2", [128, 2048], FP16))
        self.wpsl = s8("wpsl", [128, 8 * 512], F32)
        self.wp8 = s8("wp8", [128, 8 * 512], FP8)
        self.o2fm = s8("o2fm", [128, 2 * 2048], FP16)
        self.yrow = s8("yrow", [128, 2 * E], F32)
        self.x1r2 = s8("x1r2", [128, 2 * E], F32)
        self.ps_o2 = f2.enter_context(
            nc.psum_tensor("ps_o2", [128, 2048], F32))
        self._ffn2()
        self._ln4_out()
        return self._finish(es, [f2, ffn])

    def _finish(self, es, extra):
        self.wait("sync", self.s_out, self.s_out.n)
        self.wait("gpsimd", self.s_out, self.s_out.n)
        self.wait("sync", self.s_y, self.s_y.n)
        self.wait("gpsimd", self.s_y, self.s_y.n)
        with self.nc.Block() as block:
            block.sync(lambda e: [f(e) for f in self.ops["sync"]])
            block.gpsimd(lambda e: [f(e) for f in self.ops["gpsimd"]])
            block.tensor(lambda e: [f(e) for f in self.ops["tensor"]])
            block.scalar(lambda e: [f(e) for f in self.ops["scalar"]])
            block.vector(lambda e: [f(e) for f in self.ops["vector"]])
        for x in extra:
            x.close()
        es.close()
        return self.nc

    def _dump(self, tile, nko=16):
        self.join()
        yv = self.y_out.ap().rearrange("(p r) c -> p (r c)", p=128)
        prev = None
        for i in range(nko):
            sl = tile[:, i * 1024:(i + 1) * 1024]

            def f(e, d=self.scr[:, 0:1024], s=sl, g=prev):
                if g is not None:
                    e.wait_ge(self.s_out.h, g)
                e.tensor_copy(d, s).then_inc(self.s_ve.h, 1)
            self.emit("vector", f)
            v = self.s_ve.inc(1)
            self.wait("sync", self.s_ve, v)
            self.emit("sync", lambda e, d=yv[:, i * 1024:(i + 1) * 1024],
                      s=self.scr[:, 0:1024]:
                      e.dma_start(out=d, in_=s).then_inc(self.s_out.h, 16))
            prev = self.s_out.inc(16)

    def _consts(self):
        pairs = [
            (self.ident[:, :], self.id_in[:, :]),
            (self.ident32[:, :], self.id32_in[:, :]),
            (self.ones_c[:, :], self.on_in[:, :]),
            (self.ones_r[:, :], self.or_in[:, :]),
            (self.iota[:, :], self.io_in[:, :]),
            (self.btab[:, :], self.bt_in[:, :]),
            (self.ttab[:, :], self.tt_in[:, :]),
            (self.bqd[:, :], self.bq_in[:, :]),
            (self.bkT[:, :], self.bk_in[:, :]),
            (self.boT[:, :], self.bo_in[:, :]),
            (self.bvr[:, :], self.bv_in[:, :]),
        ]

        def f(e, pairs=pairs):
            for d, s in pairs:
                e.dma_start(out=d, in_=s).then_inc(self.s_c.h, 16)
        self.emit("sync", f)
        need = self.s_c.inc(16 * len(pairs))

        ve0 = self.s_ve.n

        def f_pe(e, need=need, ve0=ve0):
            e.wait_ge(self.s_c.h, need)
            for i in range(4):
                if i >= 1:
                    e.wait_ge(self.s_ve.h, ve0 + i)
                e.matmul(self.ps_bc[:, :], self.ones_r[:, :],
                         self.bvr[0:1, i * 512:(i + 1) * 512],
                         start=True, stop=True).then_inc(self.s_pe.h, 1)
        pe0 = self.s_pe.n

        def f_ve(e, pe0=pe0):
            for i in range(4):
                e.wait_ge(self.s_pe.h, pe0 + i + 1)
                e.tensor_copy(self.bv_bc[:, i * 512:(i + 1) * 512],
                              self.ps_bc[:, :]).then_inc(self.s_ve.h, 1)
        self.emit("tensor", f_pe)
        self.emit("vector", f_ve)
        self.s_pe.inc(4)
        self.bvbc_ve = self.s_ve.inc(4)
        self.wait("vector", self.s_c, need)
        self.emit("vector", lambda e: e.tensor_scalar_mul(
            self.bqd[:, :], self.bqd[:, :], SQD).then_inc(self.s_ve.h, 1))
        self.s_ve.inc(1)

    def gmax_acc(self, idx, src, waits=()):
        def f(e, src=src, idx=idx, waits=waits):
            for sm, v in waits:
                e.wait_ge(sm.h, v)
            e.tensor_reduce(self.gacc[0:1, idx:idx + 1], src,
                            axis=mybir.AxisListType.XYZWC,
                            op=ALU.abs_max).then_inc(self.s_g.h, 1)
        self.emit("gpsimd", f)
        return self.s_g.inc(1)

    def gmax_fin(self, lane, lo, hi):
        def f(e, lane=lane, lo=lo, hi=hi, v=self.s_g.n):
            e.wait_ge(self.s_g.h, v)
            e.tensor_reduce(self.gred[0:1, lane:lane + 1],
                            self.gacc[0:1, lo:hi],
                            axis=mybir.AxisListType.XYZWC,
                            op=ALU.max).then_inc(self.s_g.h, 1)
        self.emit("gpsimd", f)
        return self.s_g.inc(1)

    def _wmax_pass(self):
        slabs = [(0, self.wfshm[:, i * 256:(i + 1) * 256])
                 for i in range(64)]
        slabs += [(1, self.wpshm[:, i * 256:(i + 1) * 256])
                  for i in range(64)]
        dmav = {}

        def emit_red(i):
            par = i % 2
            dst = self.wmr[:, par * 256:(par + 1) * 256]
            self.gmax_acc(i, dst, waits=(dmav[i],))

        for i, (lane, src) in enumerate(slabs):
            par = i % 2
            dst = self.wmr[:, par * 256:(par + 1) * 256]
            sm = self.s_wm[par]
            self.emit("gpsimd", lambda e, d=dst, s=src, h=sm.h:
                      e.dma_start(out=d, in_=s).then_inc(h, 16))
            dmav[i] = (sm, sm.inc(16))
            if i >= 1:
                emit_red(i - 1)
        emit_red(127)
        self.gmax_fin(0, 0, 64)
        self.gmax_fin(1, 64, 128)
        self.wmax_g = self.s_g.n

    def _ccw_start(self):
        self.emit("gpsimd", lambda e: e.dma_start(
            out=self.ccw_in[:, :],
            in_=self.gred[0:1, 0:2]).then_inc(self.s_g.h, 16))
        gv = self.s_g.inc(16)
        self.wait("gpsimd", self.s_g, gv)
        self.emit("gpsimd", lambda e: e.collective_compute(
            "AllReduce", ALU.max, replica_groups=[list(range(NC))],
            ins=[self.ccw_in.ap().opt()],
            outs=[self.ccw_out.ap().opt()]).then_inc(self.s_cc.h, 1))
        cv = self.s_cc.inc(1)
        self.wait("gpsimd", self.s_cc, cv)
        self.emit("gpsimd", lambda e: e.dma_start(
            out=self.sml[0:1, 0:2],
            in_=self.ccw_out[:, :]).then_inc(self.s_g.h, 16))
        self._ccw_gv2 = self.s_g.inc(16)

    def _ccw_finish(self):
        gv2 = self._ccw_gv2
        self.wait("vector", self.s_g, gv2)

        base = self.s_ve.n

        def f_sc(e, base=base):
            sml = self.sml
            k = [base]

            def step(inst):
                k[0] += 1
                inst.then_inc(self.s_ve.h, 1)
                e.wait_ge(self.s_ve.h, k[0])
            step(e.tensor_scalar(sml[0:1, 4:5], sml[0:1, 0:1], 1.0 / 448.0,
                                 1e-6, ALU.mult, ALU.max))
            step(e.tensor_scalar(sml[0:1, 5:6], sml[0:1, 1:2], 1.0 / 448.0,
                                 1e-6, ALU.mult, ALU.max))
            step(e.tensor_scalar_mul(sml[0:1, 7:8], sml[0:1, 4:5], 2.0))
            step(e.reciprocal(sml[0:1, 7:8], sml[0:1, 7:8]))
            step(e.tensor_scalar_mul(sml[0:1, 8:9], sml[0:1, 5:6], 2.0))
            step(e.reciprocal(sml[0:1, 8:9], sml[0:1, 8:9]))
        self.emit("vector", f_sc)
        scv = self.s_ve.inc(6)

        def f_b(e, v=scv):
            e.wait_ge(self.s_ve.h, v)
            e.matmul(self.ps_bc[:, 0:2], self.ones_r[:, :],
                     self.sml[0:1, 7:9], start=True,
                     stop=True).then_inc(self.s_pe.h, 1)
        self.emit("tensor", f_b)
        bv = self.s_pe.inc(1)

        def f_cp(e, v=bv):
            e.wait_ge(self.s_pe.h, v)
            e.tensor_copy(self.sbc[:, 0:2],
                          self.ps_bc[:, 0:2]).then_inc(self.s_ve.h, 1)
        self.emit("vector", f_cp)
        self.sbcw_ve = self.s_ve.inc(1)

    def _ln_tm(self, src_ap, dst_ap, scr_guard=None, lane=0, sq=None):
        st = self.st8
        m = st[:, 3 * lane:3 * lane + 1]
        v1 = st[:, 3 * lane + 1:3 * lane + 2]
        v2 = st[:, 3 * lane + 2:3 * lane + 3]
        vb = self.s_ve.n
        self.emit("vector", lambda e, s=src_ap, vb=vb: (
            e.wait_ge(self.s_ve.h, vb),
            e.reduce_sum(m, s, axis=AX).then_inc(self.s_ve.h, 1)))
        vs = self.s_ve.inc(1)

        def f_se(e, s=src_ap, v=vs, g=scr_guard,
                 sq=(sq if sq is not None else self.scr[:, :])):
            e.wait_ge(self.s_ve.h, v)
            if g is not None:
                e.wait_ge(self.s_ve.h, g)
            e.activation(sq, s, ACT.Square,
                         accum_out=v1).then_inc(self.s_se.h, 1)
        self.emit("scalar", f_se)
        ss = self.s_se.inc(1)

        base = self.s_ve.n

        def f_v2(e, v=ss, base=base):
            e.wait_ge(self.s_se.h, v)
            e.tensor_scalar_mul(m, m, 1.0 / E).then_inc(self.s_ve.h, 1)
            e.tensor_scalar_mul(v1, v1, 1.0 / E).then_inc(self.s_ve.h, 1)
            e.wait_ge(self.s_ve.h, base + 1)
            e.scalar_tensor_tensor(v2, m, 1.0, m, ALU.mult,
                                   ALU.mult).then_inc(self.s_ve.h, 1)
            e.wait_ge(self.s_ve.h, base + 3)
            e.tensor_sub(v1, v1, v2).then_inc(self.s_ve.h, 1)
            e.wait_ge(self.s_ve.h, base + 4)
            e.tensor_scalar_add(v1, v1, EPS).then_inc(self.s_ve.h, 1)
        self.emit("vector", f_v2)
        vv = self.s_ve.inc(5)

        def f_s2(e, v=vv):
            e.wait_ge(self.s_ve.h, v)
            e.activation(v1, v1, ACT.Sqrt).then_inc(self.s_se.h, 1)
        self.emit("scalar", f_s2)
        s2 = self.s_se.inc(1)

        base3 = self.s_ve.n

        def f_v3(e, v=s2, base=base3):
            e.wait_ge(self.s_se.h, v)
            e.reciprocal(v1, v1).then_inc(self.s_ve.h, 1)
            e.wait_ge(self.s_ve.h, base + 1)
            e.scalar_tensor_tensor(v2, m, -1.0, v1, ALU.mult,
                                   ALU.mult).then_inc(self.s_ve.h, 1)
        self.emit("vector", f_v3)
        v3 = self.s_ve.inc(2)

        def f_s3(e, s=src_ap, d=dst_ap, v=v3):
            e.wait_ge(self.s_ve.h, v)
            e.activation(d, s, ACT.Identity, bias=v2,
                         scale=v1).then_inc(self.s_se.h, 1)
        self.emit("scalar", f_s3)
        return self.s_se.inc(1)

    def _tr_block(self, src, dst, gate_se=None, extra_ve=None, ps=None):
        if ps is None:
            ps = self.ps_trb
        sl = self._trs
        self._trs ^= 1
        pst = ps[:, sl * 1024: sl * 1024 + 128]
        gp = self._pslot.get((ps.name, sl))

        def f_pe(e, pst=pst, src=src, gse=gate_se, gp=gp):
            if gse is not None:
                e.wait_ge(self.s_se.h, gse)
            if gp is not None:
                e.wait_ge(self.s_ve.h, gp)
            e.transpose(pst, src, self.ident[:, :]).then_inc(self.s_pe.h, 1)
        self.emit("tensor", f_pe)
        pv = self.s_pe.inc(1)

        def f_ve(e, pst=pst, dst=dst, v=pv, vg=extra_ve):
            e.wait_ge(self.s_pe.h, v)
            if vg is not None:
                e.wait_ge(self.s_ve.h, vg)
            e.tensor_copy(dst, pst).then_inc(self.s_ve.h, 1)
        self.emit("vector", f_ve)
        last = self.s_ve.inc(1)
        self._pslot[(ps.name, sl)] = last
        return last

    def _transpose16(self, src_tile, src_col0, dst_cb, gate_se, tag):
        last = 0
        for i in range(16):
            src = src_tile[:, src_col0 + i * 128: src_col0 + (i + 1) * 128]
            dst, vg = dst_cb(i)
            last = self._tr_block(src, dst,
                                  gate_se=(gate_se if i == 0 else None),
                                  extra_ve=vg)
        return last

    def _ln1(self):
        self._tr_ve = {}
        tr_done = {}
        for tc in range(8):
            sl = tc % 3
            par = tc % 2
            xr = self.xr[:, sl * E:(sl + 1) * E]
            lr = self.lnr[:, par * E:(par + 1) * E]
            if tc >= 3:
                self.wait("sync", self.s_ve, tr_done[tc - 3])
            sm = self.s_xl[sl]
            self.emit("sync", lambda e, d=xr, h=sm.h,
                      s=self.x_in[tc * 128:(tc + 1) * 128, :]:
                      e.dma_start(out=d, in_=s).then_inc(h, 16))
            t = sm.inc(16)
            self.wait("vector", sm, t)
            self.wait("scalar", sm, t)
            se_ln = self._ln_tm(xr, lr, scr_guard=tr_done.get(tc - 2),
                                lane=tc % 2)

            def cb(i, tc=tc):
                return (self.ln1T[:, i * TOK + tc * 128:
                                  i * TOK + (tc + 1) * 128], None)
            tr_done[tc] = self._transpose16(self.lnr, par * E, cb, se_ln,
                                            f"ln1_{tc}")
        self.ln1_ve = self.s_ve.n

    def _proj16(self, w_hbm, rhs_tile, rhs_gate_ve, evac_cb):
        NSW = 3
        cast = {}
        gates = {}
        mms = {}
        prev = getattr(self, "_pj_prev", None)
        if prev is not None:
            gates[-2], gates[-1] = prev
        pm = getattr(self, "_pj_prev_mms", None)
        if pm is not None:
            for k in range(NSW):
                mms[-NSW + k] = pm[k]
        pc = getattr(self, "_pj_prev_cast", None)
        if pc is not None:
            for k in range(NSW):
                cast[-NSW + k] = pc[k]
        self._evac_sync = []
        for oc in range(EC):
            sl = oc % NSW
            par = oc % 2
            wsl = self.wsl[:, sl * E:(sl + 1) * E]
            wslb = self.wslb[:, sl * E:(sl + 1) * E]
            psb = self.ps_mm[:, par * TOK:(par + 1) * TOK]
            if cast.get(oc - NSW) is not None:
                self.wait("sync", self.s_ve, cast[oc - NSW])
            sm = self.s_ws[sl]
            self.emit("sync", lambda e, d=wsl, s=w_hbm[oc, :, :, :], h=sm.h:
                      e.dma_start(out=d, in_=s).then_inc(h, 16))
            t = sm.inc(16)
            for fl in self._evac_sync:
                fl()
            self._evac_sync = []
            self.wait("vector", sm, t)
            pmm = mms.get(oc - NSW)
            self.emit("vector", lambda e, d=wslb, s=wsl, pmm=pmm:
                      (pmm is not None and e.wait_ge(self.s_pe.h, pmm),
                       e.tensor_copy(d, s).then_inc(self.s_ve.h, 1)))
            cast[oc] = self.s_ve.inc(1)
            g = gates.get(oc - 2)

            def f_pe(e, wslb=wslb, psb=psb, cv=cast[oc], g=g,
                     rg=rhs_gate_ve, rhs=rhs_tile):
                e.wait_ge(self.s_ve.h, max(cv, rg))
                if g is not None:
                    e.wait_ge(self.s_se.h if g[0] == "se" else self.s_ve.h,
                              g[1])
                for ec in range(EC):
                    lhs = wslb[:, ec * 128:(ec + 1) * 128]
                    for hf in range(2):
                        mm_i = e.matmul(psb[:, hf * CH:(hf + 1) * CH], lhs,
                                        rhs[:, ec * TOK + hf * CH:
                                            ec * TOK + (hf + 1) * CH],
                                        start=(ec == 0), stop=(ec == EC - 1))
                mm_i.then_inc(self.s_pe.h, 1)
            self.emit("tensor", f_pe)
            mm = self.s_pe.inc(1)
            mms[oc] = mm
            gates[oc] = evac_cb(oc, psb, mm)
        for fl in self._evac_sync:
            fl()
        self._evac_sync = []
        self._pj_prev = (gates[EC - 2], gates[EC - 1])
        self._pj_prev_mms = tuple(mms[EC - NSW + k] for k in range(NSW))
        self._pj_prev_cast = tuple(cast[EC - NSW + k] for k in range(NSW))

    def _proj_qkv(self):
        kdma = {}

        vdma = {}
        vtr_pe = {}
        vtr_ve = {}

        def v_cb(oc, psb, mm):
            par = oc % 2
            fm = self.vfm[:, par * TOK:(par + 1) * TOK]
            g = vtr_pe.get(oc - 2)

            def f(e, fm=fm, psb=psb, v=mm, g=g):
                e.wait_ge(self.s_pe.h, v)
                if g is not None:
                    e.wait_ge(self.s_pe.h, g)
                e.tensor_copy(fm, psb).then_inc(self.s_ve.h, 1)
            self.emit("vector", f)
            fmv = self.s_ve.inc(1)
            for tc in range(8):
                sl = self._trs3
                self._trs3 = (self._trs3 + 1) % 3
                pst = self.ps_trb[:, sl * 1024: sl * 1024 + 128]
                src = fm[:, tc * 128:(tc + 1) * 128]
                evt = self.ev[:, (tc % 4) * 128:((tc % 4) + 1) * 128]
                gp = self._pslot.get((self.ps_trb.name, sl))

                def f_pe(e, pst=pst, src=src, fmv=fmv, g=gp):
                    e.wait_ge(self.s_ve.h, fmv)
                    if g is not None:
                        e.wait_ge(self.s_ve.h, g)
                    e.transpose(pst, src, self.ident[:, :]).then_inc(self.s_pe.h, 1)
                self.emit("tensor", f_pe)
                pv = self.s_pe.inc(1)
                vsm = self.s_vst[tc % 4]
                dg = vdma.get((oc, tc - 4), vdma.get((oc - 1, tc + 4)))

                def f_ve(e, pst=pst, evt=evt, v=pv, oc=oc, dg=dg, h=vsm.h):
                    e.wait_ge(self.s_pe.h, v)
                    if dg is not None:
                        e.wait_ge(h, dg)
                    e.scalar_tensor_tensor(
                        evt, pst, 1.0, self.bv_bc[:, oc * 128:(oc + 1) * 128],
                        ALU.mult, ALU.add).then_inc(self.s_ve.h, 1)
                self.emit("vector", f_ve)
                vv = self.s_ve.inc(1)
                self._pslot[(self.ps_trb.name, sl)] = vv

                def sync_part(vv=vv, evt=evt, oc=oc, tc=tc, h=vsm.h):
                    self.wait("sync", self.s_ve, vv)
                    self.emit("sync",
                              lambda e, d=self.v_st[oc, tc, :, :], s=evt:
                              e.dma_start(out=d, in_=s).then_inc(h, 16))
                self._evac_sync.append(sync_part)
                vdma[(oc, tc)] = vsm.inc(16)
            vtr_pe[oc] = self.s_pe.n
            return ("ve", fmv)
        self._proj16(self.wv, self.ln1T, self.ln1_ve, v_cb)
        self.v_staged = [(sm, sm.n) for sm in self.s_vst]
        self._ccw_start()
        self._gather_v()

        def k_cb(oc, psb, mm):
            par = oc % 2
            dst = self.kst[:, par * TOK:(par + 1) * TOK]
            g = kdma.get(oc - 2)

            sm = self.s_kst[par]

            def f(e, dst=dst, psb=psb, v=mm, oc=oc, g=g, h=sm.h):
                e.wait_ge(self.s_pe.h, v)
                if g is not None:
                    e.wait_ge(h, g)
                e.activation(dst, psb, ACT.Identity,
                             bias=self.bkT[:, oc:oc + 1]).then_inc(self.s_se.h, 1)
            self.emit("scalar", f)
            se = self.s_se.inc(1)

            def sync_part(se=se, dst=dst, oc=oc, h=sm.h):
                self.wait("sync", self.s_se, se)
                self.emit("sync", lambda e, d=self.k_st[oc, :, :], s=dst:
                          e.dma_start(out=d, in_=s).then_inc(h, 16))
            self._evac_sync.append(sync_part)
            kdma[oc] = sm.inc(16)
            return ("se", se)
        self._proj16(self.wk, self.ln1T, self.ln1_ve, k_cb)
        self.k_staged = [(sm, sm.n) for sm in self.s_kst]
        self._gather_k()

        def q_cb(oc, psb, mm):
            def f(e, oc=oc, psb=psb, v=mm):
                e.wait_ge(self.s_pe.h, v)
                e.activation(self.q[:, oc * TOK:(oc + 1) * TOK], psb,
                             ACT.Identity, bias=self.bqd[:, oc:oc + 1],
                             scale=SQD).then_inc(self.s_se.h, 1)
            self.emit("scalar", f)
            return ("se", self.s_se.inc(1))
        self._proj16(self.wq, self.ln1T, self.ln1_ve, q_cb)
        self._ccw_finish()

    def _gather_k(self):
        rg = [[0, 1, 2, 3], [4, 5, 6, 7]]
        for sm, v in self.k_staged:
            self.wait("gpsimd", sm, v)
        self.emit("gpsimd", lambda e: e.collective_compute(
            "AllGather", ALU.bypass, replica_groups=rg,
            ins=[self.k_st.ap().opt()],
            outs=[self.K_all.ap().opt()]).then_inc(self.s_cc.h, 1))
        self.k_cc = self.s_cc.inc(1)

    def _gather_v(self):
        rg = [[0, 1, 2, 3], [4, 5, 6, 7]]
        for sm, v in self.v_staged:
            self.wait("gpsimd", sm, v)
        self.emit("gpsimd", lambda e: e.collective_compute(
            "AllGather", ALU.bypass, replica_groups=rg,
            ins=[self.v_st.ap().opt()],
            outs=[self.V_all.ap().opt()]).then_inc(self.s_cc.h, 1))
        self.v_cc = self.s_cc.inc(1)

    def _attention(self):
        LAG = 3
        NS = 4
        blocks = attn_blocks()
        binfo = []
        tiles = []
        for b, (h, slot, j, nt, uni) in enumerate(blocks):
            binfo.append((2 * h + slot, (2 * h + slot) % 2, WH[h],
                          h * TOK + slot * CH + j * WH[h], nt, set(uni),
                          h, slot, j))
            for t in range(nt):
                tiles.append((b, t))
        N = len(tiles)

        mm_pe, exp_se, av_pe, mask_ve = {}, {}, {}, {}
        recip_se, evac_ve, norm_ve, last_av = {}, {}, {}, {}
        kgate, vgate = {}, {}
        seen_it = set()
        bcol = [0]
        ucol = [0]
        pend = []

        pat = {}
        uc = 0
        for (h, slot, j, nt, uni) in blocks:
            us = set(uni)
            for t in range(nt):
                if t in us:
                    if h == 4:
                        pat[(slot, t)] = (len(pat), uc)
                    uc += 1
        gen_base = self.s_ve.n

        def f_gen(e, items=sorted(pat.items(), key=lambda x: x[1][0]),
                  need=self.s_c.n, base=gen_base):
            e.wait_ge(self.s_c.h, need)
            e.wait_ge(self.s_ve.h, base)
            for (slot, t), (pid, col) in items:
                e.tensor_scalar(self.mtab[:, pid * 512:(pid + 1) * 512],
                                self.iota[:, 0:512],
                                self.ttab[:, col:col + 1], NEG,
                                ALU.is_lt, ALU.mult).then_inc(self.s_ve.h, 1)
        self.emit("vector", f_gen)
        mtab_ve = self.s_ve.inc(len(pat))

        def emit_slab(b):
            it, par, w, qc0, nt, uni, h, slot, j = binfo[b]
            if it in seen_it:
                return
            seen_it.add(it)
            if it >= 2:
                self.wait("gpsimd", self.s_pe, last_av[it - 2])
            if it == 0:
                self.wait("gpsimd", self.s_cc, self.k_cc)
            kd, vd = [], []
            for p5 in range(ESLOT[slot] // CH):
                rl = (3 - p5) if p5 <= 3 else (p5 - 4)
                lc = 0 if p5 <= 3 else CH
                kd.append((self.ksl[:, par * S + p5 * CH:
                                    par * S + (p5 + 1) * CH],
                           self.K_all[rl, h, :, lc:lc + CH]))
                for q4 in range(4):
                    g2 = p5 * 4 + q4
                    tcz = q4 + (0 if p5 <= 3 else 4)
                    vd.append((self.vsl[:, par * S + g2 * 128:
                                        par * S + (g2 + 1) * 128],
                               self.V_all[rl, h, tcz, :, :]))
            ksm, vsm = self.s_kv[par], self.s_vs[par]

            def f_dk(e, kd=kd, hh=ksm.h):
                for d, s in kd:
                    e.dma_start(out=d, in_=s).then_inc(hh, 16)
            self.emit("gpsimd", f_dk)
            kgate[it] = (ksm, ksm.inc(16 * len(kd)))
            if it == 0:
                self.wait("gpsimd", self.s_cc, self.v_cc)

            def f_dv(e, vd=vd, hh=vsm.h):
                for d, s in vd:
                    e.dma_start(out=d, in_=s).then_inc(hh, 16)
            self.emit("gpsimd", f_dv)
            vgate[it] = (vsm, vsm.inc(16 * len(vd)))

        def emit_score(g):
            b, t = tiles[g]
            it, par, w, qc0, nt, uni, h, slot, j = binfo[b]
            sp = g % NS
            pss = self.ps_s[:, sp * 512: sp * 512 + w]
            eg = exp_se.get(g - NS)
            kg = kgate.pop(it, None)

            def f_s(e, pss=pss, t=t, par=par, qc0=qc0, w=w, eg=eg, kg=kg):
                if kg is not None:
                    e.wait_ge(kg[0].h, kg[1])
                if eg is not None:
                    e.wait_ge(self.s_se.h, eg)
                e.matmul(pss, self.ksl[:, par * S + t * 128:
                                       par * S + (t + 1) * 128],
                         self.q[:, qc0:qc0 + w],
                         start=True, stop=True).then_inc(self.s_pe.h, 1)
            self.emit("tensor", f_s)
            mm_pe[g] = self.s_pe.inc(1)

            if t in uni:
                if w == 512:
                    pid = pat[(slot, t)][0]

                    def f_m(e, pss=pss, v=mm_pe[g], pid=pid, mg=mtab_ve):
                        e.wait_ge(self.s_pe.h, v)
                        e.wait_ge(self.s_ve.h, mg)
                        e.tensor_add(pss, pss,
                                     self.mtab[:, pid * 512:(pid + 1) * 512]
                                     ).then_inc(self.s_ve.h, 1)
                    self.emit("vector", f_m)
                    mask_ve[g] = self.s_ve.inc(1)
                else:
                    mp = ucol[0] % 2
                    mge = self.mgen[:, mp * 512: mp * 512 + w]
                    mbase = self.s_ve.n

                    def f_m(e, pss=pss, mge=mge, tcol=ucol[0], v=mm_pe[g],
                            w=w, base=mbase):
                        e.wait_ge(self.s_pe.h, v)
                        e.wait_ge(self.s_ve.h, base)
                        e.tensor_scalar(mge, self.iota[:, 0:w],
                                        self.ttab[:, tcol:tcol + 1], NEG,
                                        ALU.is_lt, ALU.mult
                                        ).then_inc(self.s_ve.h, 1)
                        e.wait_ge(self.s_ve.h, base + 1)
                        e.tensor_add(pss, pss, mge).then_inc(self.s_ve.h, 1)
                    self.emit("vector", f_m)
                    mask_ve[g] = self.s_ve.inc(2)
                ucol[0] += 1

            pg = av_pe.get(g - 4)
            gate = (("ve", mask_ve[g]) if g in mask_ve
                    else ("pe", mm_pe[g]))

            def f_e(e, pss=pss, col=bcol[0], g2=gate, pg=pg,
                    prb_t=self.prb[:, (g % 4) * 512: (g % 4) * 512 + w]):
                if g2[0] == "ve":
                    e.wait_ge(self.s_ve.h, g2[1])
                else:
                    e.wait_ge(self.s_pe.h, g2[1])
                if pg is not None:
                    e.wait_ge(self.s_pe.h, pg)
                e.activation(prb_t, pss, ACT.Exp,
                             bias=self.btab[:, col:col + 1],
                             scale=1.0).then_inc(self.s_se.h, 1)
            self.emit("scalar", f_e)
            exp_se[g] = self.s_se.inc(1)
            bcol[0] += 1

        def emit_av(g):
            b, t = tiles[g]
            it, par, w, qc0, nt, uni, h, slot, j = binfo[b]
            first, last = (t == 0), (t == nt - 1)
            gates = [(self.s_se, exp_se[g])]
            if first:
                vg = vgate.pop(it, None)
                if vg is not None:
                    gates.append(vg)
                rv = recip_se.get(b - 1)
                if rv is not None:
                    gates.append((self.s_ve, rv))
                ev = evac_ve.get(b - 2)
                if ev is not None:
                    gates.append((self.s_ve, ev))

            def f_av(e, t=t, w=w, par=par, first=first, last=last,
                     gates=gates,
                     prb_t=self.prb[:, (g % 4) * 512: (g % 4) * 512 + w]):
                for sm, v in gates:
                    e.wait_ge(sm.h, v)
                e.matmul(self.ps_m[0:1, 0:w],
                         self.ones_c[:, :], prb_t, start=first, stop=last)
                e.matmul(self.ps_o[:, par * 512: par * 512 + w],
                         self.vsl[:, par * S + t * 128:
                                  par * S + (t + 1) * 128],
                         prb_t, start=first,
                         stop=last).then_inc(self.s_pe.h, 1)
            self.emit("tensor", f_av)
            av_pe[g] = self.s_pe.inc(1)
            if last:
                last_av[it] = av_pe[g]
                pend.append((b, av_pe[g]))

        def emit_fin(b, avv):
            it, par, w, qc0, nt, uni, h, slot, j = binfo[b]

            def f_r(e, w=w, v=avv):
                e.wait_ge(self.s_pe.h, v)
                e.reciprocal(self.rsum[0:1, 0:w],
                             self.ps_m[0:1, 0:w]).then_inc(self.s_ve.h, 1)
            self.emit("vector", f_r)
            recip_se[b] = rv = self.s_ve.inc(1)

            def f_bc(e, w=w, v=rv, g=norm_ve.get(b - 1)):
                e.wait_ge(self.s_ve.h, v)
                if g is not None:
                    e.wait_ge(self.s_ve.h, g)
                e.matmul(self.ps_bc[:, 0:w], self.ones_r[:, :],
                         self.rsum[0:1, 0:w], start=True,
                         stop=True).then_inc(self.s_pe.h, 1)
            self.emit("tensor", f_bc)
            bcv = self.s_pe.inc(1)

            def f_ev(e, par=par, w=w, v=avv):
                e.wait_ge(self.s_pe.h, v)
                e.tensor_copy(self.osb[:, 0:w],
                              self.ps_o[:, par * 512: par * 512 + w]
                              ).then_inc(self.s_ve.h, 1)
            self.emit("vector", f_ev)
            evac_ve[b] = self.s_ve.inc(1)

            def f_n(e, qc0=qc0, w=w, b_=bcv):
                e.wait_ge(self.s_pe.h, b_)
                e.tensor_mul(self.attn[:, qc0:qc0 + w], self.osb[:, 0:w],
                             self.ps_bc[:, 0:w]).then_inc(self.s_ve.h, 1)
            self.emit("vector", f_n)
            norm_ve[b] = self.s_ve.inc(1)

        for g in range(N):
            b, t = tiles[g]
            if t == 0:
                emit_slab(b)
            emit_score(g)
            while pend:
                emit_fin(*pend.pop(0))
            if g >= LAG:
                emit_av(g - LAG)
        for g in range(N - LAG, N):
            emit_av(g)
            while pend:
                emit_fin(*pend.pop(0))
        self.attn_ve = self.s_ve.n

    def _wo_ln2(self):
        self._tr_ve = {}
        wtr = {}

        def wo_cb(oc, psb, mm):
            par = oc % 2

            def f(e, psb=psb, v=mm, par=par, oc=oc):
                e.wait_ge(self.s_pe.h, v)
                e.activation(self.aofm[:, par * TOK:(par + 1) * TOK], psb,
                             ACT.Identity,
                             bias=self.boT[:, oc:oc + 1]).then_inc(self.s_se.h, 1)
            self.emit("scalar", f)
            se = self.s_se.inc(1)
            for tc in range(8):
                src = self.aofm[:, par * TOK + tc * 128:
                                par * TOK + (tc + 1) * 128]
                dst = self.ao_tm[:, tc * E + oc * 128: tc * E + (oc + 1) * 128]
                self._tr_block(src, dst, gate_se=(se if tc == 0 else None),
                               ps=self.ps_t16)
            return ("se", se)
        self._proj16(self.wo, self.attn, self.attn_ve, wo_cb)
        ao_ve = self.s_ve.n
        self.wait("vector", self.s_ve, ao_ve)

        x1v = {}
        addv = {}
        for tc in range(8):
            par = tc % 2
            xr = self.wsl[:, par * E:(par + 1) * E]
            if tc == 0:
                self.wait("sync", self.s_ve, ao_ve)
            if tc >= 2:
                self.wait("sync", self.s_ve, addv[tc - 2])
            sm = self.s_wsl[par]
            self.emit("sync", lambda e, d=xr, h=sm.h,
                      s=self.x_in[tc * 128:(tc + 1) * 128, :]:
                      e.dma_start(out=d, in_=s).then_inc(h, 16))
            t = sm.inc(16)
            d2 = self.ln2d[:, par * E:(par + 1) * E]
            se_ln = self._ln_tm(self.ao_tm[:, tc * E:(tc + 1) * E], d2,
                                scr_guard=addv.get(tc - 2), lane=par, sq=d2)

            x1r = self.x1row[:, par * E:(par + 1) * E]
            dguard = 16 * (tc - 1) if tc >= 2 else None

            def f_add(e, xr=xr, v=se_ln, t=t, h=sm.h, x1r=x1r, dg=dguard,
                      par=par):
                e.wait_ge(self.s_se.h, v)
                e.wait_ge(h, t)
                if dg is not None:
                    e.wait_ge(self.s_x1s.h, dg)
                e.tensor_add(x1r, xr, self.ln2d[:, par * E:(par + 1) * E]
                             ).then_inc(self.s_ve.h, 1)
            self.emit("vector", f_add)
            vv = self.s_ve.inc(1)
            addv[tc] = vv
            self.wait("sync", self.s_ve, vv)
            if tc >= 1:
                self.wait("sync", self.s_x1s, 16 * tc)
            self.emit("sync", lambda e, d=self.x1_buf[tc, :, :], src=x1r:
                      e.dma_start(out=d, in_=src).then_inc(self.s_x1s.h, 16))
            self.s_x1s.inc(16)
        self.x1add = (addv[6], addv[7])
        self.x1_ve = self.s_ve.n

    def _ln3_and_cc1(self):
        lnv = {}
        mxv = {}
        for tc in range(8):
            par = tc % 2
            xm = self.s_x[1]
            xr = self.x1r[:, par * E:(par + 1) * E]
            self.wait("sync", self.s_x1s, 16 * (tc + 1))
            if tc >= 2:
                self.wait("sync", self.s_se, lnv[tc - 2])
            self.emit("sync", lambda e, d=xr, s=self.x1_buf[tc, :, :],
                      h=xm.h: e.dma_start(out=d, in_=s).then_inc(h, 16))
            t = xm.inc(16)
            self.wait("vector", xm, t)
            self.wait("scalar", xm, t)
            d3 = self.ln3[:, tc * E:(tc + 1) * E]
            se_ln = self._ln_tm(xr, d3, lane=tc % 2, sq=d3)
            lnv[tc] = se_ln
            mxv[tc] = self.gmax_acc(128 + tc, self.ln3[:, tc * E:(tc + 1) * E],
                                    waits=((self.s_se, se_ln),))

        self.gmax_fin(2, 128, 136)
        self.emit("gpsimd", lambda e: e.dma_start(
            out=self.ccx_in[:, :],
            in_=self.gred[0:1, 2:3]).then_inc(self.s_g.h, 16))
        gv = self.s_g.inc(16)
        self.wait("gpsimd", self.s_g, gv)
        self.emit("gpsimd", lambda e: e.collective_compute(
            "AllReduce", ALU.max, replica_groups=[list(range(NC))],
            ins=[self.ccx_in.ap().opt()],
            outs=[self.ccx_out.ap().opt()]).then_inc(self.s_cc.h, 1))
        cv = self.s_cc.inc(1)
        self.wait("gpsimd", self.s_cc, cv)
        self.emit("gpsimd", lambda e: e.dma_start(
            out=self.sml[0:1, 2:3],
            in_=self.ccx_out[:, :]).then_inc(self.s_g.h, 16))
        gv2 = self.s_g.inc(16)
        self.wait("vector", self.s_g, gv2)

        base = self.s_ve.n

        def f_sc(e, base=base):
            sml = self.sml
            k = [base]

            def step(inst):
                k[0] += 1
                inst.then_inc(self.s_ve.h, 1)
                e.wait_ge(self.s_ve.h, k[0])
            step(e.tensor_scalar(sml[0:1, 6:7], sml[0:1, 2:3], 1.0 / 448.0,
                                 1e-6, ALU.mult, ALU.max))
            step(e.tensor_scalar_mul(sml[0:1, 9:10], sml[0:1, 6:7], 2.0))
            step(e.reciprocal(sml[0:1, 9:10], sml[0:1, 9:10]))
            step(e.tensor_mul(sml[0:1, 10:11], sml[0:1, 4:5], sml[0:1, 6:7]))
            step(e.tensor_scalar_mul(sml[0:1, 10:11], sml[0:1, 10:11], 4.0))
        self.emit("vector", f_sc)
        scv = self.s_ve.inc(5)

        def f_b(e, v=scv):
            e.wait_ge(self.s_ve.h, v)
            e.matmul(self.ps_bc[:, 0:2], self.ones_r[:, :],
                     self.sml[0:1, 9:11], start=True,
                     stop=True).then_inc(self.s_pe.h, 1)
        self.emit("tensor", f_b)
        bv = self.s_pe.inc(1)

        def f_cp(e, v=bv):
            e.wait_ge(self.s_pe.h, v)
            e.tensor_copy(self.sbc[:, 2:4],
                          self.ps_bc[:, 0:2]).then_inc(self.s_ve.h, 1)
        self.emit("vector", f_cp)
        self.sbcx_ve = self.s_ve.inc(1)

        qv = {}
        for tc in range(8):
            for ec in range(EC):
                i = tc * EC + ec
                sl = i % 3
                pst = self.ps_tr32[:, sl * 512: sl * 512 + 128]
                src = self.ln3[:, tc * E + ec * 128: tc * E + (ec + 1) * 128]
                dst = self.ln3_8[:, ec * TOK + tc * 128:
                                 ec * TOK + (tc + 1) * 128]
                g = self._pslot_se.get(sl)

                def f_pe(e, pst=pst, src=src, g=g,
                         gse=(lnv[tc] if ec == 0 else None)):
                    if gse is not None:
                        e.wait_ge(self.s_se.h, gse)
                    if g is not None:
                        e.wait_ge(self.s_se.h, g)
                    e.transpose(pst, src,
                                self.ident32[:, :]).then_inc(self.s_pe.h, 1)
                self.emit("tensor", f_pe)
                pv2 = self.s_pe.inc(1)

                def f_q(e, pst=pst, dst=dst, v=pv2,
                        sg=(self.sbcx_ve if i == 0 else None)):
                    e.wait_ge(self.s_pe.h, v)
                    if sg is not None:
                        e.wait_ge(self.s_ve.h, sg)
                    e.activation(dst, pst, ACT.Identity,
                                 scale=self.sbc[:, 2:3]).then_inc(self.s_se.h, 1)
                self.emit("scalar", f_q)
                qv[i] = self._pslot_se[sl] = self.s_se.inc(1)
        self.ln38_se = self.s_se.n

    def _ffn1(self):
        self.wait("scalar", self.s_se, self.ln38_se)
        q8v = {}
        relu_ve = {}
        gdma = {}
        mxg = {}
        NSW = 8
        NP = EC // 2
        for fg in range(32):
            for p in range(NP):
                i = fg * NP + p
                sl = i % NSW
                wsl = self.wfsl[:, sl * 512:(sl + 1) * 512]
                w8 = self.wf8[:, sl * 512:(sl + 1) * 512]
                g = q8v.get(i - NSW)
                if g is not None:
                    self.wait("sync", self.s_se, g)
                sm = self.s_ws[sl]
                self.emit("sync", lambda e, d=wsl, s=self.wfc[fg, p, :, :],
                          h=sm.h: e.dma_start(out=d, in_=s).then_inc(h, 16))
                t = sm.inc(16)
                pg = self._f1pe.get(i - NSW)

                def f_q(e, wsl=wsl, w8=w8, t=t, pg=pg, h=sm.h,
                        sg=(self.sbcw_ve if i == 0 else None)):
                    e.wait_ge(h, t)
                    if sg is not None:
                        e.wait_ge(self.s_ve.h, sg)
                    if pg is not None:
                        e.wait_ge(self.s_pe.h, pg)
                    e.activation(w8, wsl, ACT.Identity,
                                 scale=self.sbc[:, 0:1]).then_inc(self.s_se.h, 1)
                self.emit("scalar", f_q)
                q8v[i] = self.s_se.inc(1)

                def f_pe(e, w8=w8, p=p, v=q8v[i], fg=fg,
                         rg=relu_ve.get(fg - 1)):
                    e.wait_ge(self.s_se.h, v)
                    if p == 0 and rg is not None:
                        e.wait_ge(self.s_ve.h, rg)
                    w3 = w8.rearrange('q (i m) -> q i m', i=2)
                    x3 = self.ln3_8[:, 2 * p * TOK: (2 * p + 2) * TOK
                                    ].rearrange('q (i n) -> q i n', i=2)
                    for fc2 in range(2):
                        lhs = w3[:, :, fc2 * 128:(fc2 + 1) * 128]
                        for hf in range(2):
                            mm_i = e.matmul(
                                self.ps_h[:, fc2 * 1024 + hf * 512:
                                          fc2 * 1024 + (hf + 1) * 512],
                                lhs,
                                x3[:, :, hf * 512:(hf + 1) * 512],
                                start=(p == 0), stop=(p == NP - 1),
                                perf_mode=mybir.MatmulPerfMode.DoubleRow)
                    mm_i.then_inc(self.s_pe.h, 1)
                self.emit("tensor", f_pe)
                self._f1pe[i] = self.s_pe.inc(1)
            mmv = self._f1pe[fg * NP + NP - 1]
            par = fg % 2
            gts = self.gt[:, par * 2048:(par + 1) * 2048]
            dg = gdma.get(fg - 2)

            gsm = self.s_gt[par]

            def f_relu(e, v=mmv, gts=gts, dg=dg, h=gsm.h):
                e.wait_ge(self.s_pe.h, v)
                if dg is not None:
                    e.wait_ge(h, dg)
                e.tensor_scalar_max(self.scr[:, :], self.ps_h[:, :],
                                    0.0).then_inc(self.s_ve.h, 1)
            self.emit("vector", f_relu)
            relu_ve[fg] = rv = self.s_ve.inc(1)

            def f_sq(e, v=rv, gts=gts, mg=mxg.get(fg - 2)):
                e.wait_ge(self.s_ve.h, v)
                if mg is not None:
                    e.wait_ge(self.s_g.h, mg)
                e.activation(gts, self.scr[:, :], ACT.Square,
                             scale=self.sbc[:, 3:4]).then_inc(self.s_se.h, 1)
            self.emit("scalar", f_sq)
            sq = self.s_se.inc(1)

            mxg[fg] = self.gmax_acc(136 + fg, gts, waits=((self.s_se, sq),))

            def f_d(e, fg=fg, gts=gts, h=gsm.h):
                e.dma_start(out=self.g_buf[2 * fg, :, :],
                            in_=gts[:, 0:1024]).then_inc(h, 16)
                e.dma_start(out=self.g_buf[2 * fg + 1, :, :],
                            in_=gts[:, 1024:2048]).then_inc(h, 16)
            self.emit("gpsimd", f_d)
            gdma[fg] = gsm.inc(32)
        self.g_alldma = [(sm, sm.n) for sm in self.s_gt]

        self.gmax_fin(3, 136, 168)
        self.emit("gpsimd", lambda e: e.dma_start(
            out=self.cc2_in[:, :],
            in_=self.gred[0:1, 3:4]).then_inc(self.s_g.h, 16))
        gv = self.s_g.inc(16)
        self.wait("gpsimd", self.s_g, gv)
        self.emit("gpsimd", lambda e: e.collective_compute(
            "AllReduce", ALU.max, replica_groups=[list(range(NC))],
            ins=[self.cc2_in.ap().opt()],
            outs=[self.cc2_out.ap().opt()]).then_inc(self.s_cc.h, 1))
        cv = self.s_cc.inc(1)
        self.wait("gpsimd", self.s_cc, cv)
        self.emit("gpsimd", lambda e: e.dma_start(
            out=self.sml[0:1, 15:16],
            in_=self.cc2_out[:, :]).then_inc(self.s_g.h, 16))
        gv2 = self.s_g.inc(16)
        self.wait("vector", self.s_g, gv2)

        base = self.s_ve.n

        def f_sc(e, base=base):
            sml = self.sml
            k = [base]

            def step(inst):
                k[0] += 1
                inst.then_inc(self.s_ve.h, 1)
                e.wait_ge(self.s_ve.h, k[0])
            step(e.tensor_scalar(sml[0:1, 13:14], sml[0:1, 15:16],
                                 1.0 / 448.0, 1e-6, ALU.mult, ALU.max))
            step(e.tensor_scalar_mul(sml[0:1, 11:12], sml[0:1, 13:14], 2.0))
            step(e.reciprocal(sml[0:1, 11:12], sml[0:1, 11:12]))
            step(e.tensor_mul(sml[0:1, 12:13], sml[0:1, 13:14],
                              sml[0:1, 5:6]))
            step(e.tensor_scalar_mul(sml[0:1, 12:13], sml[0:1, 12:13], 4.0))
        self.emit("vector", f_sc)
        scv = self.s_ve.inc(5)

        def f_b(e, v=scv):
            e.wait_ge(self.s_ve.h, v)
            e.matmul(self.ps_bc[:, 0:2], self.ones_r[:, :],
                     self.sml[0:1, 11:13], start=True,
                     stop=True).then_inc(self.s_pe.h, 1)
        self.emit("tensor", f_b)
        bv = self.s_pe.inc(1)

        def f_cp(e, v=bv):
            e.wait_ge(self.s_pe.h, v)
            e.tensor_copy(self.sbc[:, 4:6],
                          self.ps_bc[:, 0:2]).then_inc(self.s_ve.h, 1)
        self.emit("vector", f_cp)
        self.sbc2_ve = self.s_ve.inc(1)

    def __init_f1(self):
        pass

    def _ffn2(self):
        q8v = {}
        w8v = {}
        evac_se = {}
        tr_done = {}
        self._o2tr = {}
        NP2 = FC // 2
        for og in range(8):
            for p in range(NP2):
                if og == 0:
                    for fc in (2 * p, 2 * p + 1):
                        gi = self.gin[:, (fc % 4) * 1024:(fc % 4 + 1) * 1024]
                        g = q8v.get(fc - 4)
                        if g is not None:
                            self.wait("gpsimd", self.s_se, g)
                        if fc == 0:
                            for sm_, v_ in self.g_alldma:
                                self.wait("gpsimd", sm_, v_)
                        gsm = self.s_gin[fc % 4]

                        def f_gd(e, gi=gi, fc=fc, h=gsm.h):
                            e.dma_start(out=gi,
                                        in_=self.g_buf[fc, :, :]
                                        ).then_inc(h, 16)
                        self.emit("gpsimd", f_gd)
                        gv = gsm.inc(16)

                        def f_gq(e, gi=gi, fc=fc, v=gv, h=gsm.h,
                                 sg=(self.sbc2_ve if fc == 0 else None)):
                            e.wait_ge(h, v)
                            if sg is not None:
                                e.wait_ge(self.s_ve.h, sg)
                            e.activation(
                                self.g8[:, fc * 1024:(fc + 1) * 1024],
                                gi, ACT.Identity,
                                scale=self.sbc[:, 4:5]).then_inc(self.s_se.h, 1)
                        self.emit("scalar", f_gq)
                        q8v[fc] = self.s_se.inc(1)

                i = og * NP2 + p
                sl = i % 8
                wsl = self.wpsl[:, sl * 512:(sl + 1) * 512]
                w8 = self.wp8[:, sl * 512:(sl + 1) * 512]
                g = w8v.get(i - 8)
                if g is not None:
                    self.wait("sync", self.s_se, g)
                sm = self.s_ws[sl]
                self.emit("sync", lambda e, d=wsl, s=self.wpr[og, p, :, :],
                          h=sm.h: e.dma_start(out=d, in_=s).then_inc(h, 16))
                t = sm.inc(16)
                pg = self._f2pe.get(i - 8)

                def f_wq(e, wsl=wsl, w8=w8, t=t, pg=pg, h=sm.h,
                         sg=(self.sbc2_ve if i == 0 else None)):
                    e.wait_ge(h, t)
                    if sg is not None:
                        e.wait_ge(self.s_ve.h, sg)
                    if pg is not None:
                        e.wait_ge(self.s_pe.h, pg)
                    e.activation(w8, wsl, ACT.Identity,
                                 scale=self.sbc[:, 1:2]).then_inc(self.s_se.h, 1)
                self.emit("scalar", f_wq)
                w8v[i] = self.s_se.inc(1)
                gq = q8v.get(2 * p + 1) if og == 0 else None

                def f_pe(e, w8=w8, p=p, v=w8v[i], gq=gq,
                         eg=(evac_se.get(og - 1) if p == 0 else None)):
                    e.wait_ge(self.s_se.h, v if gq is None else max(v, gq))
                    if eg is not None:
                        e.wait_ge(self.s_se.h, eg)
                    w3 = w8.rearrange('q (i m) -> q i m', i=2)
                    x3 = self.g8[:, 2 * p * 1024: (2 * p + 2) * 1024
                                 ].rearrange('q (i n) -> q i n', i=2)
                    for oc2 in range(2):
                        lhs = w3[:, :, oc2 * 128:(oc2 + 1) * 128]
                        for hf in range(2):
                            mm_i = e.matmul(
                                self.ps_o2[:, oc2 * 1024 + hf * 512:
                                           oc2 * 1024 + (hf + 1) * 512],
                                lhs,
                                x3[:, :, hf * 512:(hf + 1) * 512],
                                start=(p == 0), stop=(p == NP2 - 1),
                                perf_mode=mybir.MatmulPerfMode.DoubleRow)
                    mm_i.then_inc(self.s_pe.h, 1)
                self.emit("tensor", f_pe)
                self._f2pe[i] = self.s_pe.inc(1)
            mmv = self._f2pe[og * NP2 + NP2 - 1]
            par = og % 2
            o2 = self.o2fm[:, par * 2048:(par + 1) * 2048]
            g = tr_done.get(og - 1)

            def f_ev(e, v=mmv, o2=o2, g=g):
                e.wait_ge(self.s_pe.h, v)
                if g is not None:
                    e.wait_ge(self.s_ve.h, g)
                e.activation(o2, self.ps_o2[:, :], ACT.Identity,
                             scale=self.sbc[:, 5:6]).then_inc(self.s_se.h, 1)
            self.emit("scalar", f_ev)
            evac_se[og] = se = self.s_se.inc(1)
            last = 0
            for i in range(16):
                oc2, tc = i // 8, i % 8
                src = o2[:, oc2 * 1024 + tc * 128: oc2 * 1024 + (tc + 1) * 128]
                dst = self.o2tm[:, tc * E + og * 256 + oc2 * 128:
                                tc * E + og * 256 + (oc2 + 1) * 128]
                last = self._tr_block(src, dst,
                                      gate_se=(se if i == 0 else None))
            tr_done[og] = last
        self.o2_ve = self.s_ve.n

    def _ln4_out(self):
        self.wait("vector", self.s_ve, self.o2_ve)
        self.wait("scalar", self.s_ve, self.o2_ve)
        yv = {}
        x1ld = {}
        for tc in range(8):
            par = tc % 2
            xm = self.s_x[par]
            xr = self.x1r2[:, par * E:(par + 1) * E]
            if tc >= 2:
                self.wait("sync", self.s_ve, x1ld[tc - 2][1])
            self.emit("sync", lambda e, d=xr, s=self.x1_buf[tc, :, :],
                      h=xm.h: e.dma_start(out=d, in_=s).then_inc(h, 16))
            t = xm.inc(16)
            yr = self.yrow[:, par * E:(par + 1) * E]
            d4 = self.gin[:, par * E:(par + 1) * E]
            se_ln = self._ln_tm(self.o2tm[:, tc * E:(tc + 1) * E], d4,
                                scr_guard=(yv.get(tc - 2) or [None, None])[1],
                                lane=par, sq=d4)
            g = yv.get(tc - 2)

            def f_add(e, yr=yr, v=se_ln, g=g, xm=xm, t=t, xr=xr, par=par):
                e.wait_ge(self.s_se.h, v)
                e.wait_ge(xm.h, t)
                if g is not None:
                    e.wait_ge(self.s_y.h, g[0])
                e.tensor_add(yr, xr, self.gin[:, par * E:(par + 1) * E]
                             ).then_inc(self.s_ve.h, 1)
            self.emit("vector", f_add)
            vv = self.s_ve.inc(1)
            x1ld[tc] = (xm, vv)
            self.wait("sync", self.s_ve, vv)
            self.emit("sync", lambda e, d=self.y_out[tc * 128:(tc + 1) * 128, :],
                      s=yr: e.dma_start(out=d, in_=s).then_inc(self.s_y.h, 16))
            yv[tc] = (self.s_y.inc(16), vv)


def _host_inputs(inputs, c):
    import ml_dtypes
    x = np.asarray(inputs["L0_attn_x"])
    (b0, p0), (b1, p1) = core_chunks(c)
    xs = np.concatenate([x[b0, p0 * CH:(p0 + 1) * CH, :],
                         x[b1, p1 * CH:(p1 + 1) * CH, :]], axis=0)
    xs = np.ascontiguousarray(xs, dtype=np.float32)

    def blk(w):
        return np.ascontiguousarray(
            np.asarray(w).reshape(EC, 128, EC, 128).transpose(0, 3, 2, 1))

    def bT(b):
        return np.ascontiguousarray(np.asarray(b).reshape(EC, 128).T)

    bias, thr = build_tables(c)
    d = {
        "x": xs,
        "wq": _host_inputs.wq, "wk": _host_inputs.wk,
        "wv": _host_inputs.wv, "wo": _host_inputs.wo,
        "wfc": _host_inputs.wfc, "wpr": _host_inputs.wpr,
        "wfshm": np.ascontiguousarray(
            _host_inputs.wfc[4 * c:4 * c + 4].transpose(2, 0, 1, 3)
            .reshape(128, 16384)),
        "wpshm": np.ascontiguousarray(
            _host_inputs.wpr[c].transpose(1, 0, 2).reshape(128, 16384)),
        "bqT": bT(inputs["L0_attn_bq"]),
        "bkT": bT(inputs["L0_attn_bk"]),
        "boT": bT(inputs["L0_attn_bo"]),
        "bv": np.ascontiguousarray(
            np.asarray(inputs["L0_attn_bv"], dtype=np.float32)[None, :]),
        "ident": np.eye(128, dtype=np.float16),
        "ident32": np.eye(128, dtype=np.float32),
        "ones_col": np.ones((128, 1), dtype=ml_dtypes.bfloat16),
        "ones_row": np.ones((1, 128), dtype=np.float32),
        "iota512": np.ascontiguousarray(np.broadcast_to(
            np.arange(512, dtype=np.float32)[None, :], (128, 512))),
        "btab": bias,
        "ttab": thr,
    }
    return d


def kernel(**inputs):
    import ml_dtypes
    f32 = lambda a: np.ascontiguousarray(np.asarray(a), dtype=np.float32)
    _host_inputs.wq = f32(np.asarray(inputs["L0_attn_Wq"])
                          .reshape(EC, 128, EC, 128).transpose(0, 3, 2, 1))
    _host_inputs.wk = f32(np.asarray(inputs["L0_attn_Wk"])
                          .reshape(EC, 128, EC, 128).transpose(0, 3, 2, 1))
    _host_inputs.wv = f32(np.asarray(inputs["L0_attn_Wv"])
                          .reshape(EC, 128, EC, 128).transpose(0, 3, 2, 1))
    _host_inputs.wo = f32(np.asarray(inputs["L0_attn_Wo"])
                          .reshape(EC, 128, EC, 128).transpose(0, 3, 2, 1))
    _host_inputs.wfc = f32(np.asarray(inputs["L0_ffn_w_fc"])
                           .reshape(32, 256, 8, 2, 128)
                           .transpose(0, 2, 4, 3, 1).reshape(32, 8, 128, 512))
    _host_inputs.wpr = f32(np.asarray(inputs["L0_ffn_w_proj"])
                           .reshape(8, 256, 32, 2, 128)
                           .transpose(0, 2, 4, 3, 1).reshape(8, 32, 128, 512))

    b = Builder()
    nc = b.build()
    in_maps = [_host_inputs(inputs, c) for c in range(NC)]
    res = run_bass_kernel_spmd(nc, in_maps, core_ids=list(range(NC)))
    out = np.zeros((2, S, E), dtype=np.float32)
    for c in range(NC):
        y = np.asarray(res.results[c]["y"])
        (b0, p0), (b1, p1) = core_chunks(c)
        out[b0, p0 * CH:(p0 + 1) * CH, :] = y[0:CH]
        out[b1, p1 * CH:(p1 + 1) * CH, :] = y[CH:TOK]
    return out


# version 14
# speedup vs baseline: 2.0728x; 1.0071x over previous
import math
from contextlib import ExitStack

import numpy as np

import concourse.bass as bass
import concourse.mybir as mybir
from concourse.bass_utils import run_bass_kernel_spmd

F32 = mybir.dt.float32
BF16 = mybir.dt.bfloat16
FP8 = mybir.dt.float8e4
FP16 = mybir.dt.float16
AX = mybir.AxisListType.X
ACT = mybir.ActivationFunctionType
ALU = mybir.AluOpType

E = 2048
S = 4096
H = 16
NC = 8
CH = 512
TOK = 2 * CH
EC = E // 128
FF = 4 * E
FC = FF // 128
SQD = 1.0 / math.sqrt(128.0)
EPS = 1e-5
NEG = -1e30
ESLOT = [2048, 4096]
SLOPES = [2.0 ** (-0.5 * (h + 1)) for h in range(H)]
WH = [128, 128, 256, 256] + [512] * 12


def core_chunks(c):
    return [(c // 4, 3 - (c % 4)), (c // 4, 4 + (c % 4))]


def attn_blocks():
    out = []
    for h in range(H):
        w = WH[h]
        for slot in range(2):
            poss = range(4) if slot == 0 else range(4, 8)
            for j in range(CH // w):
                nt = ESLOT[slot] // 128
                uni = sorted({(p * CH + j * w) // 128 + d
                              for p in poss for d in range(w // 128)})
                out.append((h, slot, j, nt, uni))
    return out


def build_tables(c):
    chunks = core_chunks(c)
    p = np.arange(128, dtype=np.float64)
    bias_cols, thr_cols = [], []
    for (h, slot, j, nt, uni) in attn_blocks():
        w = WH[h]
        pos = chunks[slot][1]
        tq_lo = pos * CH + j * w
        tq_hi = tq_lo + w - 1
        tq_mid = tq_lo + w // 2
        sl = SLOPES[h]
        for t in range(nt):
            tk0 = 128 * t
            if tk0 > tq_hi:
                bias_cols.append(np.full(128, NEG, np.float32))
            else:
                bias_cols.append((sl * (tk0 + p - tq_mid)).astype(np.float32))
        for t in uni:
            tk0 = 128 * t
            if tk0 > tq_hi:
                thr_cols.append(np.full(128, 4096.0, np.float32))
            elif tk0 + 127 >= tq_lo:
                thr_cols.append(np.minimum(np.maximum(tk0 + p - tq_lo, 0.0),
                                           4096.0).astype(np.float32))
            else:
                thr_cols.append(np.zeros(128, np.float32))
    return (np.ascontiguousarray(np.stack(bias_cols, 1)),
            np.ascontiguousarray(np.stack(thr_cols, 1)))


class Ctr:
    def __init__(self, h):
        self.h = h
        self.n = 0

    def inc(self, k=1):
        self.n += k
        return self.n


class Builder:
    def __init__(self, stage=None):
        self.stage = stage
        self.nc = bass.Bass("TRN2", target_bir_lowering=False)
        self.ops = {e: [] for e in ("sync", "gpsimd", "tensor", "scalar",
                                    "vector")}
        nb, nu = build_tables(0)
        self.NB = nb.shape[1]
        self.NU = nu.shape[1]
        self._tr_ve = {}
        self._f1pe = {}
        self._f2pe = {}
        self._trs = 0
        self._trs3 = 0
        self._pslot_se = {}

    def emit(self, eng, fn):
        self.ops[eng].append(fn)

    def wait(self, eng, sem, val):
        self.ops[eng].append(lambda e, s=sem.h, v=val: e.wait_ge(s, v))

    def join(self):
        ph = self.s_ph
        tgt = ph.n + 5
        fences = {
            "tensor": [self.s_pe], "scalar": [self.s_se],
            "vector": [self.s_ve],
            "sync": [self.s_c, self.s_in, self.s_y, self.s_out, self.s_x1s]
                    + self.s_x + self.s_wsl + self.s_wm + self.s_kst
                    + self.s_vst + self.s_kv + self.s_vs + self.s_ws
                    + self.s_xl,
            "gpsimd": [self.s_g, self.s_cc] + self.s_gt + self.s_gin,
        }
        for eng in self.ops:
            pre = [(sm.h, sm.n) for sm in fences[eng] if sm.n > 0]

            def f(e, pre=pre, s=ph.h, v=tgt):
                for h, n in pre:
                    e.wait_ge(h, n)
                e.sem_inc(s, 1)
                e.wait_ge(s, v)
            self.ops[eng].append(f)
        ph.n = tgt

    def build(self):
        nc = self.nc
        dt = nc.dram_tensor
        self.x_in = dt("x", [TOK, E], F32, kind="ExternalInput")
        self.wq = dt("wq", [EC, 128, EC, 128], F32, kind="ExternalInput")
        self.wk = dt("wk", [EC, 128, EC, 128], F32, kind="ExternalInput")
        self.wv = dt("wv", [EC, 128, EC, 128], F32, kind="ExternalInput")
        self.wo = dt("wo", [EC, 128, EC, 128], F32, kind="ExternalInput")
        self.wfc = dt("wfc", [32, 8, 128, 512], F32, kind="ExternalInput")
        self.wpr = dt("wpr", [8, 32, 128, 512], F32, kind="ExternalInput")
        self.wfshm = dt("wfshm", [128, 16384], F32, kind="ExternalInput")
        self.wpshm = dt("wpshm", [128, 16384], F32, kind="ExternalInput")
        self.bq_in = dt("bqT", [128, EC], F32, kind="ExternalInput")
        self.bk_in = dt("bkT", [128, EC], F32, kind="ExternalInput")
        self.bo_in = dt("boT", [128, EC], F32, kind="ExternalInput")
        self.bv_in = dt("bv", [1, E], F32, kind="ExternalInput")
        self.id_in = dt("ident", [128, 128], FP16, kind="ExternalInput")
        self.id32_in = dt("ident32", [128, 128], F32, kind="ExternalInput")
        self.on_in = dt("ones_col", [128, 1], BF16, kind="ExternalInput")
        self.or_in = dt("ones_row", [1, 128], F32, kind="ExternalInput")
        self.io_in = dt("iota512", [128, 512], F32, kind="ExternalInput")
        self.bt_in = dt("btab", [128, self.NB], F32, kind="ExternalInput")
        self.tt_in = dt("ttab", [128, self.NU], F32, kind="ExternalInput")
        self.y_out = dt("y", [TOK, E], F32, kind="ExternalOutput")

        self.k_st = dt("k_st", [EC, 128, TOK], FP16)
        self.v_st = dt("v_st", [EC, 8, 128, 128], FP16)
        self.K_all = dt("K_all", [4, EC, 128, TOK], FP16)
        self.V_all = dt("V_all", [4, EC, 8, 128, 128], FP16)
        self.ccw_in = dt("ccw_in", [1, 2], F32)
        self.ccw_out = dt("ccw_out", [1, 2], F32)
        self.ccx_in = dt("ccx_in", [1, 1], F32)
        self.ccx_out = dt("ccx_out", [1, 1], F32)
        self.cc2_in = dt("cc2_in", [1, 1], F32)
        self.cc2_out = dt("cc2_out", [1, 1], F32)
        self.g_buf = dt("g_buf", [FC, 128, TOK], F32)
        self.x1_buf = dt("x1_buf", [8, 128, E], F32)

        es = ExitStack()
        self.es = es
        sem = lambda n: Ctr(es.enter_context(nc.semaphore(n)))
        self.s_in = sem("s_in")
        self.s_g = sem("s_g")
        self.s_c = sem("s_c")
        self.s_x = [sem("s_x0"), sem("s_x1")]
        self.s_wsl = [sem("s_wsl0"), sem("s_wsl1")]
        self.s_wm = [sem("s_wm0"), sem("s_wm1")]
        self.s_kst = [sem("s_kst0"), sem("s_kst1")]
        self.s_vst = [sem(f"s_vst{i}") for i in range(4)]
        self.s_kv = [sem("s_kv0"), sem("s_kv1")]
        self.s_vs = [sem("s_vs0"), sem("s_vs1")]
        self.s_x1s = sem("s_x1s")
        self.s_ws = [sem(f"s_ws{i}") for i in range(8)]
        self.s_xl = [sem(f"s_xl{i}") for i in range(3)]
        self.s_gt = [sem("s_gt0"), sem("s_gt1")]
        self.s_gin = [sem(f"s_gin{i}") for i in range(4)]
        self.s_y = sem("s_y")
        self.s_pe = sem("s_pe")
        self.s_se = sem("s_se")
        self.s_ve = sem("s_ve")
        self.s_cc = sem("s_cc")
        self.s_out = sem("s_out")
        self.s_ph = sem("s_ph")

        sb = lambda *a: es.enter_context(nc.sbuf_tensor(*a))
        self.ident = sb("identc", [128, 128], FP16)
        self.ident32 = sb("identc32", [128, 128], F32)
        self.ones_c = sb("ones_c", [128, 1], BF16)
        self.ones_r = sb("ones_r", [1, 128], F32)
        self.iota = sb("iota", [128, 512], F32)
        self.btab = sb("btab_s", [128, self.NB], F32)
        self.ttab = sb("ttab_s", [128, self.NU], F32)
        self.bqd = sb("bqd", [128, EC], F32)
        self.bkT = sb("bkT_s", [128, EC], F32)
        self.boT = sb("boT_s", [128, EC], F32)
        self.bvr = sb("bvr", [1, E], F32)
        self.bv_bc = sb("bv_bc", [128, E], FP16)
        self.scr = sb("scr", [128, E], F32)
        self.st8 = sb("st8", [128, 16], F32)
        self.sml = sb("sml", [1, 16], F32)
        self.sbc = sb("sbc", [128, 8], F32)
        self.gred = sb("gred", [1, 4], F32)
        self.gacc = sb("gacc", [1, 168], F32)
        self.wmr = sb("wmr", [128, 2 * 256], F32)
        self._pslot = {}
        self.ps_bc = es.enter_context(nc.psum_tensor("ps_bc", [128, 512], F32))

        self._consts()
        if self.stage == "consts":
            self._dump(self.bv_bc, 2)
            return self._finish(es, [])
        self._wmax_pass()
        if self.stage == "wmax":
            self._dump(self.bv_bc, 2)
            return self._finish(es, [])

        at = ExitStack()
        self.attn = at.enter_context(
            nc.sbuf_tensor("attn", [128, H * TOK], FP16))
        lvl1 = ExitStack()
        s1 = lambda *a: lvl1.enter_context(nc.sbuf_tensor(*a))
        self.q = s1("q", [128, H * TOK], FP16)

        pj = ExitStack()
        s2 = lambda *a: pj.enter_context(nc.sbuf_tensor(*a))
        self.ln1T = s2("ln1T", [128, EC * TOK], FP16)
        self.xr = s2("xr", [128, 3 * E], F32)
        self.lnr = s2("lnr", [128, 2 * E], FP16)
        self.wsl = s2("wsl", [128, 3 * E], F32)
        self.wslb = s2("wslb", [128, 3 * E], FP16)
        self.kst = s2("kst", [128, 2 * TOK], FP16)
        self.vfm = s2("vfm", [128, 2 * TOK], FP16)
        self.ev = s2("ev", [128, 4 * 128], FP16)
        self.ps_mm = pj.enter_context(
            nc.psum_tensor("ps_mm", [128, 2 * TOK], F32))
        self.ps_trb = pj.enter_context(
            nc.psum_tensor("ps_trb", [128, 3 * 1024], FP16))

        self._ln1()
        if self.stage == "ln1o":
            self._dump(self.ln1T)
            return self._finish(es, [pj, lvl1, at])
        self._proj_qkv()
        if self.stage == "q":
            self._dump(self.q)
            return self._finish(es, [pj, lvl1, at])
        self.join()
        pj.close()

        at2 = ExitStack()
        s4 = lambda *a: at2.enter_context(nc.sbuf_tensor(*a))
        self.ksl = s4("ksl", [128, 2 * S], FP16)
        self.vsl = s4("vsl", [128, 2 * S], FP16)
        self.prb = s4("prb", [128, 4 * 512], BF16)
        self.mgen = s4("mgen", [128, 2 * 512], F32)
        self.mtab = s4("mtab", [128, 32 * 512], BF16)
        self.rsum = s4("rsum", [1, 512], F32)
        self.osb = s4("osb", [128, 512], F32)
        self.ps_s = at2.enter_context(
            nc.psum_tensor("ps_s", [128, 4 * 512], F32))
        self.ps_o = at2.enter_context(
            nc.psum_tensor("ps_o", [128, 2 * 512], F32))
        self.ps_m = at2.enter_context(
            nc.psum_tensor("ps_m", [1, 512], F32))

        self._attention()
        if self.stage == "attn":
            self._dump(self.attn)
            return self._finish(es, [at2, lvl1, at])
        self.join()
        at2.close()
        lvl1.close()

        mid = ExitStack()
        self.ln3 = mid.enter_context(
            nc.sbuf_tensor("ln3", [128, 8 * E], FP16))
        wo = ExitStack()
        s5 = lambda *a: wo.enter_context(nc.sbuf_tensor(*a))
        self.wsl = s5("wsl2", [128, 2 * E], F32)
        self.wslb = s5("wslb2", [128, 2 * E], FP16)
        self.aofm = s5("aofm", [128, 2 * TOK], FP16)
        self.ao_tm = s5("ao_tm", [128, 8 * E], FP16)
        self.x1row = s5("x1row", [128, 2 * E], F32)
        self.ln2d = s5("ln2d", [128, 2 * E], F32)
        self.ps_mm = wo.enter_context(
            nc.psum_tensor("ps_mm2", [128, 2 * TOK], F32))
        self.ps_t16 = wo.enter_context(
            nc.psum_tensor("ps_t16w", [128, 2048], FP16))
        self._wo_ln2()
        if self.stage == "x1":
            self._dump(self.ao_tm)
            return self._finish(es, [wo, mid, at])
        self.join()
        wo.close()

        ffn = ExitStack()
        s6 = lambda *a: ffn.enter_context(nc.sbuf_tensor(*a))
        self.ln3_8 = s6("ln3_8", [128, EC * TOK], FP8)
        self.ps_tr32 = ffn.enter_context(
            nc.psum_tensor("ps_tr32", [128, 4 * 512], FP16))
        self.wfsl = s6("wfsl", [128, 8 * 512], F32)
        self.wf8 = s6("wf8", [128, 8 * 512], FP8)
        self.gt = s6("gt", [128, 2 * 2048], F32)
        self.ps_h = ffn.enter_context(
            nc.psum_tensor("ps_h", [128, 2048], F32))
        self._ln3_and_cc1()
        if self.stage == "ln3":
            self._dump(self.ln3)
            return self._finish(es, [ffn, mid, at])
        self._ffn1()
        self.join()
        ffn.close()
        mid.close()
        at.close()
        ffn = ExitStack()

        f2 = ExitStack()
        s8 = lambda *a: f2.enter_context(nc.sbuf_tensor(*a))
        self.g8 = s8("g8", [128, FC * 1024], FP8)
        self.o2tm = s8("o2tm", [128, 8 * E], FP16)
        self.gin = s8("gin", [128, 4 * TOK], F32)
        self.ps_trb = f2.enter_context(
            nc.psum_tensor("ps_trb2", [128, 2048], FP16))
        self.wpsl = s8("wpsl", [128, 8 * 512], F32)
        self.wp8 = s8("wp8", [128, 8 * 512], FP8)
        self.o2fm = s8("o2fm", [128, 2 * 2048], FP16)
        self.yrow = s8("yrow", [128, 2 * E], F32)
        self.x1r2 = s8("x1r2", [128, 2 * E], F32)
        self.ps_o2 = f2.enter_context(
            nc.psum_tensor("ps_o2", [128, 2048], F32))
        self._ffn2()
        self._ln4_out()
        return self._finish(es, [f2, ffn])

    def _finish(self, es, extra):
        self.wait("sync", self.s_out, self.s_out.n)
        self.wait("gpsimd", self.s_out, self.s_out.n)
        self.wait("sync", self.s_y, self.s_y.n)
        self.wait("gpsimd", self.s_y, self.s_y.n)
        with self.nc.Block() as block:
            block.sync(lambda e: [f(e) for f in self.ops["sync"]])
            block.gpsimd(lambda e: [f(e) for f in self.ops["gpsimd"]])
            block.tensor(lambda e: [f(e) for f in self.ops["tensor"]])
            block.scalar(lambda e: [f(e) for f in self.ops["scalar"]])
            block.vector(lambda e: [f(e) for f in self.ops["vector"]])
        for x in extra:
            x.close()
        es.close()
        return self.nc

    def _dump(self, tile, nko=16):
        self.join()
        yv = self.y_out.ap().rearrange("(p r) c -> p (r c)", p=128)
        prev = None
        for i in range(nko):
            sl = tile[:, i * 1024:(i + 1) * 1024]

            def f(e, d=self.scr[:, 0:1024], s=sl, g=prev):
                if g is not None:
                    e.wait_ge(self.s_out.h, g)
                e.tensor_copy(d, s).then_inc(self.s_ve.h, 1)
            self.emit("vector", f)
            v = self.s_ve.inc(1)
            self.wait("sync", self.s_ve, v)
            self.emit("sync", lambda e, d=yv[:, i * 1024:(i + 1) * 1024],
                      s=self.scr[:, 0:1024]:
                      e.dma_start(out=d, in_=s).then_inc(self.s_out.h, 16))
            prev = self.s_out.inc(16)

    def _consts(self):
        pairs = [
            (self.ident[:, :], self.id_in[:, :]),
            (self.ident32[:, :], self.id32_in[:, :]),
            (self.ones_c[:, :], self.on_in[:, :]),
            (self.ones_r[:, :], self.or_in[:, :]),
            (self.iota[:, :], self.io_in[:, :]),
            (self.btab[:, :], self.bt_in[:, :]),
            (self.ttab[:, :], self.tt_in[:, :]),
            (self.bqd[:, :], self.bq_in[:, :]),
            (self.bkT[:, :], self.bk_in[:, :]),
            (self.boT[:, :], self.bo_in[:, :]),
            (self.bvr[:, :], self.bv_in[:, :]),
        ]

        def f(e, pairs=pairs):
            for d, s in pairs:
                e.dma_start(out=d, in_=s).then_inc(self.s_c.h, 16)
        self.emit("sync", f)
        need = self.s_c.inc(16 * len(pairs))

        ve0 = self.s_ve.n

        def f_pe(e, need=need, ve0=ve0):
            e.wait_ge(self.s_c.h, need)
            for i in range(4):
                if i >= 1:
                    e.wait_ge(self.s_ve.h, ve0 + i)
                e.matmul(self.ps_bc[:, :], self.ones_r[:, :],
                         self.bvr[0:1, i * 512:(i + 1) * 512],
                         start=True, stop=True).then_inc(self.s_pe.h, 1)
        pe0 = self.s_pe.n

        def f_ve(e, pe0=pe0):
            for i in range(4):
                e.wait_ge(self.s_pe.h, pe0 + i + 1)
                e.tensor_copy(self.bv_bc[:, i * 512:(i + 1) * 512],
                              self.ps_bc[:, :]).then_inc(self.s_ve.h, 1)
        self.emit("tensor", f_pe)
        self.emit("vector", f_ve)
        self.s_pe.inc(4)
        self.bvbc_ve = self.s_ve.inc(4)
        self.wait("vector", self.s_c, need)
        self.emit("vector", lambda e: e.tensor_scalar_mul(
            self.bqd[:, :], self.bqd[:, :], SQD).then_inc(self.s_ve.h, 1))
        self.s_ve.inc(1)

    def gmax_acc(self, idx, src, waits=()):
        def f(e, src=src, idx=idx, waits=waits):
            for sm, v in waits:
                e.wait_ge(sm.h, v)
            e.tensor_reduce(self.gacc[0:1, idx:idx + 1], src,
                            axis=mybir.AxisListType.XYZWC,
                            op=ALU.abs_max).then_inc(self.s_g.h, 1)
        self.emit("gpsimd", f)
        return self.s_g.inc(1)

    def gmax_fin(self, lane, lo, hi):
        def f(e, lane=lane, lo=lo, hi=hi, v=self.s_g.n):
            e.wait_ge(self.s_g.h, v)
            e.tensor_reduce(self.gred[0:1, lane:lane + 1],
                            self.gacc[0:1, lo:hi],
                            axis=mybir.AxisListType.XYZWC,
                            op=ALU.max).then_inc(self.s_g.h, 1)
        self.emit("gpsimd", f)
        return self.s_g.inc(1)

    def _wmax_pass(self):
        slabs = [(0, self.wfshm[:, i * 256:(i + 1) * 256])
                 for i in range(64)]
        slabs += [(1, self.wpshm[:, i * 256:(i + 1) * 256])
                  for i in range(64)]
        dmav = {}

        def emit_red(i):
            par = i % 2
            dst = self.wmr[:, par * 256:(par + 1) * 256]
            self.gmax_acc(i, dst, waits=(dmav[i],))

        for i, (lane, src) in enumerate(slabs):
            par = i % 2
            dst = self.wmr[:, par * 256:(par + 1) * 256]
            sm = self.s_wm[par]
            self.emit("gpsimd", lambda e, d=dst, s=src, h=sm.h:
                      e.dma_start(out=d, in_=s).then_inc(h, 16))
            dmav[i] = (sm, sm.inc(16))
            if i >= 1:
                emit_red(i - 1)
        emit_red(127)
        self.gmax_fin(0, 0, 64)
        self.gmax_fin(1, 64, 128)
        self.wmax_g = self.s_g.n

    def _ccw_start(self):
        self.emit("gpsimd", lambda e: e.dma_start(
            out=self.ccw_in[:, :],
            in_=self.gred[0:1, 0:2]).then_inc(self.s_g.h, 16))
        gv = self.s_g.inc(16)
        self.wait("gpsimd", self.s_g, gv)
        self.emit("gpsimd", lambda e: e.collective_compute(
            "AllReduce", ALU.max, replica_groups=[list(range(NC))],
            ins=[self.ccw_in.ap().opt()],
            outs=[self.ccw_out.ap().opt()]).then_inc(self.s_cc.h, 1))
        cv = self.s_cc.inc(1)
        self.wait("gpsimd", self.s_cc, cv)
        self.emit("gpsimd", lambda e: e.dma_start(
            out=self.sml[0:1, 0:2],
            in_=self.ccw_out[:, :]).then_inc(self.s_g.h, 16))
        self._ccw_gv2 = self.s_g.inc(16)

    def _ccw_finish(self):
        gv2 = self._ccw_gv2
        self.wait("vector", self.s_g, gv2)

        base = self.s_ve.n

        def f_sc(e, base=base):
            sml = self.sml
            k = [base]

            def step(inst):
                k[0] += 1
                inst.then_inc(self.s_ve.h, 1)
                e.wait_ge(self.s_ve.h, k[0])
            step(e.tensor_scalar(sml[0:1, 4:5], sml[0:1, 0:1], 1.0 / 448.0,
                                 1e-6, ALU.mult, ALU.max))
            step(e.tensor_scalar(sml[0:1, 5:6], sml[0:1, 1:2], 1.0 / 448.0,
                                 1e-6, ALU.mult, ALU.max))
            step(e.tensor_scalar_mul(sml[0:1, 7:8], sml[0:1, 4:5], 2.0))
            step(e.reciprocal(sml[0:1, 7:8], sml[0:1, 7:8]))
            step(e.tensor_scalar_mul(sml[0:1, 8:9], sml[0:1, 5:6], 2.0))
            step(e.reciprocal(sml[0:1, 8:9], sml[0:1, 8:9]))
        self.emit("vector", f_sc)
        scv = self.s_ve.inc(6)

        def f_b(e, v=scv):
            e.wait_ge(self.s_ve.h, v)
            e.matmul(self.ps_bc[:, 0:2], self.ones_r[:, :],
                     self.sml[0:1, 7:9], start=True,
                     stop=True).then_inc(self.s_pe.h, 1)
        self.emit("tensor", f_b)
        bv = self.s_pe.inc(1)

        def f_cp(e, v=bv):
            e.wait_ge(self.s_pe.h, v)
            e.tensor_copy(self.sbc[:, 0:2],
                          self.ps_bc[:, 0:2]).then_inc(self.s_ve.h, 1)
        self.emit("vector", f_cp)
        self.sbcw_ve = self.s_ve.inc(1)

    def _ln_tm(self, src_ap, dst_ap, scr_guard=None, lane=0, sq=None):
        st = self.st8
        m = st[:, 3 * lane:3 * lane + 1]
        v1 = st[:, 3 * lane + 1:3 * lane + 2]
        v2 = st[:, 3 * lane + 2:3 * lane + 3]
        vb = self.s_ve.n
        self.emit("vector", lambda e, s=src_ap, vb=vb: (
            e.wait_ge(self.s_ve.h, vb),
            e.reduce_sum(m, s, axis=AX).then_inc(self.s_ve.h, 1)))
        vs = self.s_ve.inc(1)

        def f_se(e, s=src_ap, v=vs, g=scr_guard,
                 sq=(sq if sq is not None else self.scr[:, :])):
            e.wait_ge(self.s_ve.h, v)
            if g is not None:
                e.wait_ge(self.s_ve.h, g)
            e.activation(sq, s, ACT.Square,
                         accum_out=v1).then_inc(self.s_se.h, 1)
        self.emit("scalar", f_se)
        ss = self.s_se.inc(1)

        base = self.s_ve.n

        def f_v2(e, v=ss, base=base):
            e.wait_ge(self.s_se.h, v)
            e.tensor_scalar_mul(m, m, 1.0 / E).then_inc(self.s_ve.h, 1)
            e.tensor_scalar_mul(v1, v1, 1.0 / E).then_inc(self.s_ve.h, 1)
            e.wait_ge(self.s_ve.h, base + 1)
            e.scalar_tensor_tensor(v2, m, 1.0, m, ALU.mult,
                                   ALU.mult).then_inc(self.s_ve.h, 1)
            e.wait_ge(self.s_ve.h, base + 3)
            e.tensor_sub(v1, v1, v2).then_inc(self.s_ve.h, 1)
            e.wait_ge(self.s_ve.h, base + 4)
            e.tensor_scalar_add(v1, v1, EPS).then_inc(self.s_ve.h, 1)
        self.emit("vector", f_v2)
        vv = self.s_ve.inc(5)

        def f_s2(e, v=vv):
            e.wait_ge(self.s_ve.h, v)
            e.activation(v1, v1, ACT.Sqrt).then_inc(self.s_se.h, 1)
        self.emit("scalar", f_s2)
        s2 = self.s_se.inc(1)

        base3 = self.s_ve.n

        def f_v3(e, v=s2, base=base3):
            e.wait_ge(self.s_se.h, v)
            e.reciprocal(v1, v1).then_inc(self.s_ve.h, 1)
            e.wait_ge(self.s_ve.h, base + 1)
            e.scalar_tensor_tensor(v2, m, -1.0, v1, ALU.mult,
                                   ALU.mult).then_inc(self.s_ve.h, 1)
        self.emit("vector", f_v3)
        v3 = self.s_ve.inc(2)

        def f_s3(e, s=src_ap, d=dst_ap, v=v3):
            e.wait_ge(self.s_ve.h, v)
            e.activation(d, s, ACT.Identity, bias=v2,
                         scale=v1).then_inc(self.s_se.h, 1)
        self.emit("scalar", f_s3)
        return self.s_se.inc(1)

    def _tr_block(self, src, dst, gate_se=None, extra_ve=None, ps=None):
        if ps is None:
            ps = self.ps_trb
        sl = self._trs
        self._trs ^= 1
        pst = ps[:, sl * 1024: sl * 1024 + 128]
        gp = self._pslot.get((ps.name, sl))

        def f_pe(e, pst=pst, src=src, gse=gate_se, gp=gp):
            if gse is not None:
                e.wait_ge(self.s_se.h, gse)
            if gp is not None:
                e.wait_ge(self.s_ve.h, gp)
            e.transpose(pst, src, self.ident[:, :]).then_inc(self.s_pe.h, 1)
        self.emit("tensor", f_pe)
        pv = self.s_pe.inc(1)

        def f_ve(e, pst=pst, dst=dst, v=pv, vg=extra_ve):
            e.wait_ge(self.s_pe.h, v)
            if vg is not None:
                e.wait_ge(self.s_ve.h, vg)
            e.tensor_copy(dst, pst).then_inc(self.s_ve.h, 1)
        self.emit("vector", f_ve)
        last = self.s_ve.inc(1)
        self._pslot[(ps.name, sl)] = last
        return last

    def _transpose16(self, src_tile, src_col0, dst_cb, gate_se, tag):
        last = 0
        for i in range(16):
            src = src_tile[:, src_col0 + i * 128: src_col0 + (i + 1) * 128]
            dst, vg = dst_cb(i)
            last = self._tr_block(src, dst,
                                  gate_se=(gate_se if i == 0 else None),
                                  extra_ve=vg)
        return last

    def _ln1(self):
        self._tr_ve = {}
        tr_done = {}
        for tc in range(8):
            sl = tc % 3
            par = tc % 2
            xr = self.xr[:, sl * E:(sl + 1) * E]
            lr = self.lnr[:, par * E:(par + 1) * E]
            if tc >= 3:
                self.wait("sync", self.s_ve, tr_done[tc - 3])
            sm = self.s_xl[sl]
            self.emit("sync", lambda e, d=xr, h=sm.h,
                      s=self.x_in[tc * 128:(tc + 1) * 128, :]:
                      e.dma_start(out=d, in_=s).then_inc(h, 16))
            t = sm.inc(16)
            self.wait("vector", sm, t)
            self.wait("scalar", sm, t)
            se_ln = self._ln_tm(xr, lr, scr_guard=tr_done.get(tc - 2),
                                lane=tc % 2)

            def cb(i, tc=tc):
                return (self.ln1T[:, i * TOK + tc * 128:
                                  i * TOK + (tc + 1) * 128], None)
            tr_done[tc] = self._transpose16(self.lnr, par * E, cb, se_ln,
                                            f"ln1_{tc}")
        self.ln1_ve = self.s_ve.n

    def _proj16(self, w_hbm, rhs_tile, rhs_gate_ve, evac_cb, nsw=3):
        NSW = nsw
        cast = {}
        gates = {}
        mms = {}
        prev = getattr(self, "_pj_prev", None)
        if prev is not None:
            gates[-2], gates[-1] = prev
        pm = getattr(self, "_pj_prev_mms", None)
        if pm is not None:
            for k in range(NSW):
                mms[-NSW + k] = pm[k]
        pc = getattr(self, "_pj_prev_cast", None)
        if pc is not None:
            for k in range(NSW):
                cast[-NSW + k] = pc[k]
        self._evac_sync = []
        for oc in range(EC):
            sl = oc % NSW
            par = oc % 2
            wsl = self.wsl[:, sl * E:(sl + 1) * E]
            wslb = self.wslb[:, sl * E:(sl + 1) * E]
            psb = self.ps_mm[:, par * TOK:(par + 1) * TOK]
            if cast.get(oc - NSW) is not None:
                self.wait("sync", self.s_ve, cast[oc - NSW])
            sm = self.s_ws[sl]
            self.emit("sync", lambda e, d=wsl, s=w_hbm[oc, :, :, :], h=sm.h:
                      e.dma_start(out=d, in_=s).then_inc(h, 16))
            t = sm.inc(16)
            for fl in self._evac_sync:
                fl()
            self._evac_sync = []
            self.wait("vector", sm, t)
            pmm = mms.get(oc - NSW)
            self.emit("vector", lambda e, d=wslb, s=wsl, pmm=pmm:
                      (pmm is not None and e.wait_ge(self.s_pe.h, pmm),
                       e.tensor_copy(d, s).then_inc(self.s_ve.h, 1)))
            cast[oc] = self.s_ve.inc(1)
            g = gates.get(oc - 2)

            def f_pe(e, wslb=wslb, psb=psb, cv=cast[oc], g=g,
                     rg=rhs_gate_ve, rhs=rhs_tile):
                e.wait_ge(self.s_ve.h, max(cv, rg))
                if g is not None:
                    e.wait_ge(self.s_se.h if g[0] == "se" else self.s_ve.h,
                              g[1])
                for ec in range(EC):
                    lhs = wslb[:, ec * 128:(ec + 1) * 128]
                    for hf in range(2):
                        mm_i = e.matmul(psb[:, hf * CH:(hf + 1) * CH], lhs,
                                        rhs[:, ec * TOK + hf * CH:
                                            ec * TOK + (hf + 1) * CH],
                                        start=(ec == 0), stop=(ec == EC - 1))
                mm_i.then_inc(self.s_pe.h, 1)
            self.emit("tensor", f_pe)
            mm = self.s_pe.inc(1)
            mms[oc] = mm
            gates[oc] = evac_cb(oc, psb, mm)
        for fl in self._evac_sync:
            fl()
        self._evac_sync = []
        self._pj_prev = (gates[EC - 2], gates[EC - 1])
        self._pj_prev_mms = tuple(mms[EC - NSW + k] for k in range(NSW))
        self._pj_prev_cast = tuple(cast[EC - NSW + k] for k in range(NSW))

    def _proj_qkv(self):
        kdma = {}

        vdma = {}
        vtr_pe = {}
        vtr_ve = {}

        def v_cb(oc, psb, mm):
            par = oc % 2
            fm = self.vfm[:, par * TOK:(par + 1) * TOK]
            g = vtr_pe.get(oc - 2)

            def f(e, fm=fm, psb=psb, v=mm, g=g):
                e.wait_ge(self.s_pe.h, v)
                if g is not None:
                    e.wait_ge(self.s_pe.h, g)
                e.tensor_copy(fm, psb).then_inc(self.s_ve.h, 1)
            self.emit("vector", f)
            fmv = self.s_ve.inc(1)
            for tc in range(8):
                sl = self._trs3
                self._trs3 = (self._trs3 + 1) % 3
                pst = self.ps_trb[:, sl * 1024: sl * 1024 + 128]
                src = fm[:, tc * 128:(tc + 1) * 128]
                evt = self.ev[:, (tc % 4) * 128:((tc % 4) + 1) * 128]
                gp = self._pslot.get((self.ps_trb.name, sl))

                def f_pe(e, pst=pst, src=src, fmv=fmv, g=gp):
                    e.wait_ge(self.s_ve.h, fmv)
                    if g is not None:
                        e.wait_ge(self.s_ve.h, g)
                    e.transpose(pst, src, self.ident[:, :]).then_inc(self.s_pe.h, 1)
                self.emit("tensor", f_pe)
                pv = self.s_pe.inc(1)
                vsm = self.s_vst[tc % 4]
                dg = vdma.get((oc, tc - 4), vdma.get((oc - 1, tc + 4)))

                def f_ve(e, pst=pst, evt=evt, v=pv, oc=oc, dg=dg, h=vsm.h):
                    e.wait_ge(self.s_pe.h, v)
                    if dg is not None:
                        e.wait_ge(h, dg)
                    e.scalar_tensor_tensor(
                        evt, pst, 1.0, self.bv_bc[:, oc * 128:(oc + 1) * 128],
                        ALU.mult, ALU.add).then_inc(self.s_ve.h, 1)
                self.emit("vector", f_ve)
                vv = self.s_ve.inc(1)
                self._pslot[(self.ps_trb.name, sl)] = vv

                def sync_part(vv=vv, evt=evt, oc=oc, tc=tc, h=vsm.h):
                    self.wait("sync", self.s_ve, vv)
                    self.emit("sync",
                              lambda e, d=self.v_st[oc, tc, :, :], s=evt:
                              e.dma_start(out=d, in_=s).then_inc(h, 16))
                self._evac_sync.append(sync_part)
                vdma[(oc, tc)] = vsm.inc(16)
            vtr_pe[oc] = self.s_pe.n
            return ("ve", fmv)
        self._proj16(self.wv, self.ln1T, self.ln1_ve, v_cb)
        self.v_staged = [(sm, sm.n) for sm in self.s_vst]
        self._ccw_start()
        self._gather_v()

        def k_cb(oc, psb, mm):
            par = oc % 2
            dst = self.kst[:, par * TOK:(par + 1) * TOK]
            g = kdma.get(oc - 2)

            sm = self.s_kst[par]

            def f(e, dst=dst, psb=psb, v=mm, oc=oc, g=g, h=sm.h):
                e.wait_ge(self.s_pe.h, v)
                if g is not None:
                    e.wait_ge(h, g)
                e.activation(dst, psb, ACT.Identity,
                             bias=self.bkT[:, oc:oc + 1]).then_inc(self.s_se.h, 1)
            self.emit("scalar", f)
            se = self.s_se.inc(1)

            def sync_part(se=se, dst=dst, oc=oc, h=sm.h):
                self.wait("sync", self.s_se, se)
                self.emit("sync", lambda e, d=self.k_st[oc, :, :], s=dst:
                          e.dma_start(out=d, in_=s).then_inc(h, 16))
            self._evac_sync.append(sync_part)
            kdma[oc] = sm.inc(16)
            return ("se", se)
        self._proj16(self.wk, self.ln1T, self.ln1_ve, k_cb)
        self.k_staged = [(sm, sm.n) for sm in self.s_kst]
        self._gather_k()

        def q_cb(oc, psb, mm):
            def f(e, oc=oc, psb=psb, v=mm):
                e.wait_ge(self.s_pe.h, v)
                e.activation(self.q[:, oc * TOK:(oc + 1) * TOK], psb,
                             ACT.Identity, bias=self.bqd[:, oc:oc + 1],
                             scale=SQD).then_inc(self.s_se.h, 1)
            self.emit("scalar", f)
            return ("se", self.s_se.inc(1))
        self._proj16(self.wq, self.ln1T, self.ln1_ve, q_cb)
        self._ccw_finish()

    def _gather_k(self):
        rg = [[0, 1, 2, 3], [4, 5, 6, 7]]
        for sm, v in self.k_staged:
            self.wait("gpsimd", sm, v)
        self.emit("gpsimd", lambda e: e.collective_compute(
            "AllGather", ALU.bypass, replica_groups=rg,
            ins=[self.k_st.ap().opt()],
            outs=[self.K_all.ap().opt()]).then_inc(self.s_cc.h, 1))
        self.k_cc = self.s_cc.inc(1)

    def _gather_v(self):
        rg = [[0, 1, 2, 3], [4, 5, 6, 7]]
        for sm, v in self.v_staged:
            self.wait("gpsimd", sm, v)
        self.emit("gpsimd", lambda e: e.collective_compute(
            "AllGather", ALU.bypass, replica_groups=rg,
            ins=[self.v_st.ap().opt()],
            outs=[self.V_all.ap().opt()]).then_inc(self.s_cc.h, 1))
        self.v_cc = self.s_cc.inc(1)

    def _attention(self):
        LAG = 3
        NS = 4
        blocks = attn_blocks()
        binfo = []
        tiles = []
        for b, (h, slot, j, nt, uni) in enumerate(blocks):
            binfo.append((2 * h + slot, (2 * h + slot) % 2, WH[h],
                          h * TOK + slot * CH + j * WH[h], nt, set(uni),
                          h, slot, j))
            for t in range(nt):
                tiles.append((b, t))
        N = len(tiles)

        mm_pe, exp_se, av_pe, mask_ve = {}, {}, {}, {}
        recip_se, evac_ve, norm_ve, last_av = {}, {}, {}, {}
        kgate, vgate = {}, {}
        seen_it = set()
        bcol = [0]
        ucol = [0]
        pend = []

        pat = {}
        uc = 0
        for (h, slot, j, nt, uni) in blocks:
            us = set(uni)
            for t in range(nt):
                if t in us:
                    if h == 4:
                        pat[(slot, t)] = (len(pat), uc)
                    uc += 1
        gen_base = self.s_ve.n

        def f_gen(e, items=sorted(pat.items(), key=lambda x: x[1][0]),
                  need=self.s_c.n, base=gen_base):
            e.wait_ge(self.s_c.h, need)
            e.wait_ge(self.s_ve.h, base)
            for (slot, t), (pid, col) in items:
                e.tensor_scalar(self.mtab[:, pid * 512:(pid + 1) * 512],
                                self.iota[:, 0:512],
                                self.ttab[:, col:col + 1], NEG,
                                ALU.is_lt, ALU.mult).then_inc(self.s_ve.h, 1)
        self.emit("vector", f_gen)
        mtab_ve = self.s_ve.inc(len(pat))

        def emit_slab(b):
            it, par, w, qc0, nt, uni, h, slot, j = binfo[b]
            if it in seen_it:
                return
            seen_it.add(it)
            if it >= 2:
                self.wait("gpsimd", self.s_pe, last_av[it - 2])
            if it == 0:
                self.wait("gpsimd", self.s_cc, self.k_cc)
            kd, vd = [], []
            for p5 in range(ESLOT[slot] // CH):
                rl = (3 - p5) if p5 <= 3 else (p5 - 4)
                lc = 0 if p5 <= 3 else CH
                kd.append((self.ksl[:, par * S + p5 * CH:
                                    par * S + (p5 + 1) * CH],
                           self.K_all[rl, h, :, lc:lc + CH]))
                for q4 in range(4):
                    g2 = p5 * 4 + q4
                    tcz = q4 + (0 if p5 <= 3 else 4)
                    vd.append((self.vsl[:, par * S + g2 * 128:
                                        par * S + (g2 + 1) * 128],
                               self.V_all[rl, h, tcz, :, :]))
            ksm, vsm = self.s_kv[par], self.s_vs[par]

            def f_dk(e, kd=kd, hh=ksm.h):
                for d, s in kd:
                    e.dma_start(out=d, in_=s).then_inc(hh, 16)
            self.emit("gpsimd", f_dk)
            kgate[it] = (ksm, ksm.inc(16 * len(kd)))
            if it == 0:
                self.wait("gpsimd", self.s_cc, self.v_cc)

            def f_dv(e, vd=vd, hh=vsm.h):
                for d, s in vd:
                    e.dma_start(out=d, in_=s).then_inc(hh, 16)
            self.emit("gpsimd", f_dv)
            vgate[it] = (vsm, vsm.inc(16 * len(vd)))

        def emit_score(g):
            b, t = tiles[g]
            it, par, w, qc0, nt, uni, h, slot, j = binfo[b]
            sp = g % NS
            pss = self.ps_s[:, sp * 512: sp * 512 + w]
            eg = exp_se.get(g - NS)
            kg = kgate.pop(it, None)

            def f_s(e, pss=pss, t=t, par=par, qc0=qc0, w=w, eg=eg, kg=kg):
                if kg is not None:
                    e.wait_ge(kg[0].h, kg[1])
                if eg is not None:
                    e.wait_ge(self.s_se.h, eg)
                e.matmul(pss, self.ksl[:, par * S + t * 128:
                                       par * S + (t + 1) * 128],
                         self.q[:, qc0:qc0 + w],
                         start=True, stop=True).then_inc(self.s_pe.h, 1)
            self.emit("tensor", f_s)
            mm_pe[g] = self.s_pe.inc(1)

            if t in uni:
                if w == 512:
                    pid = pat[(slot, t)][0]

                    def f_m(e, pss=pss, v=mm_pe[g], pid=pid, mg=mtab_ve):
                        e.wait_ge(self.s_pe.h, v)
                        e.wait_ge(self.s_ve.h, mg)
                        e.tensor_add(pss, pss,
                                     self.mtab[:, pid * 512:(pid + 1) * 512]
                                     ).then_inc(self.s_ve.h, 1)
                    self.emit("vector", f_m)
                    mask_ve[g] = self.s_ve.inc(1)
                else:
                    mp = ucol[0] % 2
                    mge = self.mgen[:, mp * 512: mp * 512 + w]
                    mbase = self.s_ve.n

                    def f_m(e, pss=pss, mge=mge, tcol=ucol[0], v=mm_pe[g],
                            w=w, base=mbase):
                        e.wait_ge(self.s_pe.h, v)
                        e.wait_ge(self.s_ve.h, base)
                        e.tensor_scalar(mge, self.iota[:, 0:w],
                                        self.ttab[:, tcol:tcol + 1], NEG,
                                        ALU.is_lt, ALU.mult
                                        ).then_inc(self.s_ve.h, 1)
                        e.wait_ge(self.s_ve.h, base + 1)
                        e.tensor_add(pss, pss, mge).then_inc(self.s_ve.h, 1)
                    self.emit("vector", f_m)
                    mask_ve[g] = self.s_ve.inc(2)
                ucol[0] += 1

            pg = av_pe.get(g - 4)
            gate = (("ve", mask_ve[g]) if g in mask_ve
                    else ("pe", mm_pe[g]))

            def f_e(e, pss=pss, col=bcol[0], g2=gate, pg=pg,
                    prb_t=self.prb[:, (g % 4) * 512: (g % 4) * 512 + w]):
                if g2[0] == "ve":
                    e.wait_ge(self.s_ve.h, g2[1])
                else:
                    e.wait_ge(self.s_pe.h, g2[1])
                if pg is not None:
                    e.wait_ge(self.s_pe.h, pg)
                e.activation(prb_t, pss, ACT.Exp,
                             bias=self.btab[:, col:col + 1],
                             scale=1.0).then_inc(self.s_se.h, 1)
            self.emit("scalar", f_e)
            exp_se[g] = self.s_se.inc(1)
            bcol[0] += 1

        def emit_av(g):
            b, t = tiles[g]
            it, par, w, qc0, nt, uni, h, slot, j = binfo[b]
            first, last = (t == 0), (t == nt - 1)
            gates = [(self.s_se, exp_se[g])]
            if first:
                vg = vgate.pop(it, None)
                if vg is not None:
                    gates.append(vg)
                rv = recip_se.get(b - 1)
                if rv is not None:
                    gates.append((self.s_ve, rv))
                ev = evac_ve.get(b - 2)
                if ev is not None:
                    gates.append((self.s_ve, ev))

            def f_av(e, t=t, w=w, par=par, first=first, last=last,
                     gates=gates,
                     prb_t=self.prb[:, (g % 4) * 512: (g % 4) * 512 + w]):
                for sm, v in gates:
                    e.wait_ge(sm.h, v)
                e.matmul(self.ps_m[0:1, 0:w],
                         self.ones_c[:, :], prb_t, start=first, stop=last)
                e.matmul(self.ps_o[:, par * 512: par * 512 + w],
                         self.vsl[:, par * S + t * 128:
                                  par * S + (t + 1) * 128],
                         prb_t, start=first,
                         stop=last).then_inc(self.s_pe.h, 1)
            self.emit("tensor", f_av)
            av_pe[g] = self.s_pe.inc(1)
            if last:
                last_av[it] = av_pe[g]
                pend.append((b, av_pe[g]))

        def emit_fin(b, avv):
            it, par, w, qc0, nt, uni, h, slot, j = binfo[b]

            def f_r(e, w=w, v=avv):
                e.wait_ge(self.s_pe.h, v)
                e.reciprocal(self.rsum[0:1, 0:w],
                             self.ps_m[0:1, 0:w]).then_inc(self.s_ve.h, 1)
            self.emit("vector", f_r)
            recip_se[b] = rv = self.s_ve.inc(1)

            def f_bc(e, w=w, v=rv, g=norm_ve.get(b - 1)):
                e.wait_ge(self.s_ve.h, v)
                if g is not None:
                    e.wait_ge(self.s_ve.h, g)
                e.matmul(self.ps_bc[:, 0:w], self.ones_r[:, :],
                         self.rsum[0:1, 0:w], start=True,
                         stop=True).then_inc(self.s_pe.h, 1)
            self.emit("tensor", f_bc)
            bcv = self.s_pe.inc(1)

            def f_ev(e, par=par, w=w, v=avv):
                e.wait_ge(self.s_pe.h, v)
                e.tensor_copy(self.osb[:, 0:w],
                              self.ps_o[:, par * 512: par * 512 + w]
                              ).then_inc(self.s_ve.h, 1)
            self.emit("vector", f_ev)
            evac_ve[b] = self.s_ve.inc(1)

            def f_n(e, qc0=qc0, w=w, b_=bcv):
                e.wait_ge(self.s_pe.h, b_)
                e.tensor_mul(self.attn[:, qc0:qc0 + w], self.osb[:, 0:w],
                             self.ps_bc[:, 0:w]).then_inc(self.s_ve.h, 1)
            self.emit("vector", f_n)
            norm_ve[b] = self.s_ve.inc(1)

        for g in range(N):
            b, t = tiles[g]
            if t == 0:
                emit_slab(b)
            emit_score(g)
            while pend:
                emit_fin(*pend.pop(0))
            if g >= LAG:
                emit_av(g - LAG)
        for g in range(N - LAG, N):
            emit_av(g)
            while pend:
                emit_fin(*pend.pop(0))
        self.attn_ve = self.s_ve.n

    def _wo_ln2(self):
        self._tr_ve = {}
        wtr = {}

        def wo_cb(oc, psb, mm):
            par = oc % 2

            def f(e, psb=psb, v=mm, par=par, oc=oc):
                e.wait_ge(self.s_pe.h, v)
                e.activation(self.aofm[:, par * TOK:(par + 1) * TOK], psb,
                             ACT.Identity,
                             bias=self.boT[:, oc:oc + 1]).then_inc(self.s_se.h, 1)
            self.emit("scalar", f)
            se = self.s_se.inc(1)
            for tc in range(8):
                src = self.aofm[:, par * TOK + tc * 128:
                                par * TOK + (tc + 1) * 128]
                dst = self.ao_tm[:, tc * E + oc * 128: tc * E + (oc + 1) * 128]
                self._tr_block(src, dst, gate_se=(se if tc == 0 else None),
                               ps=self.ps_t16)
            return ("se", se)
        self._proj16(self.wo, self.attn, self.attn_ve, wo_cb, nsw=2)
        ao_ve = self.s_ve.n
        self.wait("vector", self.s_ve, ao_ve)

        x1v = {}
        addv = {}
        lnv3 = {}
        mxv3 = {}
        for tc in range(8):
            par = tc % 2
            xr = self.wsl[:, par * E:(par + 1) * E]
            if tc == 0:
                self.wait("sync", self.s_ve, ao_ve)
            if tc >= 2:
                self.wait("sync", self.s_ve, addv[tc - 2])
            sm = self.s_wsl[par]
            self.emit("sync", lambda e, d=xr, h=sm.h,
                      s=self.x_in[tc * 128:(tc + 1) * 128, :]:
                      e.dma_start(out=d, in_=s).then_inc(h, 16))
            t = sm.inc(16)
            d2 = self.ln2d[:, par * E:(par + 1) * E]
            se_ln = self._ln_tm(self.ao_tm[:, tc * E:(tc + 1) * E], d2,
                                scr_guard=addv.get(tc - 2), lane=par, sq=d2)

            x1r = self.x1row[:, par * E:(par + 1) * E]
            dguard = 16 * (tc - 1) if tc >= 2 else None

            def f_add(e, xr=xr, v=se_ln, t=t, h=sm.h, x1r=x1r, dg=dguard,
                      par=par):
                e.wait_ge(self.s_se.h, v)
                e.wait_ge(h, t)
                if dg is not None:
                    e.wait_ge(self.s_x1s.h, dg)
                e.tensor_add(x1r, xr, self.ln2d[:, par * E:(par + 1) * E]
                             ).then_inc(self.s_ve.h, 1)
            self.emit("vector", f_add)
            vv = self.s_ve.inc(1)
            addv[tc] = vv
            self.wait("sync", self.s_ve, vv)
            if tc >= 1:
                self.wait("sync", self.s_x1s, 16 * tc)
            self.emit("sync", lambda e, d=self.x1_buf[tc, :, :], src=x1r:
                      e.dma_start(out=d, in_=src).then_inc(self.s_x1s.h, 16))
            self.s_x1s.inc(16)
            d3 = self.ln3[:, tc * E:(tc + 1) * E]
            se3 = self._ln_tm(x1r, d3, lane=2 + par, sq=d3)
            lnv3[tc] = se3
            mxv3[tc] = self.gmax_acc(128 + tc, d3,
                                     waits=((self.s_se, se3),))
        self.lnv3 = lnv3
        self.x1_ve = self.s_ve.n

    def _ln3_and_cc1(self):
        lnv = self.lnv3
        self.gmax_fin(2, 128, 136)
        self.emit("gpsimd", lambda e: e.dma_start(
            out=self.ccx_in[:, :],
            in_=self.gred[0:1, 2:3]).then_inc(self.s_g.h, 16))
        gv = self.s_g.inc(16)
        self.wait("gpsimd", self.s_g, gv)
        self.emit("gpsimd", lambda e: e.collective_compute(
            "AllReduce", ALU.max, replica_groups=[list(range(NC))],
            ins=[self.ccx_in.ap().opt()],
            outs=[self.ccx_out.ap().opt()]).then_inc(self.s_cc.h, 1))
        cv = self.s_cc.inc(1)
        self.wait("gpsimd", self.s_cc, cv)
        self.emit("gpsimd", lambda e: e.dma_start(
            out=self.sml[0:1, 2:3],
            in_=self.ccx_out[:, :]).then_inc(self.s_g.h, 16))
        gv2 = self.s_g.inc(16)
        self.wait("vector", self.s_g, gv2)

        base = self.s_ve.n

        def f_sc(e, base=base):
            sml = self.sml
            k = [base]

            def step(inst):
                k[0] += 1
                inst.then_inc(self.s_ve.h, 1)
                e.wait_ge(self.s_ve.h, k[0])
            step(e.tensor_scalar(sml[0:1, 6:7], sml[0:1, 2:3], 1.0 / 448.0,
                                 1e-6, ALU.mult, ALU.max))
            step(e.tensor_scalar_mul(sml[0:1, 9:10], sml[0:1, 6:7], 2.0))
            step(e.reciprocal(sml[0:1, 9:10], sml[0:1, 9:10]))
            step(e.tensor_mul(sml[0:1, 10:11], sml[0:1, 4:5], sml[0:1, 6:7]))
            step(e.tensor_scalar_mul(sml[0:1, 10:11], sml[0:1, 10:11], 4.0))
        self.emit("vector", f_sc)
        scv = self.s_ve.inc(5)

        def f_b(e, v=scv):
            e.wait_ge(self.s_ve.h, v)
            e.matmul(self.ps_bc[:, 0:2], self.ones_r[:, :],
                     self.sml[0:1, 9:11], start=True,
                     stop=True).then_inc(self.s_pe.h, 1)
        self.emit("tensor", f_b)
        bv = self.s_pe.inc(1)

        def f_cp(e, v=bv):
            e.wait_ge(self.s_pe.h, v)
            e.tensor_copy(self.sbc[:, 2:4],
                          self.ps_bc[:, 0:2]).then_inc(self.s_ve.h, 1)
        self.emit("vector", f_cp)
        self.sbcx_ve = self.s_ve.inc(1)

        qv = {}
        for tc in range(8):
            for ec in range(EC):
                i = tc * EC + ec
                sl = i % 4
                pst = self.ps_tr32[:, sl * 512: sl * 512 + 128]
                src = self.ln3[:, tc * E + ec * 128: tc * E + (ec + 1) * 128]
                dst = self.ln3_8[:, ec * TOK + tc * 128:
                                 ec * TOK + (tc + 1) * 128]
                g = self._pslot_se.get(sl)

                def f_pe(e, pst=pst, src=src, g=g,
                         gse=(lnv[tc] if ec == 0 else None)):
                    if gse is not None:
                        e.wait_ge(self.s_se.h, gse)
                    if g is not None:
                        e.wait_ge(self.s_se.h, g)
                    e.transpose(pst, src,
                                self.ident[:, :]).then_inc(self.s_pe.h, 1)
                self.emit("tensor", f_pe)
                pv2 = self.s_pe.inc(1)

                def f_q(e, pst=pst, dst=dst, v=pv2,
                        sg=(self.sbcx_ve if i == 0 else None)):
                    e.wait_ge(self.s_pe.h, v)
                    if sg is not None:
                        e.wait_ge(self.s_ve.h, sg)
                    e.activation(dst, pst, ACT.Identity,
                                 scale=self.sbc[:, 2:3]).then_inc(self.s_se.h, 1)
                self.emit("scalar", f_q)
                qv[i] = self._pslot_se[sl] = self.s_se.inc(1)
        self.ln38_se = self.s_se.n

    def _ffn1(self):
        self.wait("scalar", self.s_se, self.ln38_se)
        q8v = {}
        relu_ve = {}
        gdma = {}
        mxg = {}
        NSW = 8
        NP = EC // 2
        for fg in range(32):
            for p in range(NP):
                i = fg * NP + p
                sl = i % NSW
                wsl = self.wfsl[:, sl * 512:(sl + 1) * 512]
                w8 = self.wf8[:, sl * 512:(sl + 1) * 512]
                g = q8v.get(i - NSW)
                if g is not None:
                    self.wait("sync", self.s_se, g)
                sm = self.s_ws[sl]
                self.emit("sync", lambda e, d=wsl, s=self.wfc[fg, p, :, :],
                          h=sm.h: e.dma_start(out=d, in_=s).then_inc(h, 16))
                t = sm.inc(16)
                pg = self._f1pe.get(i - NSW)

                def f_q(e, wsl=wsl, w8=w8, t=t, pg=pg, h=sm.h,
                        sg=(self.sbcw_ve if i == 0 else None)):
                    e.wait_ge(h, t)
                    if sg is not None:
                        e.wait_ge(self.s_ve.h, sg)
                    if pg is not None:
                        e.wait_ge(self.s_pe.h, pg)
                    e.activation(w8, wsl, ACT.Identity,
                                 scale=self.sbc[:, 0:1]).then_inc(self.s_se.h, 1)
                self.emit("scalar", f_q)
                q8v[i] = self.s_se.inc(1)

                def f_pe(e, w8=w8, p=p, v=q8v[i], fg=fg,
                         rg=relu_ve.get(fg - 1)):
                    e.wait_ge(self.s_se.h, v)
                    if p == 0 and rg is not None:
                        e.wait_ge(self.s_ve.h, rg)
                    w3 = w8.rearrange('q (i m) -> q i m', i=2)
                    x3 = self.ln3_8[:, 2 * p * TOK: (2 * p + 2) * TOK
                                    ].rearrange('q (i n) -> q i n', i=2)
                    for fc2 in range(2):
                        lhs = w3[:, :, fc2 * 128:(fc2 + 1) * 128]
                        for hf in range(2):
                            mm_i = e.matmul(
                                self.ps_h[:, fc2 * 1024 + hf * 512:
                                          fc2 * 1024 + (hf + 1) * 512],
                                lhs,
                                x3[:, :, hf * 512:(hf + 1) * 512],
                                start=(p == 0), stop=(p == NP - 1),
                                perf_mode=mybir.MatmulPerfMode.DoubleRow)
                    mm_i.then_inc(self.s_pe.h, 1)
                self.emit("tensor", f_pe)
                self._f1pe[i] = self.s_pe.inc(1)
            mmv = self._f1pe[fg * NP + NP - 1]
            par = fg % 2
            gts = self.gt[:, par * 2048:(par + 1) * 2048]
            dg = gdma.get(fg - 2)

            gsm = self.s_gt[par]

            def f_relu(e, v=mmv, gts=gts, dg=dg, h=gsm.h):
                e.wait_ge(self.s_pe.h, v)
                if dg is not None:
                    e.wait_ge(h, dg)
                e.tensor_scalar_max(self.scr[:, :], self.ps_h[:, :],
                                    0.0).then_inc(self.s_ve.h, 1)
            self.emit("vector", f_relu)
            relu_ve[fg] = rv = self.s_ve.inc(1)

            def f_sq(e, v=rv, gts=gts, mg=mxg.get(fg - 2)):
                e.wait_ge(self.s_ve.h, v)
                if mg is not None:
                    e.wait_ge(self.s_g.h, mg)
                e.activation(gts, self.scr[:, :], ACT.Square,
                             scale=self.sbc[:, 3:4]).then_inc(self.s_se.h, 1)
            self.emit("scalar", f_sq)
            sq = self.s_se.inc(1)

            mxg[fg] = self.gmax_acc(136 + fg, gts, waits=((self.s_se, sq),))

            def f_d(e, fg=fg, gts=gts, h=gsm.h):
                e.dma_start(out=self.g_buf[2 * fg, :, :],
                            in_=gts[:, 0:1024]).then_inc(h, 16)
                e.dma_start(out=self.g_buf[2 * fg + 1, :, :],
                            in_=gts[:, 1024:2048]).then_inc(h, 16)
            self.emit("gpsimd", f_d)
            gdma[fg] = gsm.inc(32)
        self.g_alldma = [(sm, sm.n) for sm in self.s_gt]

        self.gmax_fin(3, 136, 168)
        self.emit("gpsimd", lambda e: e.dma_start(
            out=self.cc2_in[:, :],
            in_=self.gred[0:1, 3:4]).then_inc(self.s_g.h, 16))
        gv = self.s_g.inc(16)
        self.wait("gpsimd", self.s_g, gv)
        self.emit("gpsimd", lambda e: e.collective_compute(
            "AllReduce", ALU.max, replica_groups=[list(range(NC))],
            ins=[self.cc2_in.ap().opt()],
            outs=[self.cc2_out.ap().opt()]).then_inc(self.s_cc.h, 1))
        cv = self.s_cc.inc(1)
        self.wait("gpsimd", self.s_cc, cv)
        self.emit("gpsimd", lambda e: e.dma_start(
            out=self.sml[0:1, 15:16],
            in_=self.cc2_out[:, :]).then_inc(self.s_g.h, 16))
        gv2 = self.s_g.inc(16)
        self.wait("vector", self.s_g, gv2)

        base = self.s_ve.n

        def f_sc(e, base=base):
            sml = self.sml
            k = [base]

            def step(inst):
                k[0] += 1
                inst.then_inc(self.s_ve.h, 1)
                e.wait_ge(self.s_ve.h, k[0])
            step(e.tensor_scalar(sml[0:1, 13:14], sml[0:1, 15:16],
                                 1.0 / 448.0, 1e-6, ALU.mult, ALU.max))
            step(e.tensor_scalar_mul(sml[0:1, 11:12], sml[0:1, 13:14], 2.0))
            step(e.reciprocal(sml[0:1, 11:12], sml[0:1, 11:12]))
            step(e.tensor_mul(sml[0:1, 12:13], sml[0:1, 13:14],
                              sml[0:1, 5:6]))
            step(e.tensor_scalar_mul(sml[0:1, 12:13], sml[0:1, 12:13], 4.0))
        self.emit("vector", f_sc)
        scv = self.s_ve.inc(5)

        def f_b(e, v=scv):
            e.wait_ge(self.s_ve.h, v)
            e.matmul(self.ps_bc[:, 0:2], self.ones_r[:, :],
                     self.sml[0:1, 11:13], start=True,
                     stop=True).then_inc(self.s_pe.h, 1)
        self.emit("tensor", f_b)
        bv = self.s_pe.inc(1)

        def f_cp(e, v=bv):
            e.wait_ge(self.s_pe.h, v)
            e.tensor_copy(self.sbc[:, 4:6],
                          self.ps_bc[:, 0:2]).then_inc(self.s_ve.h, 1)
        self.emit("vector", f_cp)
        self.sbc2_ve = self.s_ve.inc(1)

    def __init_f1(self):
        pass

    def _ffn2(self):
        q8v = {}
        w8v = {}
        evac_se = {}
        tr_done = {}
        self._o2tr = {}
        NP2 = FC // 2
        for og in range(8):
            for p in range(NP2):
                if og == 0:
                    for fc in (2 * p, 2 * p + 1):
                        gi = self.gin[:, (fc % 4) * 1024:(fc % 4 + 1) * 1024]
                        g = q8v.get(fc - 4)
                        if g is not None:
                            self.wait("gpsimd", self.s_se, g)
                        if fc == 0:
                            for sm_, v_ in self.g_alldma:
                                self.wait("gpsimd", sm_, v_)
                        gsm = self.s_gin[fc % 4]

                        def f_gd(e, gi=gi, fc=fc, h=gsm.h):
                            e.dma_start(out=gi,
                                        in_=self.g_buf[fc, :, :]
                                        ).then_inc(h, 16)
                        self.emit("gpsimd", f_gd)
                        gv = gsm.inc(16)

                        def f_gq(e, gi=gi, fc=fc, v=gv, h=gsm.h,
                                 sg=(self.sbc2_ve if fc == 0 else None)):
                            e.wait_ge(h, v)
                            if sg is not None:
                                e.wait_ge(self.s_ve.h, sg)
                            e.activation(
                                self.g8[:, fc * 1024:(fc + 1) * 1024],
                                gi, ACT.Identity,
                                scale=self.sbc[:, 4:5]).then_inc(self.s_se.h, 1)
                        self.emit("scalar", f_gq)
                        q8v[fc] = self.s_se.inc(1)

                i = og * NP2 + p
                sl = i % 8
                wsl = self.wpsl[:, sl * 512:(sl + 1) * 512]
                w8 = self.wp8[:, sl * 512:(sl + 1) * 512]
                g = w8v.get(i - 8)
                if g is not None:
                    self.wait("sync", self.s_se, g)
                sm = self.s_ws[sl]
                self.emit("sync", lambda e, d=wsl, s=self.wpr[og, p, :, :],
                          h=sm.h: e.dma_start(out=d, in_=s).then_inc(h, 16))
                t = sm.inc(16)
                pg = self._f2pe.get(i - 8)

                def f_wq(e, wsl=wsl, w8=w8, t=t, pg=pg, h=sm.h,
                         sg=(self.sbc2_ve if i == 0 else None)):
                    e.wait_ge(h, t)
                    if sg is not None:
                        e.wait_ge(self.s_ve.h, sg)
                    if pg is not None:
                        e.wait_ge(self.s_pe.h, pg)
                    e.activation(w8, wsl, ACT.Identity,
                                 scale=self.sbc[:, 1:2]).then_inc(self.s_se.h, 1)
                self.emit("scalar", f_wq)
                w8v[i] = self.s_se.inc(1)
                gq = q8v.get(2 * p + 1) if og == 0 else None

                def f_pe(e, w8=w8, p=p, v=w8v[i], gq=gq,
                         eg=(evac_se.get(og - 1) if p == 0 else None)):
                    e.wait_ge(self.s_se.h, v if gq is None else max(v, gq))
                    if eg is not None:
                        e.wait_ge(self.s_se.h, eg)
                    w3 = w8.rearrange('q (i m) -> q i m', i=2)
                    x3 = self.g8[:, 2 * p * 1024: (2 * p + 2) * 1024
                                 ].rearrange('q (i n) -> q i n', i=2)
                    for oc2 in range(2):
                        lhs = w3[:, :, oc2 * 128:(oc2 + 1) * 128]
                        for hf in range(2):
                            mm_i = e.matmul(
                                self.ps_o2[:, oc2 * 1024 + hf * 512:
                                           oc2 * 1024 + (hf + 1) * 512],
                                lhs,
                                x3[:, :, hf * 512:(hf + 1) * 512],
                                start=(p == 0), stop=(p == NP2 - 1),
                                perf_mode=mybir.MatmulPerfMode.DoubleRow)
                    mm_i.then_inc(self.s_pe.h, 1)
                self.emit("tensor", f_pe)
                self._f2pe[i] = self.s_pe.inc(1)
            mmv = self._f2pe[og * NP2 + NP2 - 1]
            par = og % 2
            o2 = self.o2fm[:, par * 2048:(par + 1) * 2048]
            g = tr_done.get(og - 1)

            def f_ev(e, v=mmv, o2=o2, g=g):
                e.wait_ge(self.s_pe.h, v)
                if g is not None:
                    e.wait_ge(self.s_ve.h, g)
                e.activation(o2, self.ps_o2[:, :], ACT.Identity,
                             scale=self.sbc[:, 5:6]).then_inc(self.s_se.h, 1)
            self.emit("scalar", f_ev)
            evac_se[og] = se = self.s_se.inc(1)
            last = 0
            for i in range(16):
                oc2, tc = i // 8, i % 8
                src = o2[:, oc2 * 1024 + tc * 128: oc2 * 1024 + (tc + 1) * 128]
                dst = self.o2tm[:, tc * E + og * 256 + oc2 * 128:
                                tc * E + og * 256 + (oc2 + 1) * 128]
                last = self._tr_block(src, dst,
                                      gate_se=(se if i == 0 else None))
            tr_done[og] = last
        self.o2_ve = self.s_ve.n

    def _ln4_out(self):
        self.wait("vector", self.s_ve, self.o2_ve)
        self.wait("scalar", self.s_ve, self.o2_ve)
        yv = {}
        x1ld = {}
        for tc in range(8):
            par = tc % 2
            xm = self.s_x[par]
            xr = self.x1r2[:, par * E:(par + 1) * E]
            if tc >= 2:
                self.wait("sync", self.s_ve, x1ld[tc - 2][1])
            self.emit("sync", lambda e, d=xr, s=self.x1_buf[tc, :, :],
                      h=xm.h: e.dma_start(out=d, in_=s).then_inc(h, 16))
            t = xm.inc(16)
            yr = self.yrow[:, par * E:(par + 1) * E]
            d4 = self.gin[:, par * E:(par + 1) * E]
            se_ln = self._ln_tm(self.o2tm[:, tc * E:(tc + 1) * E], d4,
                                scr_guard=(yv.get(tc - 2) or [None, None])[1],
                                lane=par, sq=d4)
            g = yv.get(tc - 2)

            def f_add(e, yr=yr, v=se_ln, g=g, xm=xm, t=t, xr=xr, par=par):
                e.wait_ge(self.s_se.h, v)
                e.wait_ge(xm.h, t)
                if g is not None:
                    e.wait_ge(self.s_y.h, g[0])
                e.tensor_add(yr, xr, self.gin[:, par * E:(par + 1) * E]
                             ).then_inc(self.s_ve.h, 1)
            self.emit("vector", f_add)
            vv = self.s_ve.inc(1)
            x1ld[tc] = (xm, vv)
            self.wait("sync", self.s_ve, vv)
            self.emit("sync", lambda e, d=self.y_out[tc * 128:(tc + 1) * 128, :],
                      s=yr: e.dma_start(out=d, in_=s).then_inc(self.s_y.h, 16))
            yv[tc] = (self.s_y.inc(16), vv)


def _host_inputs(inputs, c):
    import ml_dtypes
    x = np.asarray(inputs["L0_attn_x"])
    (b0, p0), (b1, p1) = core_chunks(c)
    xs = np.concatenate([x[b0, p0 * CH:(p0 + 1) * CH, :],
                         x[b1, p1 * CH:(p1 + 1) * CH, :]], axis=0)
    xs = np.ascontiguousarray(xs, dtype=np.float32)

    def blk(w):
        return np.ascontiguousarray(
            np.asarray(w).reshape(EC, 128, EC, 128).transpose(0, 3, 2, 1))

    def bT(b):
        return np.ascontiguousarray(np.asarray(b).reshape(EC, 128).T)

    bias, thr = build_tables(c)
    d = {
        "x": xs,
        "wq": _host_inputs.wq, "wk": _host_inputs.wk,
        "wv": _host_inputs.wv, "wo": _host_inputs.wo,
        "wfc": _host_inputs.wfc, "wpr": _host_inputs.wpr,
        "wfshm": np.ascontiguousarray(
            _host_inputs.wfc[4 * c:4 * c + 4].transpose(2, 0, 1, 3)
            .reshape(128, 16384)),
        "wpshm": np.ascontiguousarray(
            _host_inputs.wpr[c].transpose(1, 0, 2).reshape(128, 16384)),
        "bqT": bT(inputs["L0_attn_bq"]),
        "bkT": bT(inputs["L0_attn_bk"]),
        "boT": bT(inputs["L0_attn_bo"]),
        "bv": np.ascontiguousarray(
            np.asarray(inputs["L0_attn_bv"], dtype=np.float32)[None, :]),
        "ident": np.eye(128, dtype=np.float16),
        "ident32": np.eye(128, dtype=np.float32),
        "ones_col": np.ones((128, 1), dtype=ml_dtypes.bfloat16),
        "ones_row": np.ones((1, 128), dtype=np.float32),
        "iota512": np.ascontiguousarray(np.broadcast_to(
            np.arange(512, dtype=np.float32)[None, :], (128, 512))),
        "btab": bias,
        "ttab": thr,
    }
    return d


def kernel(**inputs):
    import ml_dtypes
    f32 = lambda a: np.ascontiguousarray(np.asarray(a), dtype=np.float32)
    _host_inputs.wq = f32(np.asarray(inputs["L0_attn_Wq"])
                          .reshape(EC, 128, EC, 128).transpose(0, 3, 2, 1))
    _host_inputs.wk = f32(np.asarray(inputs["L0_attn_Wk"])
                          .reshape(EC, 128, EC, 128).transpose(0, 3, 2, 1))
    _host_inputs.wv = f32(np.asarray(inputs["L0_attn_Wv"])
                          .reshape(EC, 128, EC, 128).transpose(0, 3, 2, 1))
    _host_inputs.wo = f32(np.asarray(inputs["L0_attn_Wo"])
                          .reshape(EC, 128, EC, 128).transpose(0, 3, 2, 1))
    _host_inputs.wfc = f32(np.asarray(inputs["L0_ffn_w_fc"])
                           .reshape(32, 256, 8, 2, 128)
                           .transpose(0, 2, 4, 3, 1).reshape(32, 8, 128, 512))
    _host_inputs.wpr = f32(np.asarray(inputs["L0_ffn_w_proj"])
                           .reshape(8, 256, 32, 2, 128)
                           .transpose(0, 2, 4, 3, 1).reshape(8, 32, 128, 512))

    b = Builder()
    nc = b.build()
    in_maps = [_host_inputs(inputs, c) for c in range(NC)]
    res = run_bass_kernel_spmd(nc, in_maps, core_ids=list(range(NC)))
    out = np.zeros((2, S, E), dtype=np.float32)
    for c in range(NC):
        y = np.asarray(res.results[c]["y"])
        (b0, p0), (b1, p1) = core_chunks(c)
        out[b0, p0 * CH:(p0 + 1) * CH, :] = y[0:CH]
        out[b1, p1 * CH:(p1 + 1) * CH, :] = y[CH:TOK]
    return out
